# version 01 (final)
# baseline (speedup 1.0000x reference)
import contextlib
import numpy as np
import concourse.bass as bass
import concourse.mybir as mybir
F32 = mybir.dt.float32
BF16 = mybir.dt.bfloat16
AF = mybir.ActivationFunctionType
ALU = mybir.AluOpType
AX = mybir.AxisListType

ENGINES = ("sync", "vector", "scalar", "tensor", "gpsimd")

D_MODEL = 1024
D_INNER = 2048
NS = 16
QCH = 64
PAIR = 128
HALO_B = 257
HEADS = 16
DH = 64
WIN = 256
FP = 5632
FH = 2816


class Cfg:
    def __init__(self, Tc=2048, n_cores=8, nb_layers=6, taps=(),
                 mamba=True, partb=True):
        self.Tc = Tc
        self.n_cores = n_cores
        self.nb_layers = nb_layers
        self.taps = set(taps)
        self.mamba = mamba
        self.partb = partb
        self.npair = Tc // PAIR
        self.nch = Tc // QCH


class Rec:

    def __init__(self):
        self.streams = {e: [] for e in ENGINES}
        self.counts = {}
        self.trace = []
        self.autosem = {}
        self.autocnt = {"vector": 0, "scalar": 0}
        self.lastdone = {}
        self.semtab = {}
        self.makesem = None

    def emit(self, eng, fn, inc=None, inc_n=None):
        if inc is not None:
            n = inc_n if inc_n is not None else (16 if eng in ("sync", "gpsimd") else 1)
        else:
            n = None
        fence = eng in ("vector", "scalar") and eng in self.autosem
        if fence:
            prevtok = self.lastdone.get(eng)
            if inc is not None:
                key = inc.name if hasattr(inc, "name") else id(inc)
                newcnt = self.counts.get(key, 0) + n
                self.counts[key] = newcnt
                marker = (inc, newcnt)
            else:
                asem = self.autosem[eng]
                self.autocnt[eng] += 1
                marker = (asem, self.autocnt[eng])

            amt = n if inc is not None else 1

            def fenced(e, fn=fn, prevtok=prevtok, marker=marker, amt=amt):
                if prevtok is not None:
                    e.wait_ge(prevtok[0], prevtok[1])
                ins = fn(e)
                assert ins is not None, "auto-fenced op must return instruction"
                ins.then_inc(marker[0], amt)
            self.streams[eng].append(fenced)
            self.lastdone[eng] = marker
            self.trace.append((eng, getattr(fn, "_desc", fn.__qualname__),
                               f"marker {marker[1]}"))
            if inc is not None:
                return marker[1]
            return None

        def wrapped(e, fn=fn, inc=inc, n=n):
            ins = fn(e)
            if inc is not None:
                assert ins is not None
                ins.then_inc(inc, n)
        self.streams[eng].append(wrapped)
        if inc is not None:
            key = inc.name if hasattr(inc, "name") else id(inc)
            self.counts[key] = self.counts.get(key, 0) + n
            self.trace.append((eng, getattr(fn, "_desc", fn.__qualname__),
                               f"inc {key}->{self.counts[key]}"))
            return self.counts[key]
        self.trace.append((eng, getattr(fn, "_desc", fn.__qualname__), ""))
        return None

    def sem(self, tag):
        if not isinstance(tag, str):
            return tag
        if tag not in self.semtab:
            self.semtab[tag] = self.makesem("q_" + tag)
        return self.semtab[tag]

    def dma(self, eng, out, in_, inc, serial=True, **kw):
        def fn(e, out=out, in_=in_, kw=kw):
            return e.dma_start(out=out, in_=in_, **kw)
        c = self.emit(eng, fn, inc=self.sem(inc), inc_n=16)
        if serial:
            self.wait(eng, inc, c)
        return c

    def wait(self, eng, sem, count):
        sem = self.sem(sem)
        def fn(e, sem=sem, count=count):
            e.wait_ge(sem, count)
        key = sem.name if hasattr(sem, "name") else id(sem)
        self.trace.append((eng, "wait", f"{key}>={count}"))
        self.streams[eng].append(fn)

    def count(self, sem):
        sem = self.sem(sem)
        key = sem.name if hasattr(sem, "name") else id(sem)
        return self.counts.get(key, 0)

    def barrier_from(self, eng, sem):
        sem = self.sem(sem)
        c = self.count(sem)
        if c:
            self.wait(eng, sem, c)


def ts(i, n):
    return slice(i * n, (i + 1) * n)



def host_constants(cfg):
    c = {}
    m = np.zeros((PAIR, PAIR), np.float32)
    for b in range(2):
        u = np.arange(QCH)
        blk = (u[:, None] <= u[None, :]).astype(np.float32) - 0.5
        m[b * QCH:(b + 1) * QCH, b * QCH:(b + 1) * QCH] = blk
    c["TriC"] = m
    m2 = np.zeros((PAIR, PAIR), np.float32)
    for b in range(2):
        u = np.arange(QCH)
        m2[b * QCH:(b + 1) * QCH, b * QCH:(b + 1) * QCH] = \
            (u[:, None] <= u[None, :]).astype(np.float32)
    c["TriU"] = m2
    sels = np.zeros((NS, PAIR, PAIR), np.float32)
    for s_ in range(NS):
        sels[s_, 0:QCH, 2 * s_] = 1.0
        sels[s_, QCH:PAIR, 64 + 2 * s_] = 1.0
    c["SelS"] = sels
    ind = np.zeros((cfg.npair, cfg.nch, PAIR), np.float32)
    for p in range(cfg.npair):
        ind[p, 2 * p, 0:QCH] = 1.0
        ind[p, 2 * p + 1, QCH:PAIR] = 1.0
    c["PairInd"] = ind
    return c


def host_inputs_mamba(inp, cfg, core):
    Tc = cfg.Tc
    x = np.asarray(inp["L0_attn_x"])[0]
    T = x.shape[0]
    xT = np.ascontiguousarray(x.T)
    pad = np.zeros((D_MODEL, 3), np.float32)
    xTp = np.concatenate([pad, xT], 1)
    s = core * Tc
    d = {}
    d["x0s"] = np.ascontiguousarray(xTp[:, s:s + Tc + 3])
    d["W_inT"] = np.ascontiguousarray(np.asarray(inp["L0_attn_W_in"]).T)
    d["W_dtT"] = np.ascontiguousarray(np.asarray(inp["L0_attn_W_dt"]).T)
    d["W_xT"] = np.ascontiguousarray(np.asarray(inp["L0_attn_W_x"]).T)
    d["W_outT"] = np.ascontiguousarray(np.asarray(inp["L0_attn_W_out"]).T)
    d["conv_w"] = np.ascontiguousarray(np.asarray(inp["L0_attn_conv_w"])[:, 0, :])
    d["conv_b"] = np.asarray(inp["L0_attn_conv_b"]).reshape(D_INNER, 1)
    d["b_dt_row"] = np.asarray(inp["L0_attn_b_dt"]).reshape(1, D_INNER)
    d["A_log_row"] = np.asarray(inp["L0_attn_A_log"]).reshape(1, NS)
    d["D_row"] = np.asarray(inp["L0_attn_D_param"]).reshape(1, D_INNER)
    d["g_row"] = np.asarray(inp["L0_attn_ln_g"]).reshape(1, D_INNER)
    d["bln_row"] = np.asarray(inp["L0_attn_ln_b"]).reshape(1, D_INNER)
    pm = np.zeros((NS, 8), np.float32)
    pm[:, :core] = 1.0
    d["pmask"] = pm
    import ml_dtypes
    bf = ml_dtypes.bfloat16
    cst = host_constants(cfg)
    d["TriC"] = cst["TriC"]
    d["TriU"] = cst["TriU"].astype(bf)
    d["SelS"] = np.ascontiguousarray(
        cst["SelS"].transpose(1, 0, 2).reshape(PAIR, NS * PAIR)).astype(bf)
    d["PairInd"] = np.ascontiguousarray(
        cst["PairInd"].transpose(1, 0, 2).reshape(cfg.nch, cfg.npair * PAIR)).astype(bf)
    d["ident_in"] = np.eye(PAIR, dtype=np.float32)
    d["identb_in"] = np.eye(PAIR, dtype=np.float32).astype(bf)
    return d



class Build:
    def __init__(self, cfg):
        self.cfg = cfg
        self.nc = bass.Bass("TRN2", target_bir_lowering=False, debug=False)
        self.r = Rec()
        self.es = contextlib.ExitStack()
        self.sb = {}
        self.ps = {}
        self.dr = {}
        self.sem = {}
        self.taps_out = {}

    def dram_in(self, name, shape, dtype=F32):
        self.dr[name] = self.nc.dram_tensor(name, list(shape), dtype,
                                            kind="ExternalInput").ap()
        return self.dr[name]

    def dram_out(self, name, shape, dtype=F32):
        self.dr[name] = self.nc.dram_tensor(name, list(shape), dtype,
                                            kind="ExternalOutput").ap()
        return self.dr[name]

    def dram_tmp(self, name, shape, dtype=F32):
        kind = "ExternalOutput" if name in self.cfg.taps else "Internal"
        self.dr[name] = self.nc.dram_tensor(name, list(shape), dtype, kind=kind).ap()
        return self.dr[name]

    def sbuf(self, name, shape, dtype=F32):
        t = self.es.enter_context(self.nc.sbuf_tensor(name, list(shape), dtype))
        self.sb[name] = t
        return t

    def psum(self, name, shape, dtype=F32):
        t = self.es.enter_context(self.nc.psum_tensor(name, list(shape), dtype))
        self.ps[name] = t
        return t

    def init_autofence(self):
        self.r.makesem = self.semaphore
        for eng in ("vector", "scalar"):
            sem = self.semaphore("fence_" + eng)
            self.r.autosem[eng] = sem

    def register_consts(self, values):
        vals = sorted(set(float(v) for v in values))
        ct = self.sbuf("consttile", [128, len(vals)], F32)
        for j, v in enumerate(vals):
            self.nc.const_aps.register(F32, v, ct[:, j:j + 1])
        for j, v in enumerate(vals):
            self.r.emit("vector",
                        lambda e, j=j, v=v: e.memset(ct[:, j:j + 1], v),
                        inc=(self.sem["sV"] if j == len(vals) - 1 else None))

    def semaphore(self, name):
        s = self.es.enter_context(self.nc.semaphore(name))
        self.sem[name] = s
        return s

    def tap(self, name, ap_fn, shape, dtype=F32, eng="gpsimd"):
        if name not in self.cfg.taps:
            return
        o = self.dram_out("tap_" + name, shape, dtype)
        self.taps_out[name] = (o, ap_fn, eng)

    def finalize_block(self):
        with self.nc.Block() as block:
            for eng in ENGINES:
                getattr(block, eng)(lambda e, eng=eng: [f(e) for f in self.r.streams[eng]])

    def run_streams_and_close(self):
        self.finalize_block()
        for sc in reversed(getattr(self, "scopes", [])):
            sc.close()
        self.es.close()
        return self.nc





"""Mamba phase builder (v3: phase-scoped SBUF, DRAM-backed per-pair data)."""


MX0 = HALO_B - 3


def phase_barrier(b):
    r = b.r
    for eng in ("sync", "vector", "scalar", "tensor", "gpsimd"):
        for s in b.sem.values():
            c = b.r.count(s)
            if c:
                r.wait(eng, s, c)


def build_mamba(b):
    cfg = b.cfg
    Tc, npair, nch = cfg.Tc, cfg.npair, cfg.nch
    NCORE = cfg.n_cores
    r = b.r

    x0s = b.dram_in("x0s", [D_MODEL, Tc + 3])
    W_inT = b.dram_in("W_inT", [D_MODEL, 2 * D_INNER])
    W_dtT = b.dram_in("W_dtT", [D_INNER, D_INNER])
    W_xT = b.dram_in("W_xT", [D_INNER, 2 * NS])
    W_outT = b.dram_in("W_outT", [D_INNER, D_MODEL])
    conv_wD = b.dram_in("conv_w", [D_INNER, 4])
    conv_bD = b.dram_in("conv_b", [D_INNER, 1])
    b_dtD = b.dram_in("b_dt_row", [1, D_INNER])
    A_logD = b.dram_in("A_log_row", [1, NS])
    D_rowD = b.dram_in("D_row", [1, D_INNER])
    g_rowD = b.dram_in("g_row", [1, D_INNER])
    bln_rowD = b.dram_in("bln_row", [1, D_INNER])
    TriCD = b.dram_in("TriC", [PAIR, PAIR])
    TriUD = b.dram_in("TriU", [PAIR, PAIR], BF16)
    SelSD = b.dram_in("SelS", [PAIR, NS * PAIR], BF16)
    PairIndD = b.dram_in("PairInd", [nch, npair * PAIR], BF16)
    pmaskD = b.dram_in("pmask", [NS, 8])
    identD = b.dram_in("ident_in", [PAIR, PAIR])
    identbD = b.dram_in("identb_in", [PAIR, PAIR], BF16)

    WzbD = b.dram_tmp("WzbD", [D_MODEL, D_INNER], BF16)
    WoutbD = b.dram_tmp("WoutbD", [D_INNER, D_MODEL], BF16)
    xcFMD = b.dram_tmp("xcFMD", [D_INNER, Tc], BF16)
    xcTMD = b.dram_tmp("xcTMD", [Tc, D_INNER], BF16)
    slocD = b.dram_tmp("slocD", [Tc, D_INNER])
    dtxcD = b.dram_tmp("dtxcD", [Tc, D_INNER])
    stot2D = b.dram_tmp("stot2D", [nch, D_INNER])
    prawD = b.dram_tmp("prawD", [nch, NS, D_INNER])
    hinjD = b.dram_tmp("hinjD", [NS, nch, D_INNER], BF16)
    hlocD = b.dram_tmp("hlocD", [NS, D_INNER, nch])
    bscanD = b.dram_tmp("bscanD", [NS, D_INNER, nch])
    hcolD = b.dram_tmp("hcolD", [16, 128])
    hfullD = b.dram_tmp("hfullD", [NS, D_INNER, nch])
    yD = b.dram_tmp("yD", [cfg.Tc, D_INNER])
    wD = b.dram_tmp("wD", [cfg.Tc, D_INNER], BF16)
    zD = b.dram_tmp("zD", [cfg.Tc, D_INNER])
    hh2D = b.dram_tmp("hh2D", [NS, D_INNER, nch])
    cumD = b.dram_tmp("cumD", [NS, D_INNER, nch])
    ED = b.dram_tmp("ED", [NS, D_INNER, nch])
    cc_in = b.dram_tmp("cc_in", [NS, 2 * D_INNER])
    cc_out = b.dram_tmp("cc_out", [NCORE * NS, 2 * D_INNER])
    x1saveD = b.dram_tmp("x1saveD", [D_MODEL, Tc])
    b.x1saveD = x1saveD

    sL = b.semaphore("sL")
    sG = b.semaphore("sG")
    sT = b.semaphore("sT")
    sV = b.semaphore("sV")
    sS = b.semaphore("sS")
    sC = b.semaphore("sC")
    b.init_autofence()

    S = [b.sbuf(f"S{i}", [128, HALO_B + Tc], F32) for i in range(8)]
    b.S = S
    identf = b.sbuf("identf", [128, 128], F32)
    identb = b.sbuf("identb", [128, 128], BF16)
    ones_colb = b.sbuf("ones_colb", [128, 1], BF16)
    ones33 = b.sbuf("ones33", [128, 33], BF16)
    ones_rowf = b.sbuf("ones_rowf", [1, 128], F32)
    b.register_consts([1e-5, -0.5, -1.0, 0.5, 2.0, 1.0 / 1024, 1.0 / 2048,
                       1e-24])

    pwide = b.psum("pwide", [128, 2048], F32)
    pwide2 = b.psum("pwide2", [128, 1024], F32)
    ptr = b.psum("ptr", [128, 1024], BF16)
    ptrf = b.psum("ptrf", [128, 512], F32)

    ms = contextlib.ExitStack()
    b.mamba_scope = ms
    b.scopes = [ms]

    def msb(name, shape, dtype=F32):
        t = ms.enter_context(b.nc.sbuf_tensor(name, list(shape), dtype))
        b.sb[name] = t
        return t

    triC = msb("triC", [128, 128], F32)
    triU = msb("triU", [128, 128], BF16)
    selS = msb("selS", [128, NS * 128], BF16)
    pairInd = msb("pairInd", [nch, npair * 128], BF16)
    Acol = msb("Acol", [128, NS], F32)
    nAcol = msb("nAcol", [128, NS], F32)
    A2col = msb("A2col", [128, NS], F32)
    ssmT = msb("ssmT", [128, npair * 2 * NS], F32)
    arow = msb("arow", [1, NS], F32)
    hcol = msb("hcol", [128, 1], F32)
    stage = msb("stage", [128, 512], F32)
    evb = msb("evb", [128, D_INNER], F32)
    scr = msb("scr", [128, D_INNER], F32)
    slocp = msb("slocp", [128, D_INNER], F32)
    dtxcp = msb("dtxcp", [128, D_INNER], F32)

    for i in range(8):
        r.dma("sync", S[i][:, MX0:MX0 + Tc + 3], x0s[ts(i, 128), :], sL)
    r.dma("sync", triC[:], TriCD, sL)
    r.dma("sync", triU[:], TriUD, sL)
    r.dma("sync", selS[:], SelSD, sL)
    r.dma("sync", pairInd[:], PairIndD, sL)
    r.dma("sync", identf[:], identD, sL)
    r.dma("sync", identb[:], identbD, sL)
    r.dma("sync", arow[:], A_logD, sL)
    r.wait("vector", sL, r.count(sL))
    r.emit("vector", lambda e: e.memset(ones_colb[:], 1.0))
    r.emit("vector", lambda e: e.memset(ones33[:], 0.0))
    r.emit("vector", lambda e: e.memset(ones33[:, 32:33], 1.0))
    r.emit("vector", lambda e: e.memset(ones_rowf[:], 1.0), inc=sV)
    r.wait("tensor", sV, r.count(sV))
    r.emit("tensor", lambda e: e.matmul(out=ptrf[:, 0:NS], lhsT=ones_rowf[:],
                                        rhs=arow[:], start=True, stop=True), inc=sT)
    r.wait("scalar", sT, r.count(sT))
    r.wait("scalar", sV, r.count(sV))
    r.emit("scalar", lambda e: e.activation(out=nAcol[:], in_=ptrf[:, 0:NS],
                                            func=AF.Exp), inc=sS)
    r.wait("vector", sS, r.count(sS))
    r.emit("vector", lambda e: e.tensor_scalar_mul(out=Acol[:], in0=nAcol[:],
                                                   scalar1=-1.0))
    r.emit("vector", lambda e: e.tensor_scalar_mul(out=A2col[:], in0=Acol[:],
                                                   scalar1=2.0), inc=sV)
    r.barrier_from("tensor", sV)

    sa = contextlib.ExitStack()

    def sab(name, shape, dtype=F32):
        return sa.enter_context(b.nc.sbuf_tensor(name, list(shape), dtype))

    wld = [sab("wldA0", [128, D_INNER], F32)]
    wldZ = wld[0]
    gbufA = [sab(f"gbufA{i}", [128, D_INNER], BF16) for i in range(2)]
    Wxpb = [sab(f"Wxpb{i}", [128, D_INNER], BF16) for i in range(8)]
    x0b = [sab(f"x0bA{i}", [128, Tc + 3], BF16) for i in range(8)]
    xpbuf = sab("xpbuf", [128, Tc + 3], F32)
    convw = sab("convwA", [128, 4 * 16], F32)
    convb = sab("convbA", [128, 16], F32)
    xcft = gbufA[1]

    for i in range(16):
        r.dma("sync", convw[:, 4 * i:4 * i + 4], conv_wD[ts(i, 128), :], sL)
        r.dma("sync", convb[:, i:i + 1], conv_bD[ts(i, 128), :], sL)
    for k in range(8):
        r.barrier_from("sync", sV)
        r.dma("sync", wld[0][:], W_inT[ts(k, 128), 0:D_INNER], sL)
        r.wait("vector", sL, r.count(sL))
        r.emit("vector", lambda e, k=k: e.tensor_copy(out=Wxpb[k][:],
                                                      in_=wld[0][:]))
        r.emit("vector", lambda e, k=k: e.tensor_copy(out=x0b[k][:],
                                                      in_=S[k][:, MX0:MX0 + Tc + 3]),
               inc=sV)
        r.barrier_from("sync", sV)
        r.dma("sync", wldZ[:], W_inT[ts(k, 128), D_INNER:], sL)
        r.wait("vector", sL, r.count(sL))
        r.emit("vector", lambda e, k=k: e.tensor_copy(out=gbufA[k % 2][:],
                                                      in_=wldZ[:]), inc=sV)
        r.wait("gpsimd", sV, r.count(sV))
        r.dma("gpsimd", WzbD[ts(k, 128), :], gbufA[k % 2][:], sG)
        r.wait("vector", sG, max(0, r.count(sG) - 16))
    r.barrier_from("sync", sG)
    r.barrier_from("vector", sG)
    for k in range(16):
        r.barrier_from("sync", sV)
        r.dma("sync", wld[0][:, 0:D_MODEL], W_outT[ts(k, 128), :], sL)
        r.wait("vector", sL, r.count(sL))
        r.emit("vector", lambda e, k=k: e.tensor_copy(
            out=gbufA[k % 2][:, 0:D_MODEL], in_=wld[0][:, 0:D_MODEL]), inc=sV)
        r.wait("gpsimd", sV, r.count(sV))
        r.dma("gpsimd", WoutbD[ts(k, 128), :], gbufA[k % 2][:, 0:D_MODEL], sG)
        r.wait("vector", sG, max(0, r.count(sG) - 16))
    r.barrier_from("sync", sG)
    r.barrier_from("vector", sG)

    W = Tc + 3
    nblocks = [(o, min(512, W - o)) for o in range(0, W, 512)]
    for eM in range(16):
        r.barrier_from("tensor", sV)
        r.barrier_from("tensor", sS)
        for (o, w) in nblocks:
            for k in range(8):
                def mm_xz(e, eM=eM, o=o, w=w, k=k):
                    dst = pwide[:, o:o + w] if o < 2048 else pwide2[:, 0:w]
                    return e.matmul(out=dst, lhsT=Wxpb[k][:, ts(eM, 128)],
                                    rhs=x0b[k][:, o:o + w],
                                    start=(k == 0), stop=(k == 7))
                r.emit("tensor", mm_xz,
                       inc=(sT if (k == 7 and o == nblocks[-1][0]) else None))
        r.wait("vector", sT, r.count(sT))
        r.emit("vector", lambda e: e.tensor_copy(out=xpbuf[:, 0:min(2048, W)],
                                                 in_=pwide[:, 0:min(2048, W)]))
        if W > 2048:
            r.emit("vector", lambda e: e.tensor_copy(out=xpbuf[:, 2048:W],
                                                     in_=pwide2[:, 0:W - 2048]))
        r.emit("vector", lambda e, eM=eM: e.tensor_scalar(
            out=evb[:, 0:Tc], in0=xpbuf[:, 0:Tc],
            scalar1=convw[:, 4 * eM:4 * eM + 1], scalar2=None, op0=ALU.mult))
        for tk in (1, 2, 3):
            r.emit("vector", lambda e, eM=eM, tk=tk: e.scalar_tensor_tensor(
                out=evb[:, 0:Tc], in0=xpbuf[:, tk:tk + Tc],
                scalar=convw[:, 4 * eM + tk:4 * eM + tk + 1], in1=evb[:, 0:Tc],
                op0=ALU.mult, op1=ALU.add),
                inc=(sV if tk == 3 else None))
        r.wait("scalar", sV, r.count(sV))
        r.emit("scalar", lambda e, eM=eM: e.activation(
            out=xcft[:, 0:Tc], in_=evb[:, 0:Tc], func=AF.Silu,
            bias=convb[:, eM:eM + 1]), inc=sS)
        r.wait("gpsimd", sS, r.count(sS))
        r.dma("gpsimd", xcFMD[ts(eM, 128), :], xcft[:, 0:Tc], sG)
        r.wait("scalar", sG, r.count(sG))
        r.wait("vector", sS, r.count(sS))
    phase_barrier(b)
    sa.close()

    sb_ = contextlib.ExitStack()

    def sbb(name, shape, dtype=F32):
        return sb_.enter_context(b.nc.sbuf_tensor(name, list(shape), dtype))

    Wdtb = [sbb(f"Wdtb{i}", [128, D_INNER], BF16) for i in range(16)]
    Wxb = [sbb(f"Wxb{i}", [128, 2 * NS], BF16) for i in range(16)]
    wldB = [sbb("wldB0", [128, D_INNER], F32)]
    xcp = [sbb(f"xcp{i}", [128, 128], BF16) for i in range(16)]
    xcTb = sbb("xcTb", [128, D_INNER], BF16)
    Bdt = sbb("Bdt", [128, D_INNER], F32)
    bdtrow = scr

    r.dma("sync", bdtrow[0:1, 0:D_INNER], b_dtD, sL)
    for i in range(16):
        r.barrier_from("sync", sV)
        r.dma("sync", wldB[0][:], W_dtT[ts(i, 128), :], sL)
        r.wait("vector", sL, r.count(sL))
        r.emit("vector", lambda e, i=i: e.tensor_copy(
            out=Wdtb[i][:], in_=wldB[0][:]), inc=sV)
    for i in range(16):
        r.barrier_from("sync", sV)
        r.dma("sync", wldB[0][:, 0:2 * NS], W_xT[ts(i, 128), :], sL)
        r.wait("vector", sL, r.count(sL))
        r.emit("vector", lambda e, i=i: e.tensor_copy(
            out=Wxb[i][:], in_=wldB[0][:, 0:2 * NS]), inc=sV)
    r.barrier_from("tensor", sV)
    for blk in range(4):
        r.emit("tensor", lambda e, blk=blk: e.matmul(
            out=pwide[:, ts(blk, 512)], lhsT=ones_rowf[:],
            rhs=bdtrow[0:1, ts(blk, 512)], start=True, stop=True),
            inc=(sT if blk == 3 else None))
    r.wait("vector", sT, r.count(sT))
    r.emit("vector", lambda e: e.tensor_copy(out=Bdt[:], in_=pwide[:]), inc=sV)
    r.barrier_from("tensor", sV)

    for p in range(npair):
        for i in range(16):
            r.dma("sync", xcp[i][:], xcFMD[ts(i, 128), ts(p, 128)], sL)
        r.barrier_from("tensor", sV)
        r.barrier_from("tensor", sS)
        r.wait("tensor", sL, r.count(sL))
        for k in range(16):
            r.emit("tensor", lambda e, k=k: e.matmul(
                out=ptrf[:, 0:2 * NS], lhsT=xcp[k][:], rhs=Wxb[k][:],
                start=(k == 0), stop=(k == 15)), inc=(sT if k == 15 else None))
        r.wait("vector", sT, r.count(sT))
        r.emit("vector", lambda e, p=p: e.tensor_copy(
            out=ssmT[:, p * 2 * NS:(p + 1) * 2 * NS], in_=ptrf[:, 0:2 * NS]),
            inc=sV)
        for ob in range(4):
            for k in range(16):
                r.emit("tensor", lambda e, ob=ob, k=k: e.matmul(
                    out=pwide[:, ts(ob, 512)], lhsT=xcp[k][:],
                    rhs=Wdtb[k][:, ts(ob, 512)], start=(k == 0), stop=(k == 15)),
                    inc=(sT if (k == 15 and ob == 3) else None))
        r.wait("vector", sT, r.count(sT))
        r.emit("vector", lambda e: e.tensor_add(out=scr[:], in0=pwide[:],
                                                in1=Bdt[:]), inc=sV)
        r.wait("scalar", sV, r.count(sV))
        r.emit("scalar", lambda e: e.activation(out=evb[:], in_=scr[:],
                                                func=AF.Abs))
        r.emit("scalar", lambda e: e.activation(out=evb[:], in_=evb[:],
                                                func=AF.Exp, scale=-1.0))
        r.emit("scalar", lambda e: e.activation(out=evb[:], in_=evb[:],
                                                func=AF.Ln, bias=1.0), inc=sS)
        r.wait("vector", sS, r.count(sS))
        r.emit("vector", lambda e: e.scalar_tensor_tensor(
            out=scr[:], in0=scr[:], scalar=0.0, in1=evb[:], op0=ALU.max,
            op1=ALU.add), inc=sV)
        r.barrier_from("tensor", sV)
        for ob in range(4):
            r.emit("tensor", lambda e, ob=ob: e.matmul(
                out=pwide[:, ts(ob, 512)], lhsT=triC[:], rhs=scr[:, ts(ob, 512)],
                start=True, stop=True), inc=(sT if ob == 3 else None))
        r.wait("vector", sT, r.count(sT))
        r.emit("vector", lambda e: e.tensor_copy(out=slocp[:], in_=pwide[:]),
               inc=sV)
        r.wait("gpsimd", sV, r.count(sV))
        r.dma("gpsimd", slocD[ts(p, 128), :], slocp[:], sG)
        r.dma("gpsimd", stot2D[2 * p:2 * p + 2, :], slocp[63:128:64, :], sG)
        r.barrier_from("tensor", sV)
        for half in range(2):
            for i in range(8):
                r.emit("tensor", lambda e, half=half, i=i: e.transpose(
                    out=ptr[:, ts(i, 128)], in_=xcp[half * 8 + i][:],
                    identity=identb[:]), inc=(sT if i == 7 else None))
            r.wait("vector", sT, r.count(sT))
            for i in range(8):
                r.emit("vector", lambda e, half=half, i=i: e.tensor_copy(
                    out=xcTb[:, ts(half * 8 + i, 128)], in_=ptr[:, ts(i, 128)]),
                    inc=(sV if i == 7 else None))
            r.barrier_from("tensor", sV)
        r.emit("vector", lambda e: e.tensor_mul(out=dtxcp[:], in0=scr[:],
                                                in1=xcTb[:]), inc=sV)
        r.wait("gpsimd", sV, r.count(sV))
        r.dma("gpsimd", xcTMD[ts(p, 128), :], xcTb[:], sG)
        r.dma("gpsimd", dtxcD[ts(p, 128), :], dtxcp[:], sG)
        r.wait("sync", sG, r.count(sG))
        r.wait("vector", sG, r.count(sG))
        r.wait("scalar", sG, r.count(sG))
    phase_barrier(b)
    sb_.close()

    sc_ = contextlib.ExitStack()

    def scb(name, shape, dtype=F32):
        return sc_.enter_context(b.nc.sbuf_tensor(name, list(shape), dtype))

    gbufC = [scb(f"gbufC{i}", [128, D_INNER], BF16) for i in range(2)]
    wbufC = scb("wbufC", [128, D_INNER], F32)
    chn = [scb(f"chain{i}", [128, nch], F32) for i in range(6)]
    zerosch = scb("zerosch", [128, nch], F32)
    rankt = scb("rankt", [NS, 2 * D_INNER], F32)
    h0tile = scb("h0tile", [NS, D_INNER], F32)
    mcol = scb("mcol", [NS, 8], F32)
    stot2T = [scb(f"stot2T{i}", [128, nch], F32) for i in range(16)]
    h0T = scb("h0T", [128, 16 * NS], F32)

    r.dma("sync", mcol[:], pmaskD, sL)
    r.wait("vector", sL, r.count(sL))
    r.emit("vector", lambda e: e.memset(zerosch[:], 0.0), inc=sV)
    tcol = []
    for p in range(npair):
        r.barrier_from("sync", sV)
        r.barrier_from("sync", sS)
        r.dma("sync", slocp[:], slocD[ts(p, 128), :], sL)
        r.dma("sync", dtxcp[:], dtxcD[ts(p, 128), :], sL)
        r.barrier_from("scalar", sL)
        r.barrier_from("vector", sL)
        tcol.clear()
        for s in range(NS):
            r.wait("scalar", sV, r.count(sV))
            r.emit("scalar", lambda e, s=s: e.activation(
                out=wbufC[:], in_=slocp[:], func=AF.Exp, scale=nAcol[:, s:s + 1]),
                inc=sS)
            r.wait("vector", sS, r.count(sS))
            if s >= 2:
                r.wait("vector", sT, tcol[s - 2])
            r.emit("vector", lambda e, p=p, s=s: e.scalar_tensor_tensor(
                out=gbufC[s % 2][:], in0=wbufC[:],
                scalar=ssmT[:, p * 2 * NS + s:p * 2 * NS + s + 1],
                in1=dtxcp[:], op0=ALU.mult, op1=ALU.mult), inc=sV)
            r.wait("tensor", sV, r.count(sV))
            for ob in range(4):
                r.emit("tensor", lambda e, s=s, ob=ob: e.matmul(
                    out=pwide[:, ts(ob, 512)], lhsT=selS[:, ts(s, 128)],
                    rhs=gbufC[s % 2][:, ts(ob, 512)],
                    start=(s == 0), stop=(s == NS - 1)),
                    inc=(sT if ob == 3 else None))
            tcol.append(r.count(sT))
        r.wait("vector", sT, r.count(sT))
        for ob in range(4):
            r.emit("vector", lambda e, ob=ob: e.tensor_copy(
                out=stage[:], in_=pwide[:, ts(ob, 512)]), inc=sV)
            r.wait("gpsimd", sV, r.count(sV))
            r.dma("gpsimd", prawD[2 * p, :, ts(ob, 512)], stage[0:32:2, :], sG)
            r.dma("gpsimd", prawD[2 * p + 1, :, ts(ob, 512)], stage[64:96:2, :],
                  sG)
            r.wait("vector", sG, r.count(sG))
        r.barrier_from("tensor", sV)
    r.barrier_from("sync", sG)
    for dblk in range(16):
        r.dma("sync", stage[0:nch, 0:128], stot2D[:, ts(dblk, 128)], sL)
        r.wait("tensor", sL, r.count(sL))
        r.wait("tensor", sV, r.count(sV))
        r.emit("tensor", lambda e: e.transpose(
            out=ptrf[:, 0:nch], in_=stage[0:nch, 0:128],
            identity=identf[0:nch, 0:nch]), inc=sT)
        r.wait("vector", sT, r.count(sT))
        r.emit("vector", lambda e, dblk=dblk: e.tensor_copy(
            out=stot2T[dblk][:], in_=ptrf[:, 0:nch]), inc=sV)
        r.barrier_from("sync", sV)
        r.barrier_from("tensor", sV)
    for dblk in range(16):
        r.wait("vector", sS, r.count(sS))
        for s in range(NS):
            r.wait("scalar", sV, r.count(sV))
            r.wait("scalar", sG, r.count(sG))
            r.emit("scalar", lambda e, dblk=dblk, s=s: e.activation(
                out=chn[0][:], in_=stot2T[dblk][:], func=AF.Exp,
                scale=Acol[:, s:s + 1]), inc=sS)
            r.barrier_from("sync", sG)
            r.dma("sync", stage[0:nch, 0:128], prawD[:, s, ts(dblk, 128)], sL)
            r.wait("tensor", sL, r.count(sL))
            r.wait("tensor", sS, r.count(sS))
            r.wait("tensor", sV, r.count(sV))
            r.emit("tensor", lambda e: e.transpose(
                out=ptrf[:, 0:nch], in_=stage[0:nch, 0:128],
                identity=identf[0:nch, 0:nch]), inc=sT)
            r.wait("vector", sT, r.count(sT))
            r.wait("vector", sS, r.count(sS))
            r.wait("vector", sG, r.count(sG))
            r.emit("vector", lambda e: e.tensor_mul(out=chn[1][:], in0=chn[0][:],
                                                    in1=chn[0][:]))
            r.emit("vector", lambda e: e.tensor_mul(out=chn[2][:], in0=chn[0][:],
                                                    in1=ptrf[:, 0:nch]), inc=sV)
            r.wait("vector", sV, r.count(sV))
            r.emit("vector", lambda e: e.tensor_tensor_scan(
                out=chn[3][:], data0=chn[1][:], data1=chn[2][:], initial=0.0,
                op0=ALU.mult, op1=ALU.add))
            r.emit("vector", lambda e: e.tensor_tensor_scan(
                out=chn[4][:], data0=chn[1][:], data1=zerosch[:], initial=1.0,
                op0=ALU.mult, op1=ALU.add), inc=sV)
            r.wait("gpsimd", sV, r.count(sV))
            r.dma("gpsimd", cc_in[s:s + 1,
                                  D_INNER + dblk * 128:D_INNER + (dblk + 1) * 128],
                  chn[3][:, nch - 1:nch], sG)
            r.dma("gpsimd", cc_in[s:s + 1, ts(dblk, 128)],
                  chn[4][:, nch - 1:nch], sG)
            r.dma("gpsimd", hlocD[s, ts(dblk, 128), :], chn[3][:], sG)
            if "bscanD" in cfg.taps:
                r.dma("gpsimd", bscanD[s, ts(dblk, 128), :], chn[2][:], sG)
            r.dma("gpsimd", cumD[s, ts(dblk, 128), :], chn[4][:], sG)
            r.dma("gpsimd", ED[s, ts(dblk, 128), :], chn[0][:], sG)
            r.wait("vector", sG, r.count(sG))
            r.wait("tensor", sG, r.count(sG))
    r.barrier_from("gpsimd", sG)
    if NCORE > 1:
        r.emit("gpsimd", lambda e: e.collective_compute(
            "AllGather", ALU.bypass, replica_groups=[list(range(NCORE))],
            ins=[cc_in.opt()], outs=[cc_out.opt()]), inc=sC, inc_n=1)
        r.wait("gpsimd", sC, 1)
        r.wait("sync", sC, 1)
    r.barrier_from("vector", sG)
    r.emit("vector", lambda e: e.memset(h0tile[:], 0.0), inc=sV)
    if NCORE > 1:
        for rk in range(NCORE):
            r.barrier_from("sync", sV)
            r.dma("sync", rankt[:], cc_out[rk * NS:(rk + 1) * NS, :], sL)
            r.wait("vector", sL, r.count(sL))
            r.emit("vector", lambda e: e.tensor_scalar(
                out=rankt[:, 0:D_INNER], in0=rankt[:, 0:D_INNER],
                scalar1=-1.0, scalar2=None, op0=ALU.add))
            r.emit("vector", lambda e, rk=rk: e.tensor_scalar(
                out=rankt[:, 0:D_INNER], in0=rankt[:, 0:D_INNER],
                scalar1=mcol[:, rk:rk + 1], scalar2=1.0, op0=ALU.mult,
                op1=ALU.add))
            r.emit("vector", lambda e, rk=rk: e.tensor_scalar(
                out=rankt[:, D_INNER:], in0=rankt[:, D_INNER:],
                scalar1=mcol[:, rk:rk + 1], scalar2=None, op0=ALU.mult))
            r.emit("vector", lambda e: e.tensor_mul(
                out=h0tile[:], in0=h0tile[:], in1=rankt[:, 0:D_INNER]))
            r.emit("vector", lambda e: e.tensor_add(
                out=h0tile[:], in0=h0tile[:], in1=rankt[:, D_INNER:]), inc=sV)
    for dblk in range(16):
        r.wait("tensor", sV, r.count(sV))
        r.emit("tensor", lambda e, dblk=dblk: e.transpose(
            out=ptrf[:, 0:NS], in_=h0tile[:, ts(dblk, 128)],
            identity=identf[0:NS, 0:NS]), inc=sT)
        r.wait("vector", sT, r.count(sT))
        r.emit("vector", lambda e, dblk=dblk: e.tensor_copy(
            out=h0T[:, ts(dblk, NS)], in_=ptrf[:, 0:NS]), inc=sV)
        r.barrier_from("tensor", sV)
    for dblk in range(16):
        for s in range(NS):
            r.barrier_from("sync", sG)
            r.dma("sync", chn[3][:], hlocD[s, ts(dblk, 128), :], sL)
            r.dma("sync", chn[4][:], cumD[s, ts(dblk, 128), :], sL)
            r.dma("sync", chn[0][:], ED[s, ts(dblk, 128), :], sL)
            r.wait("vector", sL, r.count(sL))
            r.emit("vector", lambda e, dblk=dblk, s=s: e.scalar_tensor_tensor(
                out=chn[5][:], in0=chn[4][:],
                scalar=h0T[:, dblk * NS + s:dblk * NS + s + 1],
                in1=chn[3][:], op0=ALU.mult, op1=ALU.add), inc=sV)
            r.wait("vector", sV, r.count(sV))
            if nch > 1:
                r.emit("vector", lambda e: e.tensor_mul(
                    out=chn[2][:, 1:nch], in0=chn[0][:, 1:nch],
                    in1=chn[5][:, 0:nch - 1]))
            r.emit("vector", lambda e, dblk=dblk, s=s: e.tensor_scalar(
                out=chn[2][:, 0:1], in0=chn[0][:, 0:1],
                scalar1=h0T[:, dblk * NS + s:dblk * NS + s + 1], scalar2=None,
                op0=ALU.mult), inc=sV)
            if "hh2D" in cfg.taps:
                r.wait("gpsimd", sV, r.count(sV))
                r.dma("gpsimd", hfullD[s, ts(dblk, 128), :], chn[5][:], sG)
                r.dma("gpsimd", hh2D[s, ts(dblk, 128), :], chn[2][:], sG)
            r.wait("tensor", sV, r.count(sV))
            r.emit("tensor", lambda e: e.transpose(
                out=ptrf[0:nch, 0:128], in_=chn[2][:, 0:nch],
                identity=identf[:]), inc=sT)
            r.wait("vector", sT, r.count(sT))
            r.emit("vector", lambda e: e.tensor_copy(
                out=gbufC[0][0:nch, 0:128], in_=ptrf[0:nch, 0:128]), inc=sV)
            r.wait("gpsimd", sV, r.count(sV))
            r.dma("gpsimd", hinjD[s, :, ts(dblk, 128)], gbufC[0][0:nch, 0:128], sG)
            r.wait("vector", sG, r.count(sG))
            r.wait("tensor", sG, r.count(sG))
    phase_barrier(b)
    sc_.close()

    sd_ = contextlib.ExitStack()

    def sdb(name, shape, dtype=F32):
        return sd_.enter_context(b.nc.sbuf_tensor(name, list(shape), dtype))

    gbufD = [sdb(f"gbufD{i}", [128, D_INNER], BF16) for i in range(2)]
    wbufD = sdb("wbufD", [128, D_INNER], F32)
    ypair = sdb("ypair", [128, D_INNER], F32)
    xcTb2 = sdb("xcTb2", [128, D_INNER], BF16)
    hinjb = [sdb(f"hinjb{i}", [nch, D_INNER], BF16) for i in range(2)]
    wpairb = sdb("wpairb", [128, D_INNER], BF16)
    wfm = [sdb(f"wfm{i}", [128, 512], BF16) for i in range(16)]
    zxb = [sdb(f"zxb{i}", [128, 128], BF16) for i in range(8)]
    GTt = sdb("GTt", [128, D_INNER], F32)
    BTt = sdb("BTt", [128, D_INNER], F32)
    DTt = sdb("DTt", [128, D_INNER], F32)
    growB = scr

    r.emit("vector", lambda e: e.memset(hinjb[0][:], 0.0))
    r.emit("vector", lambda e: e.memset(hinjb[1][:], 0.0), inc=sV)
    for nm, dst in (("g_row", GTt), ("bln_row", BTt), ("D_row", DTt)):
        r.barrier_from("sync", sV)
        r.dma("sync", growB[0:1, 0:D_INNER], b.dr[nm], sL)
        r.barrier_from("tensor", sL)
        r.barrier_from("tensor", sV)
        for blk in range(4):
            r.emit("tensor", lambda e, blk=blk: e.matmul(
                out=pwide[:, ts(blk, 512)], lhsT=ones_rowf[:],
                rhs=growB[0:1, ts(blk, 512)], start=True, stop=True),
                inc=(sT if blk == 3 else None))
        r.wait("vector", sT, r.count(sT))
        r.emit("vector", lambda e, dst=dst: e.tensor_copy(out=dst[:],
                                                          in_=pwide[:]), inc=sV)
        r.barrier_from("tensor", sV)

    for p in range(npair):
        r.barrier_from("sync", sV)
        r.barrier_from("sync", sS)
        r.dma("sync", slocp[:], slocD[ts(p, 128), :], sL)
        r.dma("sync", dtxcp[:], dtxcD[ts(p, 128), :], sL)
        r.dma("sync", xcTb2[:], xcTMD[ts(p, 128), :], sL)
        r.barrier_from("scalar", sL)
        r.barrier_from("vector", sL)
        hts = []
        for s in range(NS):
            if s >= 2:
                r.wait("sync", sT, hts[s - 2])
            r.dma("sync", hinjb[s % 2][2 * p:2 * p + 2, :],
                  hinjD[s, 2 * p:2 * p + 2, :], sL)
            r.wait("scalar", sV, r.count(sV))
            r.emit("scalar", lambda e, s=s: e.activation(
                out=wbufD[:], in_=slocp[:], func=AF.Exp, scale=nAcol[:, s:s + 1]),
                inc=sS)
            r.wait("vector", sS, r.count(sS))
            if s >= 2:
                r.wait("vector", sT, hts[s - 2])
            r.emit("vector", lambda e, p=p, s=s: e.scalar_tensor_tensor(
                out=gbufD[s % 2][:], in0=wbufD[:],
                scalar=ssmT[:, p * 2 * NS + s:p * 2 * NS + s + 1],
                in1=dtxcp[:], op0=ALU.mult, op1=ALU.mult), inc=sV)
            gv = r.count(sV)
            r.wait("tensor", sV, gv)
            r.wait("tensor", sL, r.count(sL))
            pp = pwide
            for ob in range(4):
                r.emit("tensor", lambda e, p=p, s=s, ob=ob, pp=pp: e.matmul(
                    out=pp[:, ts(ob, 512)], lhsT=triU[:],
                    rhs=gbufD[s % 2][:, ts(ob, 512)], start=True, stop=False))
                r.emit("tensor", lambda e, p=p, s=s, ob=ob, pp=pp: e.matmul(
                    out=pp[:, ts(ob, 512)], lhsT=pairInd[:, ts(p, 128)],
                    rhs=hinjb[s % 2][:, ts(ob, 512)], start=False, stop=True),
                    inc=(sT if ob == 3 else None))
            hts.append(r.count(sT))
            r.wait("scalar", sV, gv)
            r.emit("scalar", lambda e, s=s: e.activation(
                out=wbufD[:], in_=slocp[:], func=AF.Exp, scale=Acol[:, s:s + 1]),
                inc=sS)
            r.wait("vector", sS, r.count(sS))
            r.wait("vector", sT, r.count(sT))
            r.emit("vector", lambda e, pp=pp: e.tensor_mul(out=evb[:],
                                                           in0=wbufD[:],
                                                           in1=pp[:]))
            if s == 0:
                r.emit("vector", lambda e, p=p, s=s: e.tensor_scalar(
                    out=ypair[:], in0=evb[:],
                    scalar1=ssmT[:, p * 2 * NS + NS + s:p * 2 * NS + NS + s + 1],
                    scalar2=None, op0=ALU.mult), inc=sV)
            else:
                r.emit("vector", lambda e, p=p, s=s: e.scalar_tensor_tensor(
                    out=ypair[:], in0=evb[:],
                    scalar=ssmT[:, p * 2 * NS + NS + s:p * 2 * NS + NS + s + 1],
                    in1=ypair[:], op0=ALU.mult, op1=ALU.add), inc=sV)
        if "yD" in cfg.taps:
            r.wait("gpsimd", sV, r.count(sV))
            r.dma("gpsimd", yD[ts(p, 128), :], ypair[:], sG)
        for k in range(8):
            r.emit("vector", lambda e, p=p, k=k: e.tensor_copy(
                out=zxb[k][:],
                in_=S[k][:, HALO_B + p * 128:HALO_B + (p + 1) * 128]),
                inc=(sV if k == 7 else None))
        r.barrier_from("sync", sG)
        r.barrier_from("sync", sT)
        r.barrier_from("tensor", sV)
        zts = []
        for k in range(8):
            if k >= 2:
                r.barrier_from("sync", sT)
            r.dma("sync", gbufD[k % 2][:], WzbD[ts(k, 128), :], sL)
            r.wait("tensor", sL, r.count(sL))
            for ob in range(4):
                r.emit("tensor", lambda e, k=k, ob=ob: e.matmul(
                    out=pwide[:, ts(ob, 512)], lhsT=zxb[k][:],
                    rhs=gbufD[k % 2][:, ts(ob, 512)], start=(k == 0),
                    stop=(k == 7)), inc=(sT if ob == 3 else None))
            zts.append(r.count(sT))
        r.wait("scalar", sT, r.count(sT))
        r.wait("scalar", sV, r.count(sV))
        r.emit("scalar", lambda e: e.activation(out=wbufD[:], in_=pwide[:],
                                                func=AF.Silu), inc=sS)
        r.wait("vector", sS, r.count(sS))
        r.emit("vector", lambda e: e.reduce_sum(out=hcol[:], in_=ypair[:],
                                                axis=AX.X, negate=True), inc=sV)
        r.wait("vector", sV, r.count(sV))
        r.emit("vector", lambda e: e.tensor_scalar_mul(out=hcol[:], in0=hcol[:],
                                                       scalar1=1.0 / D_INNER),
               inc=sV)
        r.wait("vector", sV, r.count(sV))
        r.emit("vector", lambda e: e.tensor_scalar(
            out=scr[:], in0=ypair[:], scalar1=hcol[:], scalar2=None,
            op0=ALU.add), inc=sV)
        r.wait("scalar", sV, r.count(sV))
        r.emit("scalar", lambda e: e.activation(out=evb[:], in_=scr[:],
                                                func=AF.Square,
                                                accum_out=hcol[:]), inc=sS)
        r.wait("vector", sS, r.count(sS))
        r.emit("vector", lambda e: e.tensor_scalar(
            out=hcol[:], in0=hcol[:], scalar1=1.0 / D_INNER, scalar2=1e-5,
            op0=ALU.mult, op1=ALU.add), inc=sV)
        r.wait("scalar", sV, r.count(sV))
        r.emit("scalar", lambda e: e.activation(out=hcol[:], in_=hcol[:],
                                                func=AF.Ln), inc=sS)
        r.wait("scalar", sS, r.count(sS))
        r.emit("scalar", lambda e: e.activation(out=hcol[:], in_=hcol[:],
                                                func=AF.Exp, scale=-0.5), inc=sS)
        r.wait("vector", sS, r.count(sS))
        r.emit("vector", lambda e: e.scalar_tensor_tensor(
            out=scr[:], in0=scr[:], scalar=hcol[:], in1=GTt[:], op0=ALU.mult,
            op1=ALU.mult))
        r.emit("vector", lambda e: e.tensor_add(out=scr[:], in0=scr[:],
                                                in1=BTt[:]))
        r.emit("vector", lambda e: e.tensor_mul(out=evb[:], in0=xcTb2[:],
                                                in1=DTt[:]))
        r.emit("vector", lambda e: e.tensor_add(out=scr[:], in0=scr[:],
                                                in1=evb[:]))
        r.emit("vector", lambda e: e.tensor_mul(out=wpairb[:], in0=scr[:],
                                                in1=wbufD[:]), inc=sV)
        if "wD" in cfg.taps:
            r.wait("gpsimd", sV, r.count(sV))
            r.dma("gpsimd", wD[ts(p, 128), :], wpairb[:], sG)
            r.dma("gpsimd", zD[ts(p, 128), :], wbufD[:], sG)
        r.barrier_from("tensor", sV)
        for half in range(2):
            for j in range(8):
                r.emit("tensor", lambda e, half=half, j=j: e.transpose(
                    out=ptr[:, ts(j, 128)], in_=wpairb[:, ts(half * 8 + j, 128)],
                    identity=identb[:]), inc=(sT if j == 7 else None))
            r.wait("vector", sT, r.count(sT))
            for j in range(8):
                r.emit("vector", lambda e, half=half, j=j, p=p: e.tensor_copy(
                    out=wfm[half * 8 + j][:, ts(p % 4, 128)],
                    in_=ptr[:, ts(j, 128)]), inc=(sV if j == 7 else None))
            r.barrier_from("tensor", sV)
        if p % 4 == 3 or p == npair - 1:
            g = p // 4
            NW = 128 * ((p % 4) + 1)
            for wave, dmMs in ((0, range(0, 6)), (1, range(6, 8))):
                r.barrier_from("sync", sG)
                r.barrier_from("sync", sT)
                r.barrier_from("tensor", sV)
                wts = []
                for k in range(16):
                    r.barrier_from("sync", sT)
                    if wave == 0:
                        r.dma("sync", gbufD[k % 2][:, 0:768],
                              WoutbD[ts(k, 128), 0:768], sL)
                    else:
                        r.dma("sync", gbufD[k % 2][:, 0:256],
                              WoutbD[ts(k, 128), 768:1024], sL)
                    r.wait("tensor", sL, r.count(sL))
                    for dmM in dmMs:
                        def mm_out(e, k=k, dmM=dmM, NW=NW, wave=wave):
                            if dmM < 4:
                                dst = pwide[:, ts(dmM, 512)][:, 0:NW]
                            else:
                                dst = pwide2[:, ts(dmM % 2 if wave else dmM - 4,
                                                   512)][:, 0:NW]
                            off = dmM * 128 - (768 if wave else 0)
                            return e.matmul(out=dst,
                                            lhsT=gbufD[k % 2][:, off:off + 128],
                                            rhs=wfm[k][:, 0:NW],
                                            start=(k == 0), stop=(k == 15))
                        r.emit("tensor", mm_out,
                               inc=(sT if dmM == dmMs[-1] else None))
                    wts.append(r.count(sT))
                r.wait("vector", sT, r.count(sT))
                for dmM in dmMs:
                    def evac_res(e, dmM=dmM, g=g, NW=NW, wave=wave):
                        if dmM < 4:
                            src = pwide[:, ts(dmM, 512)][:, 0:NW]
                        else:
                            src = pwide2[:, ts(dmM % 2 if wave else dmM - 4,
                                               512)][:, 0:NW]
                        dst = S[dmM][:, HALO_B + g * 512:HALO_B + g * 512 + NW]
                        return e.tensor_add(out=dst, in0=src, in1=dst)
                    r.emit("vector", evac_res,
                           inc=(sV if dmM == dmMs[-1] else None))
                r.barrier_from("tensor", sV)
    phase_barrier(b)
    sd_.close()

    se_ = contextlib.ExitStack()
    b.ln_scope = se_
    b.scopes.append(se_)

    def seb(name, shape, dtype=F32):
        t = se_.enter_context(b.nc.sbuf_tensor(name, list(shape), dtype))
        b.sb[name] = t
        return t

    lnrb = [seb(f"lnrb{i}", [128, 512], BF16) for i in range(2)]
    lnr2b = [seb(f"lnr2b{i}", [128, 512], BF16) for i in range(2)]
    Rt = seb("Rt", [128, 2048], F32)
    Mt = seb("Mt", [128, 2048], F32)
    lnscr = seb("lnscr", [128, 2048], F32)
    rowA = seb("rowA", [1, 2048], F32)
    rowB = seb("rowB", [1, 2048], F32)
    b.ln_bufs = (lnrb, lnr2b)

    ln_fm(b)
    r.wait("gpsimd", sV, r.count(sV))
    for i in range(8):
        r.dma("gpsimd", x1saveD[ts(i, 128), :], S[i][:, HALO_B:], sG)
    if getattr(b.cfg, "partb", False):
        phase_barrier(b)
        se_.close()
        ms.close()
        b.scopes = []
    return dict(x1saveD=x1saveD)


def ln_fm(b, extra_res=None):
    cfg = b.cfg
    Tc = cfg.Tc
    TBW = min(512, Tc)
    ntb = Tc // TBW
    r = b.r
    S = b.S
    sL, sG, sT, sV, sS = (b.sem["sL"], b.sem["sG"], b.sem["sT"], b.sem["sV"],
                          b.sem["sS"])
    pwide = b.ps["pwide"]
    pwide2 = b.ps["pwide2"]
    ones33 = b.sb["ones33"]
    ones_colb = b.sb["ones_colb"]
    ones_rowf = b.sb["ones_rowf"]
    rowA, rowB = b.sb["rowA"], b.sb["rowB"]
    Rt, Mt = b.sb["Rt"], b.sb["Mt"]
    rb, r2b = b.ln_bufs
    D = 1024

    if extra_res is not None:
        scr = b.sb["lnscr"]
        for i in range(8):
            r.barrier_from("sync", sV)
            r.dma("sync", scr[:, 0:Tc], extra_res[ts(i, 128), :], sL)
            r.wait("vector", sL, r.count(sL))
            r.emit("vector", lambda e, i=i, scr=scr: e.tensor_add(
                out=S[i][:, HALO_B:], in0=S[i][:, HALO_B:], in1=scr[:, 0:Tc]),
                inc=sV)
        r.barrier_from("scalar", sV)
    for tb in range(ntb):
        for k in range(8):
            r.barrier_from("vector", sT)
            r.barrier_from("scalar", sT)
            r.emit("vector", lambda e, tb=tb, k=k: e.tensor_copy(
                out=rb[k % 2][:, 0:TBW], in_=S[k][:, HALO_B + tb * TBW:
                                                  HALO_B + tb * TBW + TBW]),
                inc=sV)
            r.emit("scalar", lambda e, tb=tb, k=k: e.activation(
                out=r2b[k % 2][:, 0:TBW], in_=S[k][:, HALO_B + tb * TBW:
                                                   HALO_B + tb * TBW + TBW],
                func=AF.Square), inc=sS)
            r.wait("tensor", sV, r.count(sV))
            r.wait("tensor", sS, r.count(sS))
            r.emit("tensor", lambda e, tb=tb, k=k: e.matmul(
                out=pwide[0:1, ts(tb, TBW)], lhsT=ones_colb[:],
                rhs=rb[k % 2][:, 0:TBW], start=(k == 0), stop=False),
                inc=sT)
            r.emit("tensor", lambda e, tb=tb, k=k: e.matmul(
                out=pwide[0:33, ts(tb, TBW)], lhsT=ones33[:],
                rhs=r2b[k % 2][:, 0:TBW], start=False, stop=(k == 7)),
                inc=sT)
    r.wait("scalar", sT, r.count(sT))
    for tb in range(ntb):
        r.emit("scalar", lambda e, tb=tb: e.mul(
            out=rowA[:, ts(tb, TBW)], in_=pwide[0:1, ts(tb, TBW)],
            mul=1.0 / D))
        r.emit("scalar", lambda e, tb=tb: e.mul(
            out=rowB[:, ts(tb, TBW)], in_=pwide[32:33, ts(tb, TBW)],
            mul=1.0 / D), inc=(sS if tb == ntb - 1 else None))
    r.wait("vector", sS, r.count(sS))
    tmp = b.sb["Rt"]
    r.emit("vector", lambda e: e.tensor_mul(out=tmp[0:1, 0:Tc],
                                            in0=rowA[:, 0:Tc],
                                            in1=rowA[:, 0:Tc]), inc=sV)
    r.wait("vector", sV, r.count(sV))
    r.emit("vector", lambda e: e.tensor_sub(out=rowB[:, 0:Tc],
                                            in0=rowB[:, 0:Tc],
                                            in1=tmp[0:1, 0:Tc]), inc=sV)
    r.wait("scalar", sV, r.count(sV))
    r.emit("scalar", lambda e: e.activation(out=rowB[:, 0:Tc],
                                            in_=rowB[:, 0:Tc],
                                            func=AF.Ln, bias=1e-5), inc=sS)
    r.wait("scalar", sS, r.count(sS))
    r.emit("scalar", lambda e: e.activation(out=rowB[:, 0:Tc],
                                            in_=rowB[:, 0:Tc],
                                            func=AF.Exp, scale=-0.5), inc=sS)
    r.wait("vector", sS, r.count(sS))
    r.emit("vector", lambda e: e.scalar_tensor_tensor(
        out=rowA[:, 0:Tc], in0=rowA[:, 0:Tc], scalar=-1.0, in1=rowB[:, 0:Tc],
        op0=ALU.mult, op1=ALU.mult), inc=sV)
    r.barrier_from("tensor", sV)
    for tb in range(ntb):
        r.emit("tensor", lambda e, tb=tb: e.matmul(
            out=pwide2[0:128, 0:TBW], lhsT=ones_rowf[:],
            rhs=rowB[:, ts(tb, TBW)], start=True, stop=True))
        r.emit("tensor", lambda e, tb=tb: e.matmul(
            out=pwide2[0:128, 512:512 + TBW], lhsT=ones_rowf[:],
            rhs=rowA[:, ts(tb, TBW)], start=True, stop=True), inc=sT)
        r.wait("vector", sT, r.count(sT))
        r.emit("vector", lambda e, tb=tb: e.tensor_copy(
            out=Rt[:, ts(tb, TBW)], in_=pwide2[0:128, 0:TBW]))
        r.emit("vector", lambda e, tb=tb: e.tensor_copy(
            out=Mt[:, ts(tb, TBW)], in_=pwide2[0:128, 512:512 + TBW]), inc=sV)
        r.barrier_from("tensor", sV)
    for i in range(8):
        r.emit("vector", lambda e, i=i: e.tensor_mul(
            out=S[i][:, HALO_B:], in0=S[i][:, HALO_B:], in1=Rt[:, 0:Tc]))
        r.emit("vector", lambda e, i=i: e.tensor_add(
            out=S[i][:, HALO_B:], in0=S[i][:, HALO_B:], in1=Mt[:, 0:Tc]),
            inc=(sV if i == 7 else None))
    r.barrier_from("tensor", sV)

"""Part B: 6 layers of (shifted local attention + shifted GEGLU FFN) on the
FM stream S, plus final residual + layernorm. See mirror.part_b_layer.

Per-layer structure (per core, Tc tokens, FM stream S fp32):
  attn: halo_gather(257); per 512-token q-block: shifted-x cast, qkv matmuls
        (bf16 weights streamed wave-major), q/k prep (l2norm*scale + rotary
        with xpos), v -> token-major v^ tiles with a ones column (softmax
        denominator), scores per (head, itile) with additive -30000 masks,
        exp(s - max), transposed-p AV matmuls, Wo + residual into S.
  ffn:  halo_gather(1); per 1024-token block: LN stats (ones-matmuls), xn
        (g,b folded), W1 (fM-major stream) + exact Gelu + a*gelu(g), W2
        (wave-major stream) + b2 + residual into S.
Weights are cast fp32->bf16 on device by a feeder interleaved with compute.
"""

QB = 512
NIT = 4
FH = 2816
NFM = 44
BIG = 30000.0



def host_inputs_partb(inp, cfg, core):
    import ml_dtypes
    bf = ml_dtypes.bfloat16
    Tc = cfg.Tc
    d = {}
    for i in range(cfg.nb_layers):
        Wqkv = np.asarray(inp["L0_ffn_Wqkv"])[i]
        d[f"WqkvS{i}"] = np.ascontiguousarray(
            Wqkv.reshape(8, 128, 12, 256).transpose(2, 0, 1, 3)
            .reshape(12288, 256))
        d[f"WoS{i}"] = np.ascontiguousarray(np.asarray(inp["L0_ffn_Wo"])[i])
        W1 = np.asarray(inp["L0_ffn_W1"])[i]
        pad = np.zeros((1024, FH - 2730), np.float32)
        a = np.concatenate([W1[:, :2730], pad], 1).reshape(1024, 22, 128)
        g = np.concatenate([W1[:, 2730:], pad], 1).reshape(1024, 22, 128)
        W1p = np.stack([a, g], 2).reshape(1024, NFM * 128)
        d[f"W1S{i}"] = np.ascontiguousarray(
            W1p.reshape(8, 128, NFM, 128).transpose(2, 0, 1, 3)
            .reshape(NFM * 1024, 128))
        W2 = np.asarray(inp["L0_ffn_W2"])[i]
        W2p = np.concatenate([W2, np.zeros((FH - 2730, 1024), np.float32)], 0)
        d[f"W2S{i}"] = np.ascontiguousarray(
            W2p.reshape(22, 128, 4, 256).transpose(2, 0, 1, 3)
            .reshape(4 * 2816, 256))
        b1 = np.asarray(inp["L0_ffn_b1"])[i]
        b1a = np.concatenate([b1[:2730], np.zeros(FH - 2730, np.float32)])
        b1g = np.concatenate([b1[2730:], np.zeros(FH - 2730, np.float32)])
        b1p = np.stack([b1a.reshape(22, 128), b1g.reshape(22, 128)], 1)
        d[f"b1p{i}"] = np.ascontiguousarray(b1p.reshape(NFM, 128).T.copy())
        d[f"gcol8{i}"] = np.ascontiguousarray(
            np.asarray(inp["L0_ffn_ln_g"])[i].reshape(8, 128).T.copy())
        d[f"lnbcol8{i}"] = np.ascontiguousarray(
            np.asarray(inp["L0_ffn_ln_b"])[i].reshape(8, 128).T.copy())
        d[f"b2col8{i}"] = np.ascontiguousarray(
            np.asarray(inp["L0_ffn_b2"])[i].reshape(8, 128).T.copy())
        d[f"qscol{i}"] = np.tile(np.asarray(inp["L0_ffn_q_scale"])[i],
                                 2).reshape(128, 1).astype(np.float32)
        d[f"kscol{i}"] = np.tile(np.asarray(inp["L0_ffn_k_scale"])[i],
                                 2).reshape(128, 1).astype(np.float32)
    T = Tc * cfg.n_cores
    half = np.arange(0, 64, 2, dtype=np.float64)
    inv_freq = 1.0 / 10000.0 ** (half / 64.0)
    t = np.arange(T, dtype=np.float64)
    freqs = np.concatenate([t[:, None] * inv_freq[None, :]] * 2, 1)
    base = (half + 0.4 * 64.0) / (1.4 * 64.0)
    power = (t - T // 2) / 128.0
    scale = np.concatenate([base[None, :] ** power[:, None]] * 2, 1)
    cos, sin = np.cos(freqs), np.sin(freqs)
    s0 = core * Tc
    qsl = slice(s0, s0 + Tc)
    kidx = np.clip(np.arange(s0 - 256, s0 + Tc), 0, T - 1)

    def two(x):
        return np.ascontiguousarray(np.tile(x.T, (2, 1))).astype(bf)

    d["csq8"] = two(cos[qsl] * scale[qsl] * 8.0)
    d["ssq8"] = two(sin[qsl] * scale[qsl] * 8.0)
    d["csk"] = two(cos[kidx] / scale[kidx])
    d["ssk"] = two(sin[kidx] / scale[kidx])
    rot = np.zeros((128, 128), np.float32)
    for o in (0, 64):
        for j in range(32):
            rot[o + j + 32, o + j] = -1.0
            rot[o + j, o + j + 32] = 1.0
    d["ProtB"] = rot.astype(bf)
    hs = np.zeros((128, 2), np.float32)
    hs[0:64, 0] = 1.0
    hs[64:128, 1] = 1.0
    d["headsel2"] = hs.astype(bf)
    rn = np.zeros((2, 128), np.float32)
    rn[0, 0:64] = 1.0
    rn[1, 64:128] = 1.0
    d["rnsel"] = rn.astype(bf)
    w = np.arange(128)
    u = np.arange(128)
    tri0 = np.where(u[None, :] >= w[:, None], 0.0, -BIG).astype(np.float32)
    tri2 = np.where(u[None, :] <= w[:, None], 0.0, -BIG).astype(np.float32)
    zed = np.zeros((128, 128), np.float32)
    neg = np.full((128, 128), -BIG, np.float32)
    cls2 = np.concatenate([tri0, zed, tri2], 1)
    if core == 0:
        cls0 = np.concatenate([neg, neg, tri2], 1)
        cls1 = np.concatenate([neg, zed, tri2], 1)
    else:
        cls0 = cls2
        cls1 = cls2
    d["mbias"] = np.ascontiguousarray(np.concatenate([cls0, cls1, cls2], 1))
    rsel = np.zeros((128, 8), np.float32)
    if core > 0:
        rsel[:, core - 1] = 1.0
    d["rankselB"] = rsel
    return d



class CastFeeder:
    def __init__(self, b, wstr, wbb):
        self.b = b
        self.wstr = wstr
        self.wbb = wbb
        self.chunks = []
        self.milestones = {}
        self.fed = 0

    def add(self, srcD, dstD, nrows, ncols, chrows, key=None):
        a = chrows // 128
        for c0 in range(0, nrows, chrows):
            if a == 1:
                s = srcD[c0:c0 + 128, :]
                dd = dstD[c0:c0 + 128, :]
            else:
                s = srcD[c0:c0 + chrows, :].rearrange("(a p) c -> p a c", p=128)
                dd = dstD[c0:c0 + chrows, :].rearrange("(a p) c -> p a c", p=128)
            self.chunks.append((s, dd, a * ncols, a))
        if key:
            self.milestones[key] = len(self.chunks)

    def feed(self, n, cap_key=None):
        r = self.b.r
        cap = self.milestones[cap_key] if cap_key else len(self.chunks)
        for _ in range(n):
            if self.fed >= cap:
                return
            s, dd, w, a = self.chunks[self.fed]
            self.fed += 1
            r.barrier_from("gpsimd", "sWv")
            if a == 1:
                r.dma("gpsimd", self.wstr[:, 0:w], s, "sWl")
            else:
                r.dma("gpsimd",
                      self.wstr[:, 0:w].rearrange("p (a c) -> p a c", a=a),
                      s, "sWl")
            r.barrier_from("vector", "sWl")
            r.barrier_from("vector", "sW")
            r.emit("vector", lambda e, w=w: e.tensor_copy(
                out=self.wbb[:, 0:w], in_=self.wstr[:, 0:w]),
                inc=r.sem("sWv"))
            r.barrier_from("gpsimd", "sWv")
            if a == 1:
                r.dma("gpsimd", dd, self.wbb[:, 0:w], "sW")
            else:
                r.dma("gpsimd", dd,
                      self.wbb[:, 0:w].rearrange("p (a c) -> p a c", a=a),
                      "sW")

    def feed_until(self, key):
        need = self.milestones[key] - self.fed
        if need > 0:
            self.feed(need)

    def wait_use(self, eng, key):
        self.feed_until(key)
        self.b.r.wait(eng, "sW", 16 * self.milestones[key])



def build_partb(b):
    cfg = b.cfg
    Tc = cfg.Tc
    NCORE = cfg.n_cores
    NL = cfg.nb_layers
    nqb = Tc // QB
    TBB = min(1024, Tc)
    ntb = Tc // TBB
    r = b.r
    S = b.S
    sL, sG, sT, sV, sS, sC = (b.sem["sL"], b.sem["sG"], b.sem["sT"],
                              b.sem["sV"], b.sem["sS"], b.sem["sC"])
    pwide, pwide2 = b.ps["pwide"], b.ps["pwide2"]
    ptr, ptrf = b.ps["ptr"], b.ps["ptrf"]
    identb = b.sb["identb"]
    ones_colb = b.sb["ones_colb"]
    ones33 = b.sb["ones33"]
    ones_rowf = b.sb["ones_rowf"]

    phase_barrier(b)

    ins = {}
    for i in range(NL):
        for nm, shp in ((f"WqkvS{i}", [12288, 256]), (f"WoS{i}", [1024, 1024]),
                        (f"W1S{i}", [NFM * 1024, 128]),
                        (f"W2S{i}", [4 * 2816, 256]),
                        (f"b1p{i}", [128, NFM]), (f"gcol8{i}", [128, 8]),
                        (f"lnbcol8{i}", [128, 8]), (f"b2col8{i}", [128, 8]),
                        (f"qscol{i}", [128, 1]), (f"kscol{i}", [128, 1])):
            ins[nm] = b.dram_in(nm, shp, F32)
    for nm, shp in (("csq8", [128, Tc]), ("ssq8", [128, Tc]),
                    ("csk", [128, Tc + 256]), ("ssk", [128, Tc + 256]),
                    ("ProtB", [128, 128]), ("headsel2", [128, 2]),
                    ("rnsel", [2, 128])):
        ins[nm] = b.dram_in(nm, shp, BF16)
    ins["mbias"] = b.dram_in("mbias", [128, 3 * 384], F32)
    ins["rankselB"] = b.dram_in("rankselB", [128, 8], F32)
    haloI = b.dram_tmp("haloI", [1024, HALO_B])
    haloO = b.dram_tmp("haloO", [NCORE * 1024, HALO_B])
    haloI8 = b.dram_tmp("haloI8", [1024, 8])
    haloO8 = b.dram_tmp("haloO8", [NCORE * 1024, 8])
    WqkvbD = b.dram_tmp("WqkvbD", [12288, 256], BF16)
    WobD = b.dram_tmp("WobD", [1024, 1024], BF16)
    W1bD = b.dram_tmp("W1bD", [NFM * 1024, 128], BF16)
    W2bD = b.dram_tmp("W2bD", [4 * 2816, 256], BF16)
    outD = b.dram_out("outD", [D_MODEL, Tc])
    taps = {}
    for li in range(NL):
        for ph in ("a", "f"):
            nm = f"pb{ph}{li}"
            if nm in cfg.taps:
                taps[nm] = b.dram_out("tap_" + nm, [D_MODEL, Tc])

    pbc = contextlib.ExitStack()
    b.scopes.append(pbc)

    def pcb(name, shape, dtype=F32):
        t = pbc.enter_context(b.nc.sbuf_tensor(name, list(shape), dtype))
        b.sb[name] = t
        return t

    wstr = pcb("wstrB", [128, 2048], F32)
    wbb = pcb("wbbB", [128, 2048], BF16)
    mbias = pcb("mbiasT", [128, 3 * 384], F32)
    protB = pcb("protBT", [128, 128], BF16)
    headsel2 = pcb("headsel2T", [128, 2], BF16)
    rnsel = pcb("rnselT", [2, 128], BF16)
    ranksel = pcb("rankselT", [128, 8], F32)
    halot = pcb("halotB", [128, 8 * HALO_B], F32)
    qscol = pcb("qscolT", [128, 1], F32)
    kscol = pcb("kscolT", [128, 1], F32)
    biasf = pcb("biasfT", [128, NFM], F32)
    gcol8 = pcb("gcol8T", [128, 8], F32)
    lnbcol8 = pcb("lnbcol8T", [128, 8], F32)
    b2col8 = pcb("b2col8T", [128, 8], F32)
    xk1 = pcb("xk1T", [128, 8], BF16)

    for nm, dst in (("mbias", mbias), ("ProtB", protB),
                    ("headsel2", headsel2), ("rnsel", rnsel),
                    ("rankselB", ranksel)):
        r.dma("sync", dst[:], ins[nm], sL)
    r.barrier_from("vector", sL)

    feeder = CastFeeder(b, wstr, wbb)
    for i in range(NL):
        feeder.add(ins[f"WqkvS{i}"], WqkvbD, 12288, 256, 1024)
        feeder.add(ins[f"WoS{i}"], WobD, 1024, 1024, 256, key=f"attn{i}")
        feeder.add(ins[f"W1S{i}"], W1bD, NFM * 1024, 128, 2048)
        feeder.add(ins[f"W2S{i}"], W2bD, 4 * 2816, 256, 1024, key=f"ffn{i}")

    def halo_gather(width):
        hi, ho = (haloI, haloO) if width == HALO_B else (haloI8, haloO8)
        r.barrier_from("gpsimd", sV)
        for k in range(8):
            r.dma("gpsimd", hi[ts(k, 128), 0:width],
                  S[k][:, HALO_B + Tc - width:HALO_B + Tc], sG)
        if NCORE > 1:
            r.emit("gpsimd", lambda e, hi=hi, ho=ho: e.collective_compute(
                "AllGather", ALU.bypass, replica_groups=[list(range(NCORE))],
                ins=[hi.opt()], outs=[ho.opt()]), inc=sC, inc_n=1)
            r.wait("sync", sC, r.count(sC))
            hv = ho.rearrange("(c q) w -> q c w", q=1024)
            for k in range(8):
                r.barrier_from("sync", sV)
                r.dma("sync",
                      halot[:, 0:NCORE * width].rearrange(
                          "p (c w) -> p c w", c=NCORE),
                      hv[ts(k, 128), :, 0:width], sL)
                r.barrier_from("vector", sL)
                r.emit("vector", lambda e, k=k, width=width: e.tensor_scalar(
                    out=S[k][:, HALO_B - width:HALO_B],
                    in0=halot[:, 0:width], scalar1=ranksel[:, 0:1],
                    scalar2=None, op0=ALU.mult))
                for j in range(1, NCORE):
                    r.emit("vector", lambda e, k=k, j=j, width=width:
                           e.scalar_tensor_tensor(
                               out=S[k][:, HALO_B - width:HALO_B],
                               in0=halot[:, j * width:(j + 1) * width],
                               scalar=ranksel[:, j:j + 1],
                               in1=S[k][:, HALO_B - width:HALO_B],
                               op0=ALU.mult, op1=ALU.add),
                           inc=(sV if j == NCORE - 1 else None))
        else:
            for k in range(8):
                r.emit("vector", lambda e, k=k, width=width: e.memset(
                    S[k][:, HALO_B - width:HALO_B], 0.0),
                    inc=(sV if k == 7 else None))

    class _StopBuild(Exception):
        pass

    stop2 = getattr(cfg, "pb_stop2", None)

    def attn_layer(li):
        try:
            _attn_layer(li)
        except _StopBuild:
            raise

    def _attn_layer(li):
        def ck(tag):
            if li == 0 and stop2 == tag:
                raise _StopBuild()
        asc = contextlib.ExitStack()
        b.scopes.append(asc)

        def ab(name, shape, dtype=BF16):
            return asc.enter_context(
                b.nc.sbuf_tensor(name + str(li), list(shape), dtype))

        csq8 = ab("csq8", [128, Tc])
        ssq8 = ab("ssq8", [128, Tc])
        csk = ab("csk", [128, Tc + 256])
        ssk = ab("ssk", [128, Tc + 256])
        xkeep = [ab(f"xkeep{k}", [128, 257]) for k in range(8)]
        qhat = [ab(f"qhat{k}", [128, QB]) for k in range(8)]
        khat = [ab(f"khat{k}", [128, QB + 256]) for k in range(8)]
        vhat = [ab(f"vhat{j}", [128, 16 * 65]) for j in range(NIT + 2)]
        qrawb = ab("qraw", [128, 768])
        q2b = ab("q2b", [128, 768])
        qnb = ab("qnb", [128, 512])
        rnrowb = ab("rnrow", [2, 512])
        tq1 = ab("tq1", [128, 512], F32)
        vbring = [ab(f"vbring{i}", [128, 128]) for i in range(6)]
        nmcol = [ab(f"nmcol{i}", [128, 1], F32) for i in range(2)]
        invz = [ab(f"invz{i}", [128, 1], F32) for i in range(2)]

        r.barrier_from("gpsimd", sL)
        feeder.wait_use("sync", f"attn{li}")
        r.dma("sync", csq8[:], ins["csq8"], sL)
        r.dma("sync", ssq8[:], ins["ssq8"], sL)
        r.dma("sync", csk[:], ins["csk"], sL)
        r.dma("sync", ssk[:], ins["ssk"], sL)
        r.dma("sync", qscol[:], ins[f"qscol{li}"], sL)
        r.dma("sync", kscol[:], ins[f"kscol{li}"], sL)
        halo_gather(HALO_B)
        r.barrier_from("vector", sL)
        ck("t")
        wqv = WqkvbD
        wov = WobD

        for qb in range(nqb):
            qb0 = qb * QB
            sub = contextlib.ExitStack()
            b.scopes.append(sub)

            def sb2(name, shape, dtype=BF16):
                return sub.enter_context(
                    b.nc.sbuf_tensor(f"{name}{li}_{qb}", list(shape), dtype))

            xsb = [sb2(f"xsb{k}", [128, 768]) for k in range(8)]
            wqb = [sb2(f"wqb{i}", [128, 2048]) for i in range(2)]

            r.barrier_from("vector", sT)
            r.barrier_from("vector", sS)
            for k in range(8):
                dlt = 0 if k < 4 else -1
                if qb == 0:
                    r.emit("vector", lambda e, k=k, dlt=dlt, qb0=qb0:
                           e.tensor_copy(
                               out=xsb[k][:],
                               in_=S[k][:, HALO_B + qb0 - 256 + dlt:
                                        HALO_B + qb0 + 512 + dlt]),
                           inc=(sV if k == 7 else None))
                else:
                    if k < 4:
                        r.emit("vector", lambda e, k=k: e.tensor_copy(
                            out=xsb[k][:, 0:256], in_=xkeep[k][:, 1:257]))
                    else:
                        r.emit("vector", lambda e, k=k: e.tensor_copy(
                            out=xsb[k][:, 0:257], in_=xkeep[k][:, 0:257]))
                    c0 = 256 if k < 4 else 257
                    r.emit("vector", lambda e, k=k, c0=c0, qb0=qb0:
                           e.tensor_copy(
                               out=xsb[k][:, c0:768],
                               in_=S[k][:, HALO_B + qb0:
                                        HALO_B + qb0 + 768 - c0]),
                           inc=(sV if k == 7 else None))
            for j in range(NIT + 2):
                r.emit("vector", lambda e, j=j: e.memset(vhat[j][:], 1.0),
                       inc=(sV if j == NIT + 1 else None))
            r.wait("tensor", sV, r.count(sV))
            ck("x")

            wavemark = []
            evacmark = [0, 0]
            for w in range(12):
                wb = w % 2
                if w >= 2:
                    r.wait("sync", sT, wavemark[w - 2])
                r.dma("sync", wqb[wb][:].rearrange("p (k c) -> p k c", k=8),
                      wqv[w * 1024:(w + 1) * 1024, :].rearrange(
                          "(k p) c -> p k c", p=128), sL)
                r.wait("tensor", sL, r.count(sL))
                if w >= 1:
                    r.wait("tensor", sV, evacmark[0])
                    r.wait("tensor", sS, evacmark[1])
                for k in range(8):
                    for oin in range(2):
                        for (c0, cw) in ((0, 512), (512, 256)):
                            r.emit("tensor",
                                   lambda e, k=k, oin=oin, c0=c0, cw=cw, wb=wb:
                                   e.matmul(
                                       out=pwide[:, oin * 1024 + c0:
                                                 oin * 1024 + c0 + cw],
                                       lhsT=wqb[wb][:, k * 256 + oin * 128:
                                                    k * 256 + oin * 128 + 128],
                                       rhs=xsb[k][:, c0:c0 + cw],
                                       start=(k == 0), stop=(k == 7)),
                                   inc=(sT if (k == 7 and oin == 1 and
                                               c0 == 512) else None))
                wavemark.append(r.count(sT))
                ck(f"m{w}")
                for oin in range(2):
                    ot = 2 * w + oin
                    psl = pwide[:, oin * 1024:oin * 1024 + 768]
                    if ot < 16:
                        isq = ot < 8
                        lo = 256 if isq else 0
                        W = 768 - lo
                        r.wait("vector", sT, wavemark[-1])
                        r.emit("vector", lambda e, psl=psl, lo=lo, W=W:
                               e.tensor_copy(out=qrawb[:, 0:W],
                                             in_=psl[:, lo:lo + W]), inc=sV)
                        evacmark[0] = r.count(sV)
                        ck(f"ev{ot}")
                        r.wait("scalar", sV, evacmark[0])
                        r.emit("scalar", lambda e, W=W:
                               e.activation(out=q2b[:, 0:W],
                                            in_=qrawb[:, 0:W],
                                            func=AF.Square), inc=sS)
                        evacmark[1] = r.count(sS)
                        ck(f"e{ot}")
                        scol = qscol if isq else kscol
                        cstab = csq8 if isq else csk
                        sstab = ssq8 if isq else ssk
                        dest = qhat[ot] if isq else khat[ot - 8]
                        for c in range(0, W, 512):
                            cw = min(512, W - c)
                            cc = qb0 + c
                            r.wait("tensor", sS, r.count(sS))
                            r.wait("tensor", sV, r.count(sV))
                            r.emit("tensor", lambda e, c=c, cw=cw: e.matmul(
                                out=ptrf[0:2, 0:cw], lhsT=headsel2[:],
                                rhs=q2b[:, c:c + cw], start=True, stop=True),
                                inc=sT)
                            r.wait("vector", sT, r.count(sT))
                            r.emit("vector", lambda e, cw=cw: e.tensor_copy(
                                out=tq1[0:2, 0:cw], in_=ptrf[0:2, 0:cw]),
                                inc=sV)
                            r.wait("scalar", sV, r.count(sV))
                            r.emit("scalar", lambda e, cw=cw: e.activation(
                                out=tq1[64:66, 0:cw], in_=tq1[0:2, 0:cw],
                                func=AF.Ln, bias=1e-24), inc=sS)
                            r.wait("scalar", sS, r.count(sS))
                            r.emit("scalar", lambda e, cw=cw: e.activation(
                                out=rnrowb[0:2, 0:cw], in_=tq1[64:66, 0:cw],
                                func=AF.Exp, scale=-0.5), inc=sS)
                            r.wait("tensor", sS, r.count(sS))
                            r.emit("tensor", lambda e, cw=cw: e.matmul(
                                out=pwide2[:, 0:cw], lhsT=rnsel[:],
                                rhs=rnrowb[0:2, 0:cw], start=True, stop=True),
                                inc=sT)
                            r.wait("vector", sT, r.count(sT))
                            r.emit("vector", lambda e, c=c, cw=cw, scol=scol:
                                   e.scalar_tensor_tensor(
                                       out=qnb[:, 0:cw],
                                       in0=qrawb[:, c:c + cw],
                                       scalar=scol[:, 0:1],
                                       in1=pwide2[:, 0:cw],
                                       op0=ALU.mult, op1=ALU.mult), inc=sV)
                            r.wait("tensor", sV, r.count(sV))
                            r.emit("tensor", lambda e, cw=cw: e.matmul(
                                out=pwide2[:, 512:512 + cw], lhsT=protB[:],
                                rhs=qnb[:, 0:cw], start=True, stop=True),
                                inc=sT)
                            r.emit("vector", lambda e, c=c, cw=cw, cc=cc,
                                   dest=dest, cstab=cstab: e.tensor_mul(
                                       out=dest[:, c:c + cw],
                                       in0=qnb[:, 0:cw],
                                       in1=cstab[:, cc:cc + cw]))
                            r.wait("vector", sT, r.count(sT))
                            r.emit("vector", lambda e, cw=cw, cc=cc,
                                   sstab=sstab: e.tensor_mul(
                                       out=tq1[:, 0:cw],
                                       in0=pwide2[:, 512:512 + cw],
                                       in1=sstab[:, cc:cc + cw]))
                            r.emit("vector", lambda e, c=c, cw=cw, dest=dest:
                                   e.tensor_add(out=dest[:, c:c + cw],
                                                in0=dest[:, c:c + cw],
                                                in1=tq1[:, 0:cw]), inc=sV)
                        ck(f"p{ot}")
                    else:
                        vo = ot - 16
                        r.wait("vector", sT, wavemark[-1])
                        r.wait("vector", sT, r.count(sT))
                        for jt in range(6):
                            r.emit("vector", lambda e, psl=psl, jt=jt:
                                   e.tensor_copy(
                                       out=vbring[jt][:],
                                       in_=psl[:, jt * 128:jt * 128 + 128]),
                                   inc=(sV if jt == 5 else None))
                        evacmark[0] = r.count(sV)
                        ck(f"v{ot}")
                        r.wait("tensor", sV, r.count(sV))
                        for jt in range(6):
                            r.emit("tensor", lambda e, jt=jt: e.transpose(
                                out=ptr[:, jt * 128:jt * 128 + 128],
                                in_=vbring[jt][:], identity=identb[:]),
                                inc=(sT if jt == 5 else None))
                        r.wait("vector", sT, r.count(sT))
                        for jt in range(6):
                            for hh in range(2):
                                h = 2 * vo + hh
                                r.emit("vector", lambda e, jt=jt, hh=hh, h=h:
                                       e.tensor_copy(
                                           out=vhat[jt][:, 65 * h:65 * h + 64],
                                           in_=ptr[:, jt * 128 + 64 * hh:
                                                   jt * 128 + 64 * hh + 64]),
                                       inc=(sV if (jt == 5 and hh == 1)
                                            else None))
            ck("w")
            feeder.feed(33 // (2 * nqb) + 2, cap_key=f"ffn{li}")
            r.barrier_from("tensor", sV)
            r.barrier_from("tensor", sS)

            aoTM = sb2("aoTM", [128, 1024])
            aofm = [sb2(f"aofm{k}", [128, QB]) for k in range(8)]
            snb = [sb2(f"snb{i}", [128, 384], F32) for i in range(2)]
            pb_ = [sb2(f"pb{i}", [128, 384]) for i in range(2)]
            pTb = [sb2(f"pTb{i}", [128, 384]) for i in range(2)]
            sslot = [ptrf[:, 0:384], pwide2[:, 512:896]]
            avslot = [pwide2[:, 0:65], pwide2[:, 128:193]]
            snm = [0, 0]
            ptm = [0, 0]
            aom = [0, 0]
            for it in range(NIT):
                cls = min(qb * NIT + it, 2)
                for h in range(16):
                    hp = h % 2
                    oq = h // 2
                    bq = (h % 2) * 64
                    i0 = it * 128
                    if snm[hp]:
                        r.wait("tensor", sV, snm[hp])
                    r.emit("tensor", lambda e, hp=hp, oq=oq, bq=bq, i0=i0:
                           e.matmul(out=sslot[hp],
                                    lhsT=qhat[oq][bq:bq + 64, i0:i0 + 128],
                                    rhs=khat[oq][bq:bq + 64, i0:i0 + 384],
                                    start=True, stop=True), inc=sT)
                    r.wait("vector", sT, r.count(sT))
                    r.emit("vector", lambda e, hp=hp, cls=cls: e.tensor_add(
                        out=snb[hp][:], in0=sslot[hp],
                        in1=mbias[:, cls * 384:(cls + 1) * 384]))
                    snm[hp] = r.count(sV) + 1
                    r.emit("vector", lambda e, hp=hp: e.reduce_max(
                        out=nmcol[hp][:], in_=snb[hp][:], axis=AX.X,
                        negate=True), inc=sV)
                    r.wait("scalar", sV, r.count(sV))
                    r.emit("scalar", lambda e, hp=hp: e.activation(
                        out=pb_[hp][:], in_=snb[hp][:], func=AF.Exp,
                        bias=nmcol[hp][:]), inc=sS)
                    r.wait("tensor", sS, r.count(sS))
                    if ptm[hp]:
                        r.wait("tensor", sV, ptm[hp])
                    for jsub in range(3):
                        r.emit("tensor", lambda e, hp=hp, jsub=jsub:
                               e.transpose(
                                   out=ptr[:, hp * 512 + jsub * 128:
                                           hp * 512 + jsub * 128 + 128],
                                   in_=pb_[hp][:, jsub * 128:jsub * 128 + 128],
                                   identity=identb[:]),
                               inc=(sT if jsub == 2 else None))
                    r.wait("vector", sT, r.count(sT))
                    r.emit("vector", lambda e, hp=hp: e.tensor_copy(
                        out=pTb[hp][:], in_=ptr[:, hp * 512:hp * 512 + 384]),
                        inc=sV)
                    ptm[hp] = r.count(sV)
                    r.wait("tensor", sV, ptm[hp])
                    if aom[hp]:
                        r.wait("tensor", sV, aom[hp])
                    for jsub in range(3):
                        r.emit("tensor", lambda e, hp=hp, jsub=jsub, it=it,
                               h=h: e.matmul(
                                   out=avslot[hp],
                                   lhsT=pTb[hp][:, jsub * 128:jsub * 128 + 128],
                                   rhs=vhat[it + jsub][:, 65 * h:65 * h + 65],
                                   start=(jsub == 0), stop=(jsub == 2)),
                               inc=(sT if jsub == 2 else None))
                    r.wait("vector", sT, r.count(sT))
                    r.emit("vector", lambda e, hp=hp: e.reciprocal(
                        out=invz[hp][:], in_=avslot[hp][:, 64:65]))
                    r.emit("vector", lambda e, hp=hp, h=h: e.tensor_scalar(
                        out=aoTM[:, 64 * h:64 * h + 64],
                        in0=avslot[hp][:, 0:64], scalar1=invz[hp][:, 0:1],
                        scalar2=None, op0=ALU.mult), inc=sV)
                    aom[hp] = r.count(sV)
                r.wait("tensor", sV, r.count(sV))
                for k in range(8):
                    r.emit("tensor", lambda e, k=k: e.transpose(
                        out=ptr[:, k * 128:k * 128 + 128],
                        in_=aoTM[:, k * 128:k * 128 + 128],
                        identity=identb[:]), inc=(sT if k == 7 else None))
                r.wait("vector", sT, r.count(sT))
                for k in range(8):
                    r.emit("vector", lambda e, k=k, it=it: e.tensor_copy(
                        out=aofm[k][:, it * 128:it * 128 + 128],
                        in_=ptr[:, k * 128:k * 128 + 128]),
                        inc=(sV if k == 7 else None))
                ptm = [r.count(sV), r.count(sV)]
                ck(f"s{it}")
            feeder.feed(33 // (2 * nqb) + 2, cap_key=f"ffn{li}")

            wos = sb2("wos", [128, 4096])
            r.barrier_from("tensor", sV)
            for w in range(2):
                r.barrier_from("sync", sT)
                r.dma("sync", wos[:].rearrange("p (k d) -> p k d", k=8),
                      wov[:, w * 512:(w + 1) * 512].rearrange(
                          "(k p) d -> p k d", p=128), sL)
                ck(f"wl{w}")
                r.wait("tensor", sL, r.count(sL))
                r.barrier_from("tensor", sV)
                for k in range(8):
                    for dm2 in range(4):
                        r.emit("tensor", lambda e, k=k, dm2=dm2: e.matmul(
                            out=pwide[:, dm2 * 512:dm2 * 512 + 512],
                            lhsT=wos[:, k * 512 + dm2 * 128:
                                     k * 512 + dm2 * 128 + 128],
                            rhs=aofm[k][:], start=(k == 0), stop=(k == 7)),
                            inc=(sT if (k == 7 and dm2 == 3) else None))
                ck(f"wm{w}")
                if w == 0:
                    for k in range(8):
                        r.emit("vector", lambda e, k=k, qb0=qb0: e.tensor_copy(
                            out=xkeep[k][:],
                            in_=S[k][:, HALO_B + qb0 + 255:
                                     HALO_B + qb0 + 512]))
                r.wait("vector", sT, r.count(sT))
                for dm2 in range(4):
                    dm = w * 4 + dm2
                    r.emit("vector", lambda e, dm=dm, dm2=dm2, qb0=qb0:
                           e.tensor_add(
                               out=S[dm][:, HALO_B + qb0:HALO_B + qb0 + 512],
                               in0=pwide[:, dm2 * 512:dm2 * 512 + 512],
                               in1=S[dm][:, HALO_B + qb0:HALO_B + qb0 + 512]),
                           inc=(sV if dm2 == 3 else None))
                ck(f"we{w}")
            r.barrier_from("tensor", sV)
            b.scopes.remove(sub)
            sub.close()
        nm = f"pba{li}"
        if nm in taps:
            r.barrier_from("gpsimd", sV)
            for k in range(8):
                r.dma("gpsimd", taps[nm][ts(k, 128), :], S[k][:, HALO_B:], sG)
        b.scopes.remove(asc)
        asc.close()

    def ffn_layer(li):
        fsc = contextlib.ExitStack()
        b.scopes.append(fsc)

        def fb(name, shape, dtype=BF16):
            return fsc.enter_context(
                b.nc.sbuf_tensor(name + str(li), list(shape), dtype))

        xnb = [fb(f"xnb{k}", [128, TBB]) for k in range(8)]
        ag = [fb(f"ag{j}", [128, TBB]) for j in range(22)]
        w1l = [fb(f"w1l{i}", [128, 1024]) for i in range(2)]
        w2w = fb("w2w", [128, 22 * 256])
        r2b = [fb(f"r2b{i}", [128, 512]) for i in range(2)]
        Rt = fb("RtF", [128, 512], F32)
        Mt = fb("MtF", [128, 512], F32)
        tmpf = fb("tmpfF", [128, 512], F32)
        ge = fb("geF", [128, TBB])
        gbf = fb("gbfF", [128, TBB], F32)
        rowA = halot[0:1, 0:TBB]
        rowB = halot[0:1, TBB:2 * TBB]

        r.barrier_from("gpsimd", sL)
        feeder.wait_use("sync", f"ffn{li}")
        r.dma("sync", biasf[:], ins[f"b1p{li}"], sL)
        r.dma("sync", gcol8[:], ins[f"gcol8{li}"], sL)
        r.dma("sync", lnbcol8[:], ins[f"lnbcol8{li}"], sL)
        r.dma("sync", b2col8[:], ins[f"b2col8{li}"], sL)
        halo_gather(8)
        r.barrier_from("vector", sL)

        for tb in range(ntb):
            tb0 = tb * TBB
            r.barrier_from("vector", sT)
            r.barrier_from("scalar", sT)
            for k in range(8):
                dlt = 0 if k < 4 else -1
                r.emit("vector", lambda e, k=k, dlt=dlt, tb0=tb0:
                       e.tensor_copy(out=xnb[k][:],
                                     in_=S[k][:, HALO_B + tb0 + dlt:
                                              HALO_B + tb0 + TBB + dlt]),
                       inc=(sV if (tb == 0 or k < 4) and k == 7 else None))
                if tb > 0 and k >= 4:
                    r.emit("vector", lambda e, k=k: e.tensor_copy(
                        out=xnb[k][:, 0:1], in_=xk1[:, k:k + 1]),
                        inc=(sV if k == 7 else None))
            nsub = TBB // 512
            for sb_ in range(nsub):
                smk = []
                for k in range(8):
                    r.wait("scalar", sV, r.count(sV))
                    if k >= 2:
                        r.wait("scalar", sT, smk[k - 2])
                    r.emit("scalar", lambda e, k=k, sb_=sb_: e.activation(
                        out=r2b[k % 2][:],
                        in_=xnb[k][:, sb_ * 512:sb_ * 512 + 512],
                        func=AF.Square), inc=sS)
                    r.wait("tensor", sS, r.count(sS))
                    r.wait("tensor", sV, r.count(sV))
                    r.emit("tensor", lambda e, k=k, sb_=sb_: e.matmul(
                        out=pwide[0:1, sb_ * 512:sb_ * 512 + 512],
                        lhsT=ones_colb[:],
                        rhs=xnb[k][:, sb_ * 512:sb_ * 512 + 512],
                        start=(k == 0), stop=False))
                    r.emit("tensor", lambda e, k=k, sb_=sb_: e.matmul(
                        out=pwide[0:33, sb_ * 512:sb_ * 512 + 512],
                        lhsT=ones33[:], rhs=r2b[k % 2][:],
                        start=False, stop=(k == 7)),
                        inc=sT)
                    smk.append(r.count(sT))
            r.wait("scalar", sT, r.count(sT))
            for sb_ in range(nsub):
                r.emit("scalar", lambda e, sb_=sb_: e.mul(
                    out=rowA[:, sb_ * 512:sb_ * 512 + 512],
                    in_=pwide[0:1, sb_ * 512:sb_ * 512 + 512],
                    mul=1.0 / 1024))
                r.emit("scalar", lambda e, sb_=sb_: e.mul(
                    out=rowB[:, sb_ * 512:sb_ * 512 + 512],
                    in_=pwide[32:33, sb_ * 512:sb_ * 512 + 512],
                    mul=1.0 / 1024), inc=(sS if sb_ == nsub - 1 else None))
            r.wait("vector", sS, r.count(sS))
            for sb_ in range(nsub):
                sl = slice(sb_ * 512, sb_ * 512 + 512)
                r.emit("vector", lambda e, sl=sl: e.tensor_mul(
                    out=Rt[0:1, 0:512], in0=rowA[:, sl], in1=rowA[:, sl]))
                r.emit("vector", lambda e, sl=sl: e.tensor_sub(
                    out=rowB[:, sl], in0=rowB[:, sl], in1=Rt[0:1, 0:512]),
                    inc=(sV if sb_ == nsub - 1 else None))
            r.wait("scalar", sV, r.count(sV))
            r.emit("scalar", lambda e: e.activation(out=rowB[:], in_=rowB[:],
                                                    func=AF.Ln, bias=1e-5),
                   inc=sS)
            r.wait("scalar", sS, r.count(sS))
            r.emit("scalar", lambda e: e.activation(out=rowB[:], in_=rowB[:],
                                                    func=AF.Exp, scale=-0.5),
                   inc=sS)
            r.wait("vector", sS, r.count(sS))
            r.emit("vector", lambda e: e.scalar_tensor_tensor(
                out=rowA[:], in0=rowA[:], scalar=-1.0, in1=rowB[:],
                op0=ALU.mult, op1=ALU.mult), inc=sV)
            for sb_ in range(nsub):
                r.wait("tensor", sV, r.count(sV))
                r.emit("tensor", lambda e, sb_=sb_: e.matmul(
                    out=pwide2[:, 0:512], lhsT=ones_rowf[:],
                    rhs=rowB[:, sb_ * 512:sb_ * 512 + 512],
                    start=True, stop=True))
                r.emit("tensor", lambda e, sb_=sb_: e.matmul(
                    out=pwide2[:, 512:1024], lhsT=ones_rowf[:],
                    rhs=rowA[:, sb_ * 512:sb_ * 512 + 512],
                    start=True, stop=True), inc=sT)
                r.wait("vector", sT, r.count(sT))
                r.emit("vector", lambda e: e.tensor_copy(
                    out=Rt[:], in_=pwide2[:, 0:512]))
                r.emit("vector", lambda e: e.tensor_copy(
                    out=Mt[:], in_=pwide2[:, 512:1024]), inc=sV)
                for k in range(8):
                    sl = slice(sb_ * 512, sb_ * 512 + 512)
                    r.emit("vector", lambda e, k=k, sl=sl: e.tensor_mul(
                        out=tmpf[:], in0=xnb[k][:, sl], in1=Rt[:]))
                    r.emit("vector", lambda e: e.tensor_add(
                        out=tmpf[:], in0=tmpf[:], in1=Mt[:]))
                    r.emit("vector", lambda e, k=k, sl=sl: e.tensor_scalar(
                        out=xnb[k][:, sl], in0=tmpf[:],
                        scalar1=gcol8[:, k:k + 1], scalar2=lnbcol8[:, k:k + 1],
                        op0=ALU.mult, op1=ALU.add),
                        inc=(sV if k == 7 else None))
            if li == 0 and tb == 0 and "xn0" in cfg.taps:
                xnD = b.dram_out("tap_xn0", [1024, TBB], BF16)
                r.wait("gpsimd", sV, r.count(sV))
                for k in range(8):
                    r.dma("gpsimd", xnD[ts(k, 128), :], xnb[k][:], sG)
            r.barrier_from("tensor", sV)

            w1m = []
            egm = [0, 0, 0]
            slots = [pwide[:, 0:1024], pwide[:, 1024:2048], pwide2[:, 0:1024]]
            for j in range(22):
                for hf in range(2):
                    fM = 2 * j + hf
                    lb = fM % 2
                    if fM >= 2:
                        r.wait("sync", sT, w1m[fM - 2])
                    r.dma("sync",
                          w1l[lb][:].rearrange("p (k c) -> p k c", k=8),
                          W1bD[fM * 1024:(fM + 1) * 1024, :].rearrange(
                              "(k p) c -> p k c", p=128), sL)
                    r.wait("tensor", sL, r.count(sL))
                    if egm[fM % 3]:
                        r.wait("tensor", sV, egm[fM % 3])
                    for k in range(8):
                        for c2 in range(TBB // 512):
                            r.emit("tensor", lambda e, k=k, c2=c2, lb=lb,
                                   fM=fM: e.matmul(
                                       out=slots[fM % 3][:, c2 * 512:
                                                         c2 * 512 + 512],
                                       lhsT=w1l[lb][:, k * 128:k * 128 + 128],
                                       rhs=xnb[k][:, c2 * 512:c2 * 512 + 512],
                                       start=(k == 0), stop=(k == 7)),
                                   inc=(sT if (k == 7 and
                                               c2 == TBB // 512 - 1)
                                        else None))
                    w1m.append(r.count(sT))
                r.wait("vector", sT, w1m[-1])
                r.emit("vector", lambda e, j=j: e.tensor_copy(
                    out=gbf[:], in_=slots[(2 * j + 1) % 3][:, 0:TBB]), inc=sV)
                egm[(2 * j + 1) % 3] = r.count(sV)
                r.wait("scalar", sV, r.count(sV))
                r.emit("scalar", lambda e, j=j: e.activation(
                    out=ge[:], in_=gbf[:],
                    func=AF.Gelu, bias=biasf[:, 2 * j + 1:2 * j + 2]), inc=sS)
                r.wait("vector", sS, r.count(sS))
                r.emit("vector", lambda e, j=j: e.scalar_tensor_tensor(
                    out=ag[j][:], in0=slots[(2 * j) % 3][:, 0:TBB],
                    scalar=biasf[:, 2 * j:2 * j + 1], in1=ge[:],
                    op0=ALU.add, op1=ALU.mult), inc=sV)
                egm[(2 * j) % 3] = r.count(sV)
            if li + 1 < NL:
                r.barrier_from("gpsimd", sL)
                feeder.feed(16 // ntb + 1, cap_key=f"attn{li + 1}")
            r.barrier_from("tensor", sV)
            r.barrier_from("tensor", sS)

            for w in range(4):
                r.barrier_from("sync", sT)
                r.dma("sync", w2w[:].rearrange("p (k c) -> p k c", k=22),
                      W2bD[w * 2816:(w + 1) * 2816, :].rearrange(
                          "(k p) c -> p k c", p=128), sL)
                r.wait("tensor", sL, r.count(sL))
                r.barrier_from("tensor", sV)
                for k in range(22):
                    for d2 in range(2):
                        for c2 in range(TBB // 512):
                            r.emit("tensor", lambda e, k=k, d2=d2, c2=c2:
                                   e.matmul(
                                       out=pwide[:, d2 * 1024 + c2 * 512:
                                                 d2 * 1024 + c2 * 512 + 512],
                                       lhsT=w2w[:, k * 256 + d2 * 128:
                                                k * 256 + d2 * 128 + 128],
                                       rhs=ag[k][:, c2 * 512:c2 * 512 + 512],
                                       start=(k == 0), stop=(k == 21)),
                                   inc=(sT if (k == 21 and d2 == 1 and
                                               c2 == TBB // 512 - 1)
                                        else None))
                if w == 0:
                    for k in range(8):
                        r.emit("vector", lambda e, k=k, tb0=tb0: e.tensor_copy(
                            out=xk1[:, k:k + 1],
                            in_=S[k][:, HALO_B + tb0 + TBB - 1:
                                     HALO_B + tb0 + TBB]))
                r.wait("vector", sT, r.count(sT))
                for d2 in range(2):
                    dm = w * 2 + d2
                    r.emit("vector", lambda e, dm=dm, d2=d2, tb0=tb0:
                           e.scalar_tensor_tensor(
                               out=S[dm][:, HALO_B + tb0:HALO_B + tb0 + TBB],
                               in0=pwide[:, d2 * 1024:d2 * 1024 + TBB],
                               scalar=b2col8[:, dm:dm + 1],
                               in1=S[dm][:, HALO_B + tb0:HALO_B + tb0 + TBB],
                               op0=ALU.add, op1=ALU.add),
                           inc=(sV if d2 == 1 else None))
            r.barrier_from("tensor", sV)
        nm = f"pbf{li}"
        if nm in taps:
            r.barrier_from("gpsimd", sV)
            for k in range(8):
                r.dma("gpsimd", taps[nm][ts(k, 128), :], S[k][:, HALO_B:], sG)
        b.scopes.remove(fsc)
        fsc.close()

    stop = getattr(cfg, "pb_stop", None)
    feeder.feed(16)
    done = stop == "pre"
    if not done:
        try:
            for li in range(NL):
                attn_layer(li)
                if stop == f"a{li}":
                    done = True
                    break
                ffn_layer(li)
                if stop == f"f{li}":
                    done = True
                    break
        except _StopBuild:
            done = True
    if done:
        r.barrier_from("gpsimd", sV)
        r.barrier_from("gpsimd", sT)
        for k in range(8):
            r.dma("gpsimd", outD[ts(k, 128), :], S[k][:, HALO_B:], sG)
        phase_barrier(b)
        return dict(outD=outD)

    fse = contextlib.ExitStack()
    b.scopes.append(fse)

    def feb(name, shape, dtype=F32):
        t = fse.enter_context(b.nc.sbuf_tensor(name + "Z", list(shape), dtype))
        b.sb[name] = t
        return t

    lnrb = [feb(f"lnrbF{i}", [128, 512], BF16) for i in range(2)]
    lnr2b = [feb(f"lnr2bF{i}", [128, 512], BF16) for i in range(2)]
    feb("Rt", [128, Tc])
    feb("Mt", [128, Tc])
    feb("lnscr", [128, Tc])
    feb("rowA", [1, Tc])
    feb("rowB", [1, Tc])
    b.ln_bufs = (lnrb, lnr2b)
    ln_fm(b, extra_res=b.x1saveD)
    r.wait("gpsimd", sV, r.count(sV))
    for k in range(8):
        r.dma("gpsimd", outD[ts(k, 128), :], S[k][:, HALO_B:], sG)
    phase_barrier(b)
    return dict(outD=outD)


def kernel(**inputs):
    from concourse.bass_utils import run_bass_kernel_spmd
    cfg = Cfg(Tc=2048, n_cores=8, taps=())
    b = Build(cfg)
    build_mamba(b)
    build_partb(b)
    nc = b.run_streams_and_close()
    in_maps = []
    for core in range(cfg.n_cores):
        m = host_inputs_mamba(inputs, cfg, core)
        m.update(host_inputs_partb(inputs, cfg, core))
        in_maps.append(m)
    res = run_bass_kernel_spmd(nc, in_maps, core_ids=list(range(cfg.n_cores)))
    out = np.concatenate([np.asarray(res.results[c]["outD"]).T
                          for c in range(cfg.n_cores)], 0)
    return np.ascontiguousarray(out[None]).astype(np.float32)


# version 02
# speedup vs baseline: 1.0007x; 1.0007x over previous
import contextlib
import numpy as np
import concourse.bass as bass
import concourse.mybir as mybir
F32 = mybir.dt.float32
BF16 = mybir.dt.bfloat16
AF = mybir.ActivationFunctionType
ALU = mybir.AluOpType
AX = mybir.AxisListType

ENGINES = ("sync", "vector", "scalar", "tensor", "gpsimd")

D_MODEL = 1024
D_INNER = 2048
NS = 16
QCH = 64
PAIR = 128
HALO_B = 257
HEADS = 16
DH = 64
WIN = 256
FP = 5632
FH = 2816


class Cfg:
    def __init__(self, Tc=2048, n_cores=8, nb_layers=6, taps=(),
                 mamba=True, partb=True):
        self.Tc = Tc
        self.n_cores = n_cores
        self.nb_layers = nb_layers
        self.taps = set(taps)
        self.mamba = mamba
        self.partb = partb
        self.npair = Tc // PAIR
        self.nch = Tc // QCH


class Rec:

    def __init__(self):
        self.streams = {e: [] for e in ENGINES}
        self.counts = {}
        self.trace = []
        self.autosem = {}
        self.autocnt = {"vector": 0, "scalar": 0}
        self.lastdone = {}
        self.semtab = {}
        self.makesem = None

    def emit(self, eng, fn, inc=None, inc_n=None):
        if inc is not None:
            n = inc_n if inc_n is not None else (16 if eng in ("sync", "gpsimd") else 1)
        else:
            n = None
        fence = eng in ("vector", "scalar") and eng in self.autosem
        if fence:
            prevtok = self.lastdone.get(eng)
            if inc is not None:
                key = inc.name if hasattr(inc, "name") else id(inc)
                newcnt = self.counts.get(key, 0) + n
                self.counts[key] = newcnt
                marker = (inc, newcnt)
            else:
                asem = self.autosem[eng]
                self.autocnt[eng] += 1
                marker = (asem, self.autocnt[eng])

            amt = n if inc is not None else 1

            def fenced(e, fn=fn, prevtok=prevtok, marker=marker, amt=amt):
                if prevtok is not None:
                    e.wait_ge(prevtok[0], prevtok[1])
                ins = fn(e)
                assert ins is not None, "auto-fenced op must return instruction"
                ins.then_inc(marker[0], amt)
            self.streams[eng].append(fenced)
            self.lastdone[eng] = marker
            self.trace.append((eng, getattr(fn, "_desc", fn.__qualname__),
                               f"marker {marker[1]}"))
            if inc is not None:
                return marker[1]
            return None

        def wrapped(e, fn=fn, inc=inc, n=n):
            ins = fn(e)
            if inc is not None:
                assert ins is not None
                ins.then_inc(inc, n)
        self.streams[eng].append(wrapped)
        if inc is not None:
            key = inc.name if hasattr(inc, "name") else id(inc)
            self.counts[key] = self.counts.get(key, 0) + n
            self.trace.append((eng, getattr(fn, "_desc", fn.__qualname__),
                               f"inc {key}->{self.counts[key]}"))
            return self.counts[key]
        self.trace.append((eng, getattr(fn, "_desc", fn.__qualname__), ""))
        return None

    def sem(self, tag):
        if not isinstance(tag, str):
            return tag
        if tag not in self.semtab:
            self.semtab[tag] = self.makesem("q_" + tag)
        return self.semtab[tag]

    def dma(self, eng, out, in_, inc, serial=True, **kw):
        def fn(e, out=out, in_=in_, kw=kw):
            return e.dma_start(out=out, in_=in_, **kw)
        c = self.emit(eng, fn, inc=self.sem(inc), inc_n=16)
        if serial:
            self.wait(eng, inc, c)
        return c

    def wait(self, eng, sem, count):
        sem = self.sem(sem)
        def fn(e, sem=sem, count=count):
            e.wait_ge(sem, count)
        key = sem.name if hasattr(sem, "name") else id(sem)
        self.trace.append((eng, "wait", f"{key}>={count}"))
        self.streams[eng].append(fn)

    def count(self, sem):
        sem = self.sem(sem)
        key = sem.name if hasattr(sem, "name") else id(sem)
        return self.counts.get(key, 0)

    def barrier_from(self, eng, sem):
        sem = self.sem(sem)
        c = self.count(sem)
        if c:
            self.wait(eng, sem, c)


def ts(i, n):
    return slice(i * n, (i + 1) * n)



def host_constants(cfg):
    c = {}
    m = np.zeros((PAIR, PAIR), np.float32)
    for b in range(2):
        u = np.arange(QCH)
        blk = (u[:, None] <= u[None, :]).astype(np.float32) - 0.5
        m[b * QCH:(b + 1) * QCH, b * QCH:(b + 1) * QCH] = blk
    c["TriC"] = m
    m2 = np.zeros((PAIR, PAIR), np.float32)
    for b in range(2):
        u = np.arange(QCH)
        m2[b * QCH:(b + 1) * QCH, b * QCH:(b + 1) * QCH] = \
            (u[:, None] <= u[None, :]).astype(np.float32)
    c["TriU"] = m2
    sels = np.zeros((NS, PAIR, PAIR), np.float32)
    for s_ in range(NS):
        sels[s_, 0:QCH, 2 * s_] = 1.0
        sels[s_, QCH:PAIR, 64 + 2 * s_] = 1.0
    c["SelS"] = sels
    ind = np.zeros((cfg.npair, cfg.nch, PAIR), np.float32)
    for p in range(cfg.npair):
        ind[p, 2 * p, 0:QCH] = 1.0
        ind[p, 2 * p + 1, QCH:PAIR] = 1.0
    c["PairInd"] = ind
    return c


def host_inputs_mamba(inp, cfg, core):
    Tc = cfg.Tc
    x = np.asarray(inp["L0_attn_x"])[0]
    T = x.shape[0]
    xT = np.ascontiguousarray(x.T)
    pad = np.zeros((D_MODEL, 3), np.float32)
    xTp = np.concatenate([pad, xT], 1)
    s = core * Tc
    d = {}
    d["x0s"] = np.ascontiguousarray(xTp[:, s:s + Tc + 3])
    d["W_inT"] = np.ascontiguousarray(np.asarray(inp["L0_attn_W_in"]).T)
    d["W_dtT"] = np.ascontiguousarray(np.asarray(inp["L0_attn_W_dt"]).T)
    d["W_xT"] = np.ascontiguousarray(np.asarray(inp["L0_attn_W_x"]).T)
    d["W_outT"] = np.ascontiguousarray(np.asarray(inp["L0_attn_W_out"]).T)
    d["conv_w"] = np.ascontiguousarray(np.asarray(inp["L0_attn_conv_w"])[:, 0, :])
    d["conv_b"] = np.asarray(inp["L0_attn_conv_b"]).reshape(D_INNER, 1)
    d["b_dt_row"] = np.asarray(inp["L0_attn_b_dt"]).reshape(1, D_INNER)
    d["A_log_row"] = np.asarray(inp["L0_attn_A_log"]).reshape(1, NS)
    d["D_row"] = np.asarray(inp["L0_attn_D_param"]).reshape(1, D_INNER)
    d["g_row"] = np.asarray(inp["L0_attn_ln_g"]).reshape(1, D_INNER)
    d["bln_row"] = np.asarray(inp["L0_attn_ln_b"]).reshape(1, D_INNER)
    pm = np.zeros((NS, 8), np.float32)
    pm[:, :core] = 1.0
    d["pmask"] = pm
    import ml_dtypes
    bf = ml_dtypes.bfloat16
    cst = host_constants(cfg)
    d["TriC"] = cst["TriC"]
    d["TriU"] = cst["TriU"].astype(bf)
    d["SelS"] = np.ascontiguousarray(
        cst["SelS"].transpose(1, 0, 2).reshape(PAIR, NS * PAIR)).astype(bf)
    d["PairInd"] = np.ascontiguousarray(
        cst["PairInd"].transpose(1, 0, 2).reshape(cfg.nch, cfg.npair * PAIR)).astype(bf)
    d["ident_in"] = np.eye(PAIR, dtype=np.float32)
    d["identb_in"] = np.eye(PAIR, dtype=np.float32).astype(bf)
    return d



class Build:
    def __init__(self, cfg):
        self.cfg = cfg
        self.nc = bass.Bass("TRN2", target_bir_lowering=False, debug=False)
        self.r = Rec()
        self.es = contextlib.ExitStack()
        self.sb = {}
        self.ps = {}
        self.dr = {}
        self.sem = {}
        self.taps_out = {}

    def dram_in(self, name, shape, dtype=F32):
        self.dr[name] = self.nc.dram_tensor(name, list(shape), dtype,
                                            kind="ExternalInput").ap()
        return self.dr[name]

    def dram_out(self, name, shape, dtype=F32):
        self.dr[name] = self.nc.dram_tensor(name, list(shape), dtype,
                                            kind="ExternalOutput").ap()
        return self.dr[name]

    def dram_tmp(self, name, shape, dtype=F32):
        kind = "ExternalOutput" if name in self.cfg.taps else "Internal"
        self.dr[name] = self.nc.dram_tensor(name, list(shape), dtype, kind=kind).ap()
        return self.dr[name]

    def sbuf(self, name, shape, dtype=F32):
        t = self.es.enter_context(self.nc.sbuf_tensor(name, list(shape), dtype))
        self.sb[name] = t
        return t

    def psum(self, name, shape, dtype=F32):
        t = self.es.enter_context(self.nc.psum_tensor(name, list(shape), dtype))
        self.ps[name] = t
        return t

    def init_autofence(self):
        self.r.makesem = self.semaphore
        for eng in ("vector", "scalar"):
            sem = self.semaphore("fence_" + eng)
            self.r.autosem[eng] = sem

    def register_consts(self, values):
        vals = sorted(set(float(v) for v in values))
        ct = self.sbuf("consttile", [128, len(vals)], F32)
        for j, v in enumerate(vals):
            self.nc.const_aps.register(F32, v, ct[:, j:j + 1])
        for j, v in enumerate(vals):
            self.r.emit("vector",
                        lambda e, j=j, v=v: e.memset(ct[:, j:j + 1], v),
                        inc=(self.sem["sV"] if j == len(vals) - 1 else None))

    def semaphore(self, name):
        s = self.es.enter_context(self.nc.semaphore(name))
        self.sem[name] = s
        return s

    def tap(self, name, ap_fn, shape, dtype=F32, eng="gpsimd"):
        if name not in self.cfg.taps:
            return
        o = self.dram_out("tap_" + name, shape, dtype)
        self.taps_out[name] = (o, ap_fn, eng)

    def finalize_block(self):
        with self.nc.Block() as block:
            for eng in ENGINES:
                getattr(block, eng)(lambda e, eng=eng: [f(e) for f in self.r.streams[eng]])

    def run_streams_and_close(self):
        self.finalize_block()
        for sc in reversed(getattr(self, "scopes", [])):
            sc.close()
        self.es.close()
        return self.nc





"""Mamba phase builder (v3: phase-scoped SBUF, DRAM-backed per-pair data)."""


MX0 = HALO_B - 3


def phase_barrier(b):
    r = b.r
    for eng in ("sync", "vector", "scalar", "tensor", "gpsimd"):
        for s in b.sem.values():
            c = b.r.count(s)
            if c:
                r.wait(eng, s, c)


def build_mamba(b):
    cfg = b.cfg
    Tc, npair, nch = cfg.Tc, cfg.npair, cfg.nch
    NCORE = cfg.n_cores
    r = b.r

    x0s = b.dram_in("x0s", [D_MODEL, Tc + 3])
    W_inT = b.dram_in("W_inT", [D_MODEL, 2 * D_INNER])
    W_dtT = b.dram_in("W_dtT", [D_INNER, D_INNER])
    W_xT = b.dram_in("W_xT", [D_INNER, 2 * NS])
    W_outT = b.dram_in("W_outT", [D_INNER, D_MODEL])
    conv_wD = b.dram_in("conv_w", [D_INNER, 4])
    conv_bD = b.dram_in("conv_b", [D_INNER, 1])
    b_dtD = b.dram_in("b_dt_row", [1, D_INNER])
    A_logD = b.dram_in("A_log_row", [1, NS])
    D_rowD = b.dram_in("D_row", [1, D_INNER])
    g_rowD = b.dram_in("g_row", [1, D_INNER])
    bln_rowD = b.dram_in("bln_row", [1, D_INNER])
    TriCD = b.dram_in("TriC", [PAIR, PAIR])
    TriUD = b.dram_in("TriU", [PAIR, PAIR], BF16)
    SelSD = b.dram_in("SelS", [PAIR, NS * PAIR], BF16)
    PairIndD = b.dram_in("PairInd", [nch, npair * PAIR], BF16)
    pmaskD = b.dram_in("pmask", [NS, 8])
    identD = b.dram_in("ident_in", [PAIR, PAIR])
    identbD = b.dram_in("identb_in", [PAIR, PAIR], BF16)

    WzbD = b.dram_tmp("WzbD", [D_MODEL, D_INNER], BF16)
    WoutbD = b.dram_tmp("WoutbD", [D_INNER, D_MODEL], BF16)
    xcFMD = b.dram_tmp("xcFMD", [D_INNER, Tc], BF16)
    xcTMD = b.dram_tmp("xcTMD", [Tc, D_INNER], BF16)
    slocD = b.dram_tmp("slocD", [Tc, D_INNER])
    dtxcD = b.dram_tmp("dtxcD", [Tc, D_INNER])
    stot2D = b.dram_tmp("stot2D", [nch, D_INNER])
    prawD = b.dram_tmp("prawD", [nch, NS, D_INNER])
    hinjD = b.dram_tmp("hinjD", [NS, nch, D_INNER], BF16)
    hlocD = b.dram_tmp("hlocD", [NS, D_INNER, nch])
    bscanD = b.dram_tmp("bscanD", [NS, D_INNER, nch])
    hcolD = b.dram_tmp("hcolD", [16, 128])
    hfullD = b.dram_tmp("hfullD", [NS, D_INNER, nch])
    yD = b.dram_tmp("yD", [cfg.Tc, D_INNER])
    wD = b.dram_tmp("wD", [cfg.Tc, D_INNER], BF16)
    zD = b.dram_tmp("zD", [cfg.Tc, D_INNER])
    hh2D = b.dram_tmp("hh2D", [NS, D_INNER, nch])
    cumD = b.dram_tmp("cumD", [NS, D_INNER, nch])
    ED = b.dram_tmp("ED", [NS, D_INNER, nch])
    cc_in = b.dram_tmp("cc_in", [NS, 2 * D_INNER])
    cc_out = b.dram_tmp("cc_out", [NCORE * NS, 2 * D_INNER])
    x1saveD = b.dram_tmp("x1saveD", [D_MODEL, Tc])
    b.x1saveD = x1saveD

    sL = b.semaphore("sL")
    sG = b.semaphore("sG")
    sT = b.semaphore("sT")
    sV = b.semaphore("sV")
    sS = b.semaphore("sS")
    sC = b.semaphore("sC")
    b.init_autofence()

    S = [b.sbuf(f"S{i}", [128, HALO_B + Tc], F32) for i in range(8)]
    b.S = S
    identf = b.sbuf("identf", [128, 128], F32)
    identb = b.sbuf("identb", [128, 128], BF16)
    ones_colb = b.sbuf("ones_colb", [128, 1], BF16)
    ones33 = b.sbuf("ones33", [128, 33], BF16)
    ones_rowf = b.sbuf("ones_rowf", [1, 128], F32)
    b.register_consts([1e-5, -0.5, -1.0, 0.5, 2.0, 1.0 / 1024, 1.0 / 2048,
                       1e-24])

    pwide = b.psum("pwide", [128, 2048], F32)
    pwide2 = b.psum("pwide2", [128, 1024], F32)
    ptr = b.psum("ptr", [128, 1024], BF16)
    ptrf = b.psum("ptrf", [128, 512], F32)

    ms = contextlib.ExitStack()
    b.mamba_scope = ms
    b.scopes = [ms]

    def msb(name, shape, dtype=F32):
        t = ms.enter_context(b.nc.sbuf_tensor(name, list(shape), dtype))
        b.sb[name] = t
        return t

    triC = msb("triC", [128, 128], F32)
    triU = msb("triU", [128, 128], BF16)
    selS = msb("selS", [128, NS * 128], BF16)
    pairInd = msb("pairInd", [nch, npair * 128], BF16)
    Acol = msb("Acol", [128, NS], F32)
    nAcol = msb("nAcol", [128, NS], F32)
    A2col = msb("A2col", [128, NS], F32)
    ssmT = msb("ssmT", [128, npair * 2 * NS], F32)
    arow = msb("arow", [1, NS], F32)
    hcol = msb("hcol", [128, 1], F32)
    stage = msb("stage", [128, 512], F32)
    evb = msb("evb", [128, D_INNER], F32)
    scr = msb("scr", [128, D_INNER], F32)
    slocp = msb("slocp", [128, D_INNER], F32)
    dtxcp = msb("dtxcp", [128, D_INNER], F32)

    for i in range(8):
        r.dma("sync", S[i][:, MX0:MX0 + Tc + 3], x0s[ts(i, 128), :], sL)
    r.dma("sync", triC[:], TriCD, sL)
    r.dma("sync", triU[:], TriUD, sL)
    r.dma("sync", selS[:], SelSD, sL)
    r.dma("sync", pairInd[:], PairIndD, sL)
    r.dma("sync", identf[:], identD, sL)
    r.dma("sync", identb[:], identbD, sL)
    r.dma("sync", arow[:], A_logD, sL)
    r.wait("vector", sL, r.count(sL))
    r.emit("vector", lambda e: e.memset(ones_colb[:], 1.0))
    r.emit("vector", lambda e: e.memset(ones33[:], 0.0))
    r.emit("vector", lambda e: e.memset(ones33[:, 32:33], 1.0))
    r.emit("vector", lambda e: e.memset(ones_rowf[:], 1.0), inc=sV)
    r.wait("tensor", sV, r.count(sV))
    r.emit("tensor", lambda e: e.matmul(out=ptrf[:, 0:NS], lhsT=ones_rowf[:],
                                        rhs=arow[:], start=True, stop=True), inc=sT)
    r.wait("scalar", sT, r.count(sT))
    r.wait("scalar", sV, r.count(sV))
    r.emit("scalar", lambda e: e.activation(out=nAcol[:], in_=ptrf[:, 0:NS],
                                            func=AF.Exp), inc=sS)
    r.wait("vector", sS, r.count(sS))
    r.emit("vector", lambda e: e.tensor_scalar_mul(out=Acol[:], in0=nAcol[:],
                                                   scalar1=-1.0))
    r.emit("vector", lambda e: e.tensor_scalar_mul(out=A2col[:], in0=Acol[:],
                                                   scalar1=2.0), inc=sV)
    r.barrier_from("tensor", sV)

    sa = contextlib.ExitStack()

    def sab(name, shape, dtype=F32):
        return sa.enter_context(b.nc.sbuf_tensor(name, list(shape), dtype))

    wld = [sab("wldA0", [128, D_INNER], F32)]
    wldZ = wld[0]
    gbufA = [sab(f"gbufA{i}", [128, D_INNER], BF16) for i in range(2)]
    Wxpb = [sab(f"Wxpb{i}", [128, D_INNER], BF16) for i in range(8)]
    x0b = [sab(f"x0bA{i}", [128, Tc + 3], BF16) for i in range(8)]
    xpbuf = sab("xpbuf", [128, Tc + 3], F32)
    convw = sab("convwA", [128, 4 * 16], F32)
    convb = sab("convbA", [128, 16], F32)
    xcft = gbufA[1]

    for i in range(16):
        r.dma("sync", convw[:, 4 * i:4 * i + 4], conv_wD[ts(i, 128), :], sL)
        r.dma("sync", convb[:, i:i + 1], conv_bD[ts(i, 128), :], sL)
    for k in range(8):
        r.barrier_from("sync", sV)
        r.dma("sync", wld[0][:], W_inT[ts(k, 128), 0:D_INNER], sL)
        r.wait("vector", sL, r.count(sL))
        r.emit("vector", lambda e, k=k: e.tensor_copy(out=Wxpb[k][:],
                                                      in_=wld[0][:]))
        r.emit("vector", lambda e, k=k: e.tensor_copy(out=x0b[k][:],
                                                      in_=S[k][:, MX0:MX0 + Tc + 3]),
               inc=sV)
        r.barrier_from("sync", sV)
        r.dma("sync", wldZ[:], W_inT[ts(k, 128), D_INNER:], sL)
        r.wait("vector", sL, r.count(sL))
        r.emit("vector", lambda e, k=k: e.tensor_copy(out=gbufA[k % 2][:],
                                                      in_=wldZ[:]), inc=sV)
        r.wait("gpsimd", sV, r.count(sV))
        r.dma("gpsimd", WzbD[ts(k, 128), :], gbufA[k % 2][:], sG)
        r.wait("vector", sG, max(0, r.count(sG) - 16))
    r.barrier_from("sync", sG)
    r.barrier_from("vector", sG)
    for k in range(16):
        r.barrier_from("sync", sV)
        r.dma("sync", wld[0][:, 0:D_MODEL], W_outT[ts(k, 128), :], sL)
        r.wait("vector", sL, r.count(sL))
        r.emit("vector", lambda e, k=k: e.tensor_copy(
            out=gbufA[k % 2][:, 0:D_MODEL], in_=wld[0][:, 0:D_MODEL]), inc=sV)
        r.wait("gpsimd", sV, r.count(sV))
        r.dma("gpsimd", WoutbD[ts(k, 128), :], gbufA[k % 2][:, 0:D_MODEL], sG)
        r.wait("vector", sG, max(0, r.count(sG) - 16))
    r.barrier_from("sync", sG)
    r.barrier_from("vector", sG)

    W = Tc + 3
    nblocks = [(o, min(512, W - o)) for o in range(0, W, 512)]
    for eM in range(16):
        r.barrier_from("tensor", sV)
        r.barrier_from("tensor", sS)
        for (o, w) in nblocks:
            for k in range(8):
                def mm_xz(e, eM=eM, o=o, w=w, k=k):
                    dst = pwide[:, o:o + w] if o < 2048 else pwide2[:, 0:w]
                    return e.matmul(out=dst, lhsT=Wxpb[k][:, ts(eM, 128)],
                                    rhs=x0b[k][:, o:o + w],
                                    start=(k == 0), stop=(k == 7))
                r.emit("tensor", mm_xz,
                       inc=(sT if (k == 7 and o == nblocks[-1][0]) else None))
        r.wait("vector", sT, r.count(sT))
        r.emit("vector", lambda e: e.tensor_copy(out=xpbuf[:, 0:min(2048, W)],
                                                 in_=pwide[:, 0:min(2048, W)]))
        if W > 2048:
            r.emit("vector", lambda e: e.tensor_copy(out=xpbuf[:, 2048:W],
                                                     in_=pwide2[:, 0:W - 2048]))
        r.emit("vector", lambda e, eM=eM: e.tensor_scalar(
            out=evb[:, 0:Tc], in0=xpbuf[:, 0:Tc],
            scalar1=convw[:, 4 * eM:4 * eM + 1], scalar2=None, op0=ALU.mult))
        for tk in (1, 2, 3):
            r.emit("vector", lambda e, eM=eM, tk=tk: e.scalar_tensor_tensor(
                out=evb[:, 0:Tc], in0=xpbuf[:, tk:tk + Tc],
                scalar=convw[:, 4 * eM + tk:4 * eM + tk + 1], in1=evb[:, 0:Tc],
                op0=ALU.mult, op1=ALU.add),
                inc=(sV if tk == 3 else None))
        r.wait("scalar", sV, r.count(sV))
        r.emit("scalar", lambda e, eM=eM: e.activation(
            out=xcft[:, 0:Tc], in_=evb[:, 0:Tc], func=AF.Silu,
            bias=convb[:, eM:eM + 1]), inc=sS)
        r.wait("gpsimd", sS, r.count(sS))
        r.dma("gpsimd", xcFMD[ts(eM, 128), :], xcft[:, 0:Tc], sG)
        r.wait("scalar", sG, r.count(sG))
        r.wait("vector", sS, r.count(sS))
    phase_barrier(b)
    sa.close()

    sb_ = contextlib.ExitStack()

    def sbb(name, shape, dtype=F32):
        return sb_.enter_context(b.nc.sbuf_tensor(name, list(shape), dtype))

    Wdtb = [sbb(f"Wdtb{i}", [128, D_INNER], BF16) for i in range(16)]
    Wxb = [sbb(f"Wxb{i}", [128, 2 * NS], BF16) for i in range(16)]
    wldB = [sbb("wldB0", [128, D_INNER], F32)]
    xcp = [sbb(f"xcp{i}", [128, 128], BF16) for i in range(16)]
    xcTb = sbb("xcTb", [128, D_INNER], BF16)
    Bdt = sbb("Bdt", [128, D_INNER], F32)
    bdtrow = scr

    r.dma("sync", bdtrow[0:1, 0:D_INNER], b_dtD, sL)
    for i in range(16):
        r.barrier_from("sync", sV)
        r.dma("sync", wldB[0][:], W_dtT[ts(i, 128), :], sL)
        r.wait("vector", sL, r.count(sL))
        r.emit("vector", lambda e, i=i: e.tensor_copy(
            out=Wdtb[i][:], in_=wldB[0][:]), inc=sV)
    for i in range(16):
        r.barrier_from("sync", sV)
        r.dma("sync", wldB[0][:, 0:2 * NS], W_xT[ts(i, 128), :], sL)
        r.wait("vector", sL, r.count(sL))
        r.emit("vector", lambda e, i=i: e.tensor_copy(
            out=Wxb[i][:], in_=wldB[0][:, 0:2 * NS]), inc=sV)
    r.barrier_from("tensor", sV)
    for blk in range(4):
        r.emit("tensor", lambda e, blk=blk: e.matmul(
            out=pwide[:, ts(blk, 512)], lhsT=ones_rowf[:],
            rhs=bdtrow[0:1, ts(blk, 512)], start=True, stop=True),
            inc=(sT if blk == 3 else None))
    r.wait("vector", sT, r.count(sT))
    r.emit("vector", lambda e: e.tensor_copy(out=Bdt[:], in_=pwide[:]), inc=sV)
    r.barrier_from("tensor", sV)

    for p in range(npair):
        for i in range(16):
            r.dma("sync", xcp[i][:], xcFMD[ts(i, 128), ts(p, 128)], sL)
        r.barrier_from("tensor", sV)
        r.barrier_from("tensor", sS)
        r.wait("tensor", sL, r.count(sL))
        for k in range(16):
            r.emit("tensor", lambda e, k=k: e.matmul(
                out=ptrf[:, 0:2 * NS], lhsT=xcp[k][:], rhs=Wxb[k][:],
                start=(k == 0), stop=(k == 15)), inc=(sT if k == 15 else None))
        r.wait("vector", sT, r.count(sT))
        r.emit("vector", lambda e, p=p: e.tensor_copy(
            out=ssmT[:, p * 2 * NS:(p + 1) * 2 * NS], in_=ptrf[:, 0:2 * NS]),
            inc=sV)
        for ob in range(4):
            for k in range(16):
                r.emit("tensor", lambda e, ob=ob, k=k: e.matmul(
                    out=pwide[:, ts(ob, 512)], lhsT=xcp[k][:],
                    rhs=Wdtb[k][:, ts(ob, 512)], start=(k == 0), stop=(k == 15)),
                    inc=(sT if (k == 15 and ob == 3) else None))
        r.wait("vector", sT, r.count(sT))
        r.emit("vector", lambda e: e.tensor_add(out=scr[:], in0=pwide[:],
                                                in1=Bdt[:]), inc=sV)
        r.wait("scalar", sV, r.count(sV))
        r.emit("scalar", lambda e: e.activation(out=evb[:], in_=scr[:],
                                                func=AF.Abs))
        r.emit("scalar", lambda e: e.activation(out=evb[:], in_=evb[:],
                                                func=AF.Exp, scale=-1.0))
        r.emit("scalar", lambda e: e.activation(out=evb[:], in_=evb[:],
                                                func=AF.Ln, bias=1.0), inc=sS)
        r.wait("vector", sS, r.count(sS))
        r.emit("vector", lambda e: e.scalar_tensor_tensor(
            out=scr[:], in0=scr[:], scalar=0.0, in1=evb[:], op0=ALU.max,
            op1=ALU.add), inc=sV)
        r.barrier_from("tensor", sV)
        for ob in range(4):
            r.emit("tensor", lambda e, ob=ob: e.matmul(
                out=pwide[:, ts(ob, 512)], lhsT=triC[:], rhs=scr[:, ts(ob, 512)],
                start=True, stop=True), inc=(sT if ob == 3 else None))
        r.wait("vector", sT, r.count(sT))
        r.emit("vector", lambda e: e.tensor_copy(out=slocp[:], in_=pwide[:]),
               inc=sV)
        r.wait("gpsimd", sV, r.count(sV))
        r.dma("gpsimd", slocD[ts(p, 128), :], slocp[:], sG)
        r.dma("gpsimd", stot2D[2 * p:2 * p + 2, :], slocp[63:128:64, :], sG)
        r.barrier_from("tensor", sV)
        for half in range(2):
            for i in range(8):
                r.emit("tensor", lambda e, half=half, i=i: e.transpose(
                    out=ptr[:, ts(i, 128)], in_=xcp[half * 8 + i][:],
                    identity=identb[:]), inc=(sT if i == 7 else None))
            r.wait("vector", sT, r.count(sT))
            for i in range(8):
                r.emit("vector", lambda e, half=half, i=i: e.tensor_copy(
                    out=xcTb[:, ts(half * 8 + i, 128)], in_=ptr[:, ts(i, 128)]),
                    inc=(sV if i == 7 else None))
            r.barrier_from("tensor", sV)
        r.emit("vector", lambda e: e.tensor_mul(out=dtxcp[:], in0=scr[:],
                                                in1=xcTb[:]), inc=sV)
        r.wait("gpsimd", sV, r.count(sV))
        r.dma("gpsimd", xcTMD[ts(p, 128), :], xcTb[:], sG)
        r.dma("gpsimd", dtxcD[ts(p, 128), :], dtxcp[:], sG)
        r.wait("sync", sG, r.count(sG))
        r.wait("vector", sG, r.count(sG))
        r.wait("scalar", sG, r.count(sG))
    phase_barrier(b)
    sb_.close()

    sc_ = contextlib.ExitStack()

    def scb(name, shape, dtype=F32):
        return sc_.enter_context(b.nc.sbuf_tensor(name, list(shape), dtype))

    gbufC = [scb(f"gbufC{i}", [128, D_INNER], BF16) for i in range(2)]
    wbufC = scb("wbufC", [128, D_INNER], F32)
    chn = [scb(f"chain{i}", [128, nch], F32) for i in range(6)]
    zerosch = scb("zerosch", [128, nch], F32)
    rankt = scb("rankt", [NS, 2 * D_INNER], F32)
    h0tile = scb("h0tile", [NS, D_INNER], F32)
    mcol = scb("mcol", [NS, 8], F32)
    stot2T = [scb(f"stot2T{i}", [128, nch], F32) for i in range(16)]
    h0T = scb("h0T", [128, 16 * NS], F32)

    r.dma("sync", mcol[:], pmaskD, sL)
    r.wait("vector", sL, r.count(sL))
    r.emit("vector", lambda e: e.memset(zerosch[:], 0.0), inc=sV)
    tcol = []
    for p in range(npair):
        r.barrier_from("sync", sV)
        r.barrier_from("sync", sS)
        r.dma("sync", slocp[:], slocD[ts(p, 128), :], sL)
        r.dma("sync", dtxcp[:], dtxcD[ts(p, 128), :], sL)
        r.barrier_from("scalar", sL)
        r.barrier_from("vector", sL)
        tcol.clear()
        for s in range(NS):
            r.wait("scalar", sV, r.count(sV))
            r.emit("scalar", lambda e, s=s: e.activation(
                out=wbufC[:], in_=slocp[:], func=AF.Exp, scale=nAcol[:, s:s + 1]),
                inc=sS)
            r.wait("vector", sS, r.count(sS))
            if s >= 2:
                r.wait("vector", sT, tcol[s - 2])
            r.emit("vector", lambda e, p=p, s=s: e.scalar_tensor_tensor(
                out=gbufC[s % 2][:], in0=wbufC[:],
                scalar=ssmT[:, p * 2 * NS + s:p * 2 * NS + s + 1],
                in1=dtxcp[:], op0=ALU.mult, op1=ALU.mult), inc=sV)
            r.wait("tensor", sV, r.count(sV))
            for ob in range(4):
                r.emit("tensor", lambda e, s=s, ob=ob: e.matmul(
                    out=pwide[:, ts(ob, 512)], lhsT=selS[:, ts(s, 128)],
                    rhs=gbufC[s % 2][:, ts(ob, 512)],
                    start=(s == 0), stop=(s == NS - 1)),
                    inc=(sT if ob == 3 else None))
            tcol.append(r.count(sT))
        r.wait("vector", sT, r.count(sT))
        for ob in range(4):
            r.emit("vector", lambda e, ob=ob: e.tensor_copy(
                out=stage[:], in_=pwide[:, ts(ob, 512)]), inc=sV)
            r.wait("gpsimd", sV, r.count(sV))
            r.dma("gpsimd", prawD[2 * p, :, ts(ob, 512)], stage[0:32:2, :], sG)
            r.dma("gpsimd", prawD[2 * p + 1, :, ts(ob, 512)], stage[64:96:2, :],
                  sG)
            r.wait("vector", sG, r.count(sG))
        r.barrier_from("tensor", sV)
    r.barrier_from("sync", sG)
    for dblk in range(16):
        r.dma("sync", stage[0:nch, 0:128], stot2D[:, ts(dblk, 128)], sL)
        r.wait("tensor", sL, r.count(sL))
        r.wait("tensor", sV, r.count(sV))
        r.emit("tensor", lambda e: e.transpose(
            out=ptrf[:, 0:nch], in_=stage[0:nch, 0:128],
            identity=identf[0:nch, 0:nch]), inc=sT)
        r.wait("vector", sT, r.count(sT))
        r.emit("vector", lambda e, dblk=dblk: e.tensor_copy(
            out=stot2T[dblk][:], in_=ptrf[:, 0:nch]), inc=sV)
        r.barrier_from("sync", sV)
        r.barrier_from("tensor", sV)
    for dblk in range(16):
        r.wait("vector", sS, r.count(sS))
        for s in range(NS):
            r.wait("scalar", sV, r.count(sV))
            r.wait("scalar", sG, r.count(sG))
            r.emit("scalar", lambda e, dblk=dblk, s=s: e.activation(
                out=chn[0][:], in_=stot2T[dblk][:], func=AF.Exp,
                scale=Acol[:, s:s + 1]), inc=sS)
            r.barrier_from("sync", sG)
            r.dma("sync", stage[0:nch, 0:128], prawD[:, s, ts(dblk, 128)], sL)
            r.wait("tensor", sL, r.count(sL))
            r.wait("tensor", sS, r.count(sS))
            r.wait("tensor", sV, r.count(sV))
            r.emit("tensor", lambda e: e.transpose(
                out=ptrf[:, 0:nch], in_=stage[0:nch, 0:128],
                identity=identf[0:nch, 0:nch]), inc=sT)
            r.wait("vector", sT, r.count(sT))
            r.wait("vector", sS, r.count(sS))
            r.wait("vector", sG, r.count(sG))
            r.emit("vector", lambda e: e.tensor_mul(out=chn[1][:], in0=chn[0][:],
                                                    in1=chn[0][:]))
            r.emit("vector", lambda e: e.tensor_mul(out=chn[2][:], in0=chn[0][:],
                                                    in1=ptrf[:, 0:nch]), inc=sV)
            r.wait("vector", sV, r.count(sV))
            r.emit("vector", lambda e: e.tensor_tensor_scan(
                out=chn[3][:], data0=chn[1][:], data1=chn[2][:], initial=0.0,
                op0=ALU.mult, op1=ALU.add))
            r.emit("vector", lambda e: e.tensor_tensor_scan(
                out=chn[4][:], data0=chn[1][:], data1=zerosch[:], initial=1.0,
                op0=ALU.mult, op1=ALU.add), inc=sV)
            r.wait("gpsimd", sV, r.count(sV))
            r.dma("gpsimd", cc_in[s:s + 1,
                                  D_INNER + dblk * 128:D_INNER + (dblk + 1) * 128],
                  chn[3][:, nch - 1:nch], sG)
            r.dma("gpsimd", cc_in[s:s + 1, ts(dblk, 128)],
                  chn[4][:, nch - 1:nch], sG)
            r.dma("gpsimd", hlocD[s, ts(dblk, 128), :], chn[3][:], sG)
            if "bscanD" in cfg.taps:
                r.dma("gpsimd", bscanD[s, ts(dblk, 128), :], chn[2][:], sG)
            r.dma("gpsimd", cumD[s, ts(dblk, 128), :], chn[4][:], sG)
            r.dma("gpsimd", ED[s, ts(dblk, 128), :], chn[0][:], sG)
            r.wait("vector", sG, r.count(sG))
            r.wait("tensor", sG, r.count(sG))
    r.barrier_from("gpsimd", sG)
    if NCORE > 1:
        r.emit("gpsimd", lambda e: e.collective_compute(
            "AllGather", ALU.bypass, replica_groups=[list(range(NCORE))],
            ins=[cc_in.opt()], outs=[cc_out.opt()]), inc=sC, inc_n=1)
        r.wait("gpsimd", sC, 1)
        r.wait("sync", sC, 1)
    r.barrier_from("vector", sG)
    r.emit("vector", lambda e: e.memset(h0tile[:], 0.0), inc=sV)
    if NCORE > 1:
        for rk in range(NCORE):
            r.barrier_from("sync", sV)
            r.dma("sync", rankt[:], cc_out[rk * NS:(rk + 1) * NS, :], sL)
            r.wait("vector", sL, r.count(sL))
            r.emit("vector", lambda e: e.tensor_scalar(
                out=rankt[:, 0:D_INNER], in0=rankt[:, 0:D_INNER],
                scalar1=-1.0, scalar2=None, op0=ALU.add))
            r.emit("vector", lambda e, rk=rk: e.tensor_scalar(
                out=rankt[:, 0:D_INNER], in0=rankt[:, 0:D_INNER],
                scalar1=mcol[:, rk:rk + 1], scalar2=1.0, op0=ALU.mult,
                op1=ALU.add))
            r.emit("vector", lambda e, rk=rk: e.tensor_scalar(
                out=rankt[:, D_INNER:], in0=rankt[:, D_INNER:],
                scalar1=mcol[:, rk:rk + 1], scalar2=None, op0=ALU.mult))
            r.emit("vector", lambda e: e.tensor_mul(
                out=h0tile[:], in0=h0tile[:], in1=rankt[:, 0:D_INNER]))
            r.emit("vector", lambda e: e.tensor_add(
                out=h0tile[:], in0=h0tile[:], in1=rankt[:, D_INNER:]), inc=sV)
    for dblk in range(16):
        r.wait("tensor", sV, r.count(sV))
        r.emit("tensor", lambda e, dblk=dblk: e.transpose(
            out=ptrf[:, 0:NS], in_=h0tile[:, ts(dblk, 128)],
            identity=identf[0:NS, 0:NS]), inc=sT)
        r.wait("vector", sT, r.count(sT))
        r.emit("vector", lambda e, dblk=dblk: e.tensor_copy(
            out=h0T[:, ts(dblk, NS)], in_=ptrf[:, 0:NS]), inc=sV)
        r.barrier_from("tensor", sV)
    for dblk in range(16):
        for s in range(NS):
            r.barrier_from("sync", sG)
            r.dma("sync", chn[3][:], hlocD[s, ts(dblk, 128), :], sL)
            r.dma("sync", chn[4][:], cumD[s, ts(dblk, 128), :], sL)
            r.dma("sync", chn[0][:], ED[s, ts(dblk, 128), :], sL)
            r.wait("vector", sL, r.count(sL))
            r.emit("vector", lambda e, dblk=dblk, s=s: e.scalar_tensor_tensor(
                out=chn[5][:], in0=chn[4][:],
                scalar=h0T[:, dblk * NS + s:dblk * NS + s + 1],
                in1=chn[3][:], op0=ALU.mult, op1=ALU.add), inc=sV)
            r.wait("vector", sV, r.count(sV))
            if nch > 1:
                r.emit("vector", lambda e: e.tensor_mul(
                    out=chn[2][:, 1:nch], in0=chn[0][:, 1:nch],
                    in1=chn[5][:, 0:nch - 1]))
            r.emit("vector", lambda e, dblk=dblk, s=s: e.tensor_scalar(
                out=chn[2][:, 0:1], in0=chn[0][:, 0:1],
                scalar1=h0T[:, dblk * NS + s:dblk * NS + s + 1], scalar2=None,
                op0=ALU.mult), inc=sV)
            if "hh2D" in cfg.taps:
                r.wait("gpsimd", sV, r.count(sV))
                r.dma("gpsimd", hfullD[s, ts(dblk, 128), :], chn[5][:], sG)
                r.dma("gpsimd", hh2D[s, ts(dblk, 128), :], chn[2][:], sG)
            r.wait("tensor", sV, r.count(sV))
            r.emit("tensor", lambda e: e.transpose(
                out=ptrf[0:nch, 0:128], in_=chn[2][:, 0:nch],
                identity=identf[:]), inc=sT)
            r.wait("vector", sT, r.count(sT))
            r.emit("vector", lambda e: e.tensor_copy(
                out=gbufC[0][0:nch, 0:128], in_=ptrf[0:nch, 0:128]), inc=sV)
            r.wait("gpsimd", sV, r.count(sV))
            r.dma("gpsimd", hinjD[s, :, ts(dblk, 128)], gbufC[0][0:nch, 0:128], sG)
            r.wait("vector", sG, r.count(sG))
            r.wait("tensor", sG, r.count(sG))
    phase_barrier(b)
    sc_.close()

    sd_ = contextlib.ExitStack()

    def sdb(name, shape, dtype=F32):
        return sd_.enter_context(b.nc.sbuf_tensor(name, list(shape), dtype))

    gbufD = [sdb(f"gbufD{i}", [128, D_INNER], BF16) for i in range(2)]
    wbufD = sdb("wbufD", [128, D_INNER], F32)
    ypair = sdb("ypair", [128, D_INNER], F32)
    xcTb2 = sdb("xcTb2", [128, D_INNER], BF16)
    hinjb = [sdb(f"hinjb{i}", [nch, D_INNER], BF16) for i in range(2)]
    wpairb = sdb("wpairb", [128, D_INNER], BF16)
    wfm = [sdb(f"wfm{i}", [128, 512], BF16) for i in range(16)]
    zxb = [sdb(f"zxb{i}", [128, 128], BF16) for i in range(8)]
    GTt = sdb("GTt", [128, D_INNER], F32)
    BTt = sdb("BTt", [128, D_INNER], F32)
    DTt = sdb("DTt", [128, D_INNER], F32)
    growB = scr

    r.emit("vector", lambda e: e.memset(hinjb[0][:], 0.0))
    r.emit("vector", lambda e: e.memset(hinjb[1][:], 0.0), inc=sV)
    for nm, dst in (("g_row", GTt), ("bln_row", BTt), ("D_row", DTt)):
        r.barrier_from("sync", sV)
        r.dma("sync", growB[0:1, 0:D_INNER], b.dr[nm], sL)
        r.barrier_from("tensor", sL)
        r.barrier_from("tensor", sV)
        for blk in range(4):
            r.emit("tensor", lambda e, blk=blk: e.matmul(
                out=pwide[:, ts(blk, 512)], lhsT=ones_rowf[:],
                rhs=growB[0:1, ts(blk, 512)], start=True, stop=True),
                inc=(sT if blk == 3 else None))
        r.wait("vector", sT, r.count(sT))
        r.emit("vector", lambda e, dst=dst: e.tensor_copy(out=dst[:],
                                                          in_=pwide[:]), inc=sV)
        r.barrier_from("tensor", sV)

    for p in range(npair):
        r.barrier_from("sync", sV)
        r.barrier_from("sync", sS)
        r.dma("sync", slocp[:], slocD[ts(p, 128), :], sL)
        r.dma("sync", dtxcp[:], dtxcD[ts(p, 128), :], sL)
        r.dma("sync", xcTb2[:], xcTMD[ts(p, 128), :], sL)
        r.barrier_from("scalar", sL)
        r.barrier_from("vector", sL)
        hts = []
        for s in range(NS):
            if s >= 2:
                r.wait("sync", sT, hts[s - 2])
            r.dma("sync", hinjb[s % 2][2 * p:2 * p + 2, :],
                  hinjD[s, 2 * p:2 * p + 2, :], sL)
            r.wait("scalar", sV, r.count(sV))
            r.emit("scalar", lambda e, s=s: e.activation(
                out=wbufD[:], in_=slocp[:], func=AF.Exp, scale=nAcol[:, s:s + 1]),
                inc=sS)
            r.wait("vector", sS, r.count(sS))
            if s >= 2:
                r.wait("vector", sT, hts[s - 2])
            r.emit("vector", lambda e, p=p, s=s: e.scalar_tensor_tensor(
                out=gbufD[s % 2][:], in0=wbufD[:],
                scalar=ssmT[:, p * 2 * NS + s:p * 2 * NS + s + 1],
                in1=dtxcp[:], op0=ALU.mult, op1=ALU.mult), inc=sV)
            gv = r.count(sV)
            r.wait("tensor", sV, gv)
            r.wait("tensor", sL, r.count(sL))
            pp = pwide
            for ob in range(4):
                r.emit("tensor", lambda e, p=p, s=s, ob=ob, pp=pp: e.matmul(
                    out=pp[:, ts(ob, 512)], lhsT=triU[:],
                    rhs=gbufD[s % 2][:, ts(ob, 512)], start=True, stop=False))
                r.emit("tensor", lambda e, p=p, s=s, ob=ob, pp=pp: e.matmul(
                    out=pp[:, ts(ob, 512)], lhsT=pairInd[:, ts(p, 128)],
                    rhs=hinjb[s % 2][:, ts(ob, 512)], start=False, stop=True),
                    inc=(sT if ob == 3 else None))
            hts.append(r.count(sT))
            r.wait("scalar", sV, gv)
            r.emit("scalar", lambda e, s=s: e.activation(
                out=wbufD[:], in_=slocp[:], func=AF.Exp, scale=Acol[:, s:s + 1]),
                inc=sS)
            r.wait("vector", sS, r.count(sS))
            r.wait("vector", sT, r.count(sT))
            r.emit("vector", lambda e, pp=pp: e.tensor_mul(out=evb[:],
                                                           in0=wbufD[:],
                                                           in1=pp[:]))
            if s == 0:
                r.emit("vector", lambda e, p=p, s=s: e.tensor_scalar(
                    out=ypair[:], in0=evb[:],
                    scalar1=ssmT[:, p * 2 * NS + NS + s:p * 2 * NS + NS + s + 1],
                    scalar2=None, op0=ALU.mult), inc=sV)
            else:
                r.emit("vector", lambda e, p=p, s=s: e.scalar_tensor_tensor(
                    out=ypair[:], in0=evb[:],
                    scalar=ssmT[:, p * 2 * NS + NS + s:p * 2 * NS + NS + s + 1],
                    in1=ypair[:], op0=ALU.mult, op1=ALU.add), inc=sV)
        if "yD" in cfg.taps:
            r.wait("gpsimd", sV, r.count(sV))
            r.dma("gpsimd", yD[ts(p, 128), :], ypair[:], sG)
        for k in range(8):
            r.emit("vector", lambda e, p=p, k=k: e.tensor_copy(
                out=zxb[k][:],
                in_=S[k][:, HALO_B + p * 128:HALO_B + (p + 1) * 128]),
                inc=(sV if k == 7 else None))
        r.barrier_from("sync", sG)
        r.barrier_from("sync", sT)
        r.barrier_from("tensor", sV)
        zts = []
        for k in range(8):
            if k >= 2:
                r.barrier_from("sync", sT)
            r.dma("sync", gbufD[k % 2][:], WzbD[ts(k, 128), :], sL)
            r.wait("tensor", sL, r.count(sL))
            for ob in range(4):
                r.emit("tensor", lambda e, k=k, ob=ob: e.matmul(
                    out=pwide[:, ts(ob, 512)], lhsT=zxb[k][:],
                    rhs=gbufD[k % 2][:, ts(ob, 512)], start=(k == 0),
                    stop=(k == 7)), inc=(sT if ob == 3 else None))
            zts.append(r.count(sT))
        r.wait("scalar", sT, r.count(sT))
        r.wait("scalar", sV, r.count(sV))
        r.emit("scalar", lambda e: e.activation(out=wbufD[:], in_=pwide[:],
                                                func=AF.Silu), inc=sS)
        r.wait("vector", sS, r.count(sS))
        r.emit("vector", lambda e: e.reduce_sum(out=hcol[:], in_=ypair[:],
                                                axis=AX.X, negate=True), inc=sV)
        r.wait("vector", sV, r.count(sV))
        r.emit("vector", lambda e: e.tensor_scalar_mul(out=hcol[:], in0=hcol[:],
                                                       scalar1=1.0 / D_INNER),
               inc=sV)
        r.wait("vector", sV, r.count(sV))
        r.emit("vector", lambda e: e.tensor_scalar(
            out=scr[:], in0=ypair[:], scalar1=hcol[:], scalar2=None,
            op0=ALU.add), inc=sV)
        r.wait("scalar", sV, r.count(sV))
        r.emit("scalar", lambda e: e.activation(out=evb[:], in_=scr[:],
                                                func=AF.Square,
                                                accum_out=hcol[:]), inc=sS)
        r.wait("vector", sS, r.count(sS))
        r.emit("vector", lambda e: e.tensor_scalar(
            out=hcol[:], in0=hcol[:], scalar1=1.0 / D_INNER, scalar2=1e-5,
            op0=ALU.mult, op1=ALU.add), inc=sV)
        r.wait("scalar", sV, r.count(sV))
        r.emit("scalar", lambda e: e.activation(out=hcol[:], in_=hcol[:],
                                                func=AF.Ln), inc=sS)
        r.wait("scalar", sS, r.count(sS))
        r.emit("scalar", lambda e: e.activation(out=hcol[:], in_=hcol[:],
                                                func=AF.Exp, scale=-0.5), inc=sS)
        r.wait("vector", sS, r.count(sS))
        r.emit("vector", lambda e: e.scalar_tensor_tensor(
            out=scr[:], in0=scr[:], scalar=hcol[:], in1=GTt[:], op0=ALU.mult,
            op1=ALU.mult))
        r.emit("vector", lambda e: e.tensor_add(out=scr[:], in0=scr[:],
                                                in1=BTt[:]))
        r.emit("vector", lambda e: e.tensor_mul(out=evb[:], in0=xcTb2[:],
                                                in1=DTt[:]))
        r.emit("vector", lambda e: e.tensor_add(out=scr[:], in0=scr[:],
                                                in1=evb[:]))
        r.emit("vector", lambda e: e.tensor_mul(out=wpairb[:], in0=scr[:],
                                                in1=wbufD[:]), inc=sV)
        if "wD" in cfg.taps:
            r.wait("gpsimd", sV, r.count(sV))
            r.dma("gpsimd", wD[ts(p, 128), :], wpairb[:], sG)
            r.dma("gpsimd", zD[ts(p, 128), :], wbufD[:], sG)
        r.barrier_from("tensor", sV)
        for half in range(2):
            for j in range(8):
                r.emit("tensor", lambda e, half=half, j=j: e.transpose(
                    out=ptr[:, ts(j, 128)], in_=wpairb[:, ts(half * 8 + j, 128)],
                    identity=identb[:]), inc=(sT if j == 7 else None))
            r.wait("vector", sT, r.count(sT))
            for j in range(8):
                r.emit("vector", lambda e, half=half, j=j, p=p: e.tensor_copy(
                    out=wfm[half * 8 + j][:, ts(p % 4, 128)],
                    in_=ptr[:, ts(j, 128)]), inc=(sV if j == 7 else None))
            r.barrier_from("tensor", sV)
        if p % 4 == 3 or p == npair - 1:
            g = p // 4
            NW = 128 * ((p % 4) + 1)
            for wave, dmMs in ((0, range(0, 6)), (1, range(6, 8))):
                r.barrier_from("sync", sG)
                r.barrier_from("sync", sT)
                r.barrier_from("tensor", sV)
                wts = []
                for k in range(16):
                    r.barrier_from("sync", sT)
                    if wave == 0:
                        r.dma("sync", gbufD[k % 2][:, 0:768],
                              WoutbD[ts(k, 128), 0:768], sL)
                    else:
                        r.dma("sync", gbufD[k % 2][:, 0:256],
                              WoutbD[ts(k, 128), 768:1024], sL)
                    r.wait("tensor", sL, r.count(sL))
                    for dmM in dmMs:
                        def mm_out(e, k=k, dmM=dmM, NW=NW, wave=wave):
                            if dmM < 4:
                                dst = pwide[:, ts(dmM, 512)][:, 0:NW]
                            else:
                                dst = pwide2[:, ts(dmM % 2 if wave else dmM - 4,
                                                   512)][:, 0:NW]
                            off = dmM * 128 - (768 if wave else 0)
                            return e.matmul(out=dst,
                                            lhsT=gbufD[k % 2][:, off:off + 128],
                                            rhs=wfm[k][:, 0:NW],
                                            start=(k == 0), stop=(k == 15))
                        r.emit("tensor", mm_out,
                               inc=(sT if dmM == dmMs[-1] else None))
                    wts.append(r.count(sT))
                r.wait("vector", sT, r.count(sT))
                for dmM in dmMs:
                    def evac_res(e, dmM=dmM, g=g, NW=NW, wave=wave):
                        if dmM < 4:
                            src = pwide[:, ts(dmM, 512)][:, 0:NW]
                        else:
                            src = pwide2[:, ts(dmM % 2 if wave else dmM - 4,
                                               512)][:, 0:NW]
                        dst = S[dmM][:, HALO_B + g * 512:HALO_B + g * 512 + NW]
                        return e.tensor_add(out=dst, in0=src, in1=dst)
                    r.emit("vector", evac_res,
                           inc=(sV if dmM == dmMs[-1] else None))
                r.barrier_from("tensor", sV)
    phase_barrier(b)
    sd_.close()

    se_ = contextlib.ExitStack()
    b.ln_scope = se_
    b.scopes.append(se_)

    def seb(name, shape, dtype=F32):
        t = se_.enter_context(b.nc.sbuf_tensor(name, list(shape), dtype))
        b.sb[name] = t
        return t

    lnrb = [seb(f"lnrb{i}", [128, 512], BF16) for i in range(2)]
    lnr2b = [seb(f"lnr2b{i}", [128, 512], BF16) for i in range(2)]
    Rt = seb("Rt", [128, 2048], F32)
    Mt = seb("Mt", [128, 2048], F32)
    lnscr = seb("lnscr", [128, 2048], F32)
    rowA = seb("rowA", [1, 2048], F32)
    rowB = seb("rowB", [1, 2048], F32)
    b.ln_bufs = (lnrb, lnr2b)

    ln_fm(b)
    r.wait("gpsimd", sV, r.count(sV))
    for i in range(8):
        r.dma("gpsimd", x1saveD[ts(i, 128), :], S[i][:, HALO_B:], sG)
    if getattr(b.cfg, "partb", False):
        phase_barrier(b)
        se_.close()
        ms.close()
        b.scopes = []
    return dict(x1saveD=x1saveD)


def ln_fm(b, extra_res=None):
    cfg = b.cfg
    Tc = cfg.Tc
    TBW = min(512, Tc)
    ntb = Tc // TBW
    r = b.r
    S = b.S
    sL, sG, sT, sV, sS = (b.sem["sL"], b.sem["sG"], b.sem["sT"], b.sem["sV"],
                          b.sem["sS"])
    pwide = b.ps["pwide"]
    pwide2 = b.ps["pwide2"]
    ones33 = b.sb["ones33"]
    ones_colb = b.sb["ones_colb"]
    ones_rowf = b.sb["ones_rowf"]
    rowA, rowB = b.sb["rowA"], b.sb["rowB"]
    Rt, Mt = b.sb["Rt"], b.sb["Mt"]
    rb, r2b = b.ln_bufs
    D = 1024

    if extra_res is not None:
        scr = b.sb["lnscr"]
        for i in range(8):
            r.barrier_from("sync", sV)
            r.dma("sync", scr[:, 0:Tc], extra_res[ts(i, 128), :], sL)
            r.wait("vector", sL, r.count(sL))
            r.emit("vector", lambda e, i=i, scr=scr: e.tensor_add(
                out=S[i][:, HALO_B:], in0=S[i][:, HALO_B:], in1=scr[:, 0:Tc]),
                inc=sV)
        r.barrier_from("scalar", sV)
    for tb in range(ntb):
        for k in range(8):
            r.barrier_from("vector", sT)
            r.barrier_from("scalar", sT)
            r.emit("vector", lambda e, tb=tb, k=k: e.tensor_copy(
                out=rb[k % 2][:, 0:TBW], in_=S[k][:, HALO_B + tb * TBW:
                                                  HALO_B + tb * TBW + TBW]),
                inc=sV)
            r.emit("scalar", lambda e, tb=tb, k=k: e.activation(
                out=r2b[k % 2][:, 0:TBW], in_=S[k][:, HALO_B + tb * TBW:
                                                   HALO_B + tb * TBW + TBW],
                func=AF.Square), inc=sS)
            r.wait("tensor", sV, r.count(sV))
            r.wait("tensor", sS, r.count(sS))
            r.emit("tensor", lambda e, tb=tb, k=k: e.matmul(
                out=pwide[0:1, ts(tb, TBW)], lhsT=ones_colb[:],
                rhs=rb[k % 2][:, 0:TBW], start=(k == 0), stop=False),
                inc=sT)
            r.emit("tensor", lambda e, tb=tb, k=k: e.matmul(
                out=pwide[0:33, ts(tb, TBW)], lhsT=ones33[:],
                rhs=r2b[k % 2][:, 0:TBW], start=False, stop=(k == 7)),
                inc=sT)
    r.wait("scalar", sT, r.count(sT))
    for tb in range(ntb):
        r.emit("scalar", lambda e, tb=tb: e.mul(
            out=rowA[:, ts(tb, TBW)], in_=pwide[0:1, ts(tb, TBW)],
            mul=1.0 / D))
        r.emit("scalar", lambda e, tb=tb: e.mul(
            out=rowB[:, ts(tb, TBW)], in_=pwide[32:33, ts(tb, TBW)],
            mul=1.0 / D), inc=(sS if tb == ntb - 1 else None))
    r.wait("vector", sS, r.count(sS))
    tmp = b.sb["Rt"]
    r.emit("vector", lambda e: e.tensor_mul(out=tmp[0:1, 0:Tc],
                                            in0=rowA[:, 0:Tc],
                                            in1=rowA[:, 0:Tc]), inc=sV)
    r.wait("vector", sV, r.count(sV))
    r.emit("vector", lambda e: e.tensor_sub(out=rowB[:, 0:Tc],
                                            in0=rowB[:, 0:Tc],
                                            in1=tmp[0:1, 0:Tc]), inc=sV)
    r.wait("scalar", sV, r.count(sV))
    r.emit("scalar", lambda e: e.activation(out=rowB[:, 0:Tc],
                                            in_=rowB[:, 0:Tc],
                                            func=AF.Ln, bias=1e-5), inc=sS)
    r.wait("scalar", sS, r.count(sS))
    r.emit("scalar", lambda e: e.activation(out=rowB[:, 0:Tc],
                                            in_=rowB[:, 0:Tc],
                                            func=AF.Exp, scale=-0.5), inc=sS)
    r.wait("vector", sS, r.count(sS))
    r.emit("vector", lambda e: e.scalar_tensor_tensor(
        out=rowA[:, 0:Tc], in0=rowA[:, 0:Tc], scalar=-1.0, in1=rowB[:, 0:Tc],
        op0=ALU.mult, op1=ALU.mult), inc=sV)
    r.barrier_from("tensor", sV)
    for tb in range(ntb):
        r.emit("tensor", lambda e, tb=tb: e.matmul(
            out=pwide2[0:128, 0:TBW], lhsT=ones_rowf[:],
            rhs=rowB[:, ts(tb, TBW)], start=True, stop=True))
        r.emit("tensor", lambda e, tb=tb: e.matmul(
            out=pwide2[0:128, 512:512 + TBW], lhsT=ones_rowf[:],
            rhs=rowA[:, ts(tb, TBW)], start=True, stop=True), inc=sT)
        r.wait("vector", sT, r.count(sT))
        r.emit("vector", lambda e, tb=tb: e.tensor_copy(
            out=Rt[:, ts(tb, TBW)], in_=pwide2[0:128, 0:TBW]))
        r.emit("vector", lambda e, tb=tb: e.tensor_copy(
            out=Mt[:, ts(tb, TBW)], in_=pwide2[0:128, 512:512 + TBW]), inc=sV)
        r.barrier_from("tensor", sV)
    for i in range(8):
        r.emit("vector", lambda e, i=i: e.tensor_mul(
            out=S[i][:, HALO_B:], in0=S[i][:, HALO_B:], in1=Rt[:, 0:Tc]))
        r.emit("vector", lambda e, i=i: e.tensor_add(
            out=S[i][:, HALO_B:], in0=S[i][:, HALO_B:], in1=Mt[:, 0:Tc]),
            inc=(sV if i == 7 else None))
    r.barrier_from("tensor", sV)

"""Part B: 6 layers of (shifted local attention + shifted GEGLU FFN) on the
FM stream S, plus final residual + layernorm. See mirror.part_b_layer.

Per-layer structure (per core, Tc tokens, FM stream S fp32):
  attn: halo_gather(257); per 512-token q-block: shifted-x cast, qkv matmuls
        (bf16 weights streamed wave-major), q/k prep (l2norm*scale + rotary
        with xpos), v -> token-major v^ tiles with a ones column (softmax
        denominator), scores per (head, itile) with additive -30000 masks,
        exp(s - max), transposed-p AV matmuls, Wo + residual into S.
  ffn:  halo_gather(1); per 1024-token block: LN stats (ones-matmuls), xn
        (g,b folded), W1 (fM-major stream) + exact Gelu + a*gelu(g), W2
        (wave-major stream) + b2 + residual into S.
Weights are cast fp32->bf16 on device by a feeder interleaved with compute.
"""

QB = 512
NIT = 4
FH = 2816
NFM = 44
BIG = 30000.0



def host_inputs_partb(inp, cfg, core):
    import ml_dtypes
    bf = ml_dtypes.bfloat16
    Tc = cfg.Tc
    d = {}
    for i in range(cfg.nb_layers):
        Wqkv = np.asarray(inp["L0_ffn_Wqkv"])[i]
        d[f"WqkvS{i}"] = np.ascontiguousarray(
            Wqkv.reshape(8, 128, 12, 256).transpose(2, 0, 1, 3)
            .reshape(12288, 256))
        d[f"WoS{i}"] = np.ascontiguousarray(np.asarray(inp["L0_ffn_Wo"])[i])
        W1 = np.asarray(inp["L0_ffn_W1"])[i]
        pad = np.zeros((1024, FH - 2730), np.float32)
        a = np.concatenate([W1[:, :2730], pad], 1).reshape(1024, 22, 128)
        g = np.concatenate([W1[:, 2730:], pad], 1).reshape(1024, 22, 128)
        W1p = np.stack([a, g], 2).reshape(1024, NFM * 128)
        d[f"W1S{i}"] = np.ascontiguousarray(
            W1p.reshape(8, 128, NFM, 128).transpose(2, 0, 1, 3)
            .reshape(NFM * 1024, 128))
        W2 = np.asarray(inp["L0_ffn_W2"])[i]
        W2p = np.concatenate([W2, np.zeros((FH - 2730, 1024), np.float32)], 0)
        d[f"W2S{i}"] = np.ascontiguousarray(
            W2p.reshape(22, 128, 4, 256).transpose(2, 0, 1, 3)
            .reshape(4 * 2816, 256))
        b1 = np.asarray(inp["L0_ffn_b1"])[i]
        b1a = np.concatenate([b1[:2730], np.zeros(FH - 2730, np.float32)])
        b1g = np.concatenate([b1[2730:], np.zeros(FH - 2730, np.float32)])
        b1p = np.stack([b1a.reshape(22, 128), b1g.reshape(22, 128)], 1)
        d[f"b1p{i}"] = np.ascontiguousarray(b1p.reshape(NFM, 128).T.copy())
        d[f"gcol8{i}"] = np.ascontiguousarray(
            np.asarray(inp["L0_ffn_ln_g"])[i].reshape(8, 128).T.copy())
        d[f"lnbcol8{i}"] = np.ascontiguousarray(
            np.asarray(inp["L0_ffn_ln_b"])[i].reshape(8, 128).T.copy())
        d[f"b2col8{i}"] = np.ascontiguousarray(
            np.asarray(inp["L0_ffn_b2"])[i].reshape(8, 128).T.copy())
        d[f"qscol{i}"] = np.tile(np.asarray(inp["L0_ffn_q_scale"])[i],
                                 2).reshape(128, 1).astype(np.float32)
        d[f"kscol{i}"] = np.tile(np.asarray(inp["L0_ffn_k_scale"])[i],
                                 2).reshape(128, 1).astype(np.float32)
    T = Tc * cfg.n_cores
    half = np.arange(0, 64, 2, dtype=np.float64)
    inv_freq = 1.0 / 10000.0 ** (half / 64.0)
    t = np.arange(T, dtype=np.float64)
    freqs = np.concatenate([t[:, None] * inv_freq[None, :]] * 2, 1)
    base = (half + 0.4 * 64.0) / (1.4 * 64.0)
    power = (t - T // 2) / 128.0
    scale = np.concatenate([base[None, :] ** power[:, None]] * 2, 1)
    cos, sin = np.cos(freqs), np.sin(freqs)
    s0 = core * Tc
    qsl = slice(s0, s0 + Tc)
    kidx = np.clip(np.arange(s0 - 256, s0 + Tc), 0, T - 1)

    def two(x):
        return np.ascontiguousarray(np.tile(x.T, (2, 1))).astype(bf)

    d["csq8"] = two(cos[qsl] * scale[qsl] * 8.0)
    d["ssq8"] = two(sin[qsl] * scale[qsl] * 8.0)
    d["csk"] = two(cos[kidx] / scale[kidx])
    d["ssk"] = two(sin[kidx] / scale[kidx])
    rot = np.zeros((128, 128), np.float32)
    for o in (0, 64):
        for j in range(32):
            rot[o + j + 32, o + j] = -1.0
            rot[o + j, o + j + 32] = 1.0
    d["ProtB"] = rot.astype(bf)
    hs = np.zeros((128, 2), np.float32)
    hs[0:64, 0] = 1.0
    hs[64:128, 1] = 1.0
    d["headsel2"] = hs.astype(bf)
    rn = np.zeros((2, 128), np.float32)
    rn[0, 0:64] = 1.0
    rn[1, 64:128] = 1.0
    d["rnsel"] = rn.astype(bf)
    w = np.arange(128)
    u = np.arange(128)
    tri0 = np.where(u[None, :] >= w[:, None], 0.0, -BIG).astype(np.float32)
    tri2 = np.where(u[None, :] <= w[:, None], 0.0, -BIG).astype(np.float32)
    zed = np.zeros((128, 128), np.float32)
    neg = np.full((128, 128), -BIG, np.float32)
    cls2 = np.concatenate([tri0, zed, tri2], 1)
    if core == 0:
        cls0 = np.concatenate([neg, neg, tri2], 1)
        cls1 = np.concatenate([neg, zed, tri2], 1)
    else:
        cls0 = cls2
        cls1 = cls2
    d["mbias"] = np.ascontiguousarray(np.concatenate([cls0, cls1, cls2], 1))
    rsel = np.zeros((128, 8), np.float32)
    if core > 0:
        rsel[:, core - 1] = 1.0
    d["rankselB"] = rsel
    return d



class CastFeeder:
    def __init__(self, b, wstr, wbb):
        self.b = b
        self.wstr = wstr
        self.wbb = wbb
        self.chunks = []
        self.milestones = {}
        self.fed = 0

    def add(self, srcD, dstD, nrows, ncols, chrows, key=None):
        a = chrows // 128
        for c0 in range(0, nrows, chrows):
            if a == 1:
                s = srcD[c0:c0 + 128, :]
                dd = dstD[c0:c0 + 128, :]
            else:
                s = srcD[c0:c0 + chrows, :].rearrange("(a p) c -> p a c", p=128)
                dd = dstD[c0:c0 + chrows, :].rearrange("(a p) c -> p a c", p=128)
            self.chunks.append((s, dd, a * ncols, a))
        if key:
            self.milestones[key] = len(self.chunks)

    def feed(self, n, cap_key=None):
        r = self.b.r
        cap = self.milestones[cap_key] if cap_key else len(self.chunks)
        for _ in range(n):
            if self.fed >= cap:
                return
            s, dd, w, a = self.chunks[self.fed]
            self.fed += 1
            r.barrier_from("gpsimd", "sWv")
            if a == 1:
                r.dma("gpsimd", self.wstr[:, 0:w], s, "sWl")
            else:
                r.dma("gpsimd",
                      self.wstr[:, 0:w].rearrange("p (a c) -> p a c", a=a),
                      s, "sWl")
            r.barrier_from("vector", "sWl")
            r.barrier_from("vector", "sW")
            r.emit("vector", lambda e, w=w: e.tensor_copy(
                out=self.wbb[:, 0:w], in_=self.wstr[:, 0:w]),
                inc=r.sem("sWv"))
            r.barrier_from("gpsimd", "sWv")
            if a == 1:
                r.dma("gpsimd", dd, self.wbb[:, 0:w], "sW")
            else:
                r.dma("gpsimd", dd,
                      self.wbb[:, 0:w].rearrange("p (a c) -> p a c", a=a),
                      "sW")

    def feed_until(self, key):
        need = self.milestones[key] - self.fed
        if need > 0:
            self.feed(need)

    def wait_use(self, eng, key):
        self.feed_until(key)
        self.b.r.wait(eng, "sW", 16 * self.milestones[key])



def build_partb(b):
    cfg = b.cfg
    Tc = cfg.Tc
    NCORE = cfg.n_cores
    NL = cfg.nb_layers
    nqb = Tc // QB
    TBB = min(1024, Tc)
    ntb = Tc // TBB
    r = b.r
    S = b.S
    sL, sG, sT, sV, sS, sC = (b.sem["sL"], b.sem["sG"], b.sem["sT"],
                              b.sem["sV"], b.sem["sS"], b.sem["sC"])
    pwide, pwide2 = b.ps["pwide"], b.ps["pwide2"]
    ptr, ptrf = b.ps["ptr"], b.ps["ptrf"]
    identb = b.sb["identb"]
    ones_colb = b.sb["ones_colb"]
    ones33 = b.sb["ones33"]
    ones_rowf = b.sb["ones_rowf"]

    phase_barrier(b)

    ins = {}
    for i in range(NL):
        for nm, shp in ((f"WqkvS{i}", [12288, 256]), (f"WoS{i}", [1024, 1024]),
                        (f"W1S{i}", [NFM * 1024, 128]),
                        (f"W2S{i}", [4 * 2816, 256]),
                        (f"b1p{i}", [128, NFM]), (f"gcol8{i}", [128, 8]),
                        (f"lnbcol8{i}", [128, 8]), (f"b2col8{i}", [128, 8]),
                        (f"qscol{i}", [128, 1]), (f"kscol{i}", [128, 1])):
            ins[nm] = b.dram_in(nm, shp, F32)
    for nm, shp in (("csq8", [128, Tc]), ("ssq8", [128, Tc]),
                    ("csk", [128, Tc + 256]), ("ssk", [128, Tc + 256]),
                    ("ProtB", [128, 128]), ("headsel2", [128, 2]),
                    ("rnsel", [2, 128])):
        ins[nm] = b.dram_in(nm, shp, BF16)
    ins["mbias"] = b.dram_in("mbias", [128, 3 * 384], F32)
    ins["rankselB"] = b.dram_in("rankselB", [128, 8], F32)
    haloI = b.dram_tmp("haloI", [1024, HALO_B])
    haloO = b.dram_tmp("haloO", [NCORE * 1024, HALO_B])
    haloI8 = b.dram_tmp("haloI8", [1024, 8])
    haloO8 = b.dram_tmp("haloO8", [NCORE * 1024, 8])
    WqkvbD = b.dram_tmp("WqkvbD", [12288, 256], BF16)
    WobD = b.dram_tmp("WobD", [1024, 1024], BF16)
    W1bD = b.dram_tmp("W1bD", [NFM * 1024, 128], BF16)
    W2bD = b.dram_tmp("W2bD", [4 * 2816, 256], BF16)
    outD = b.dram_out("outD", [D_MODEL, Tc])
    taps = {}
    for li in range(NL):
        for ph in ("a", "f"):
            nm = f"pb{ph}{li}"
            if nm in cfg.taps:
                taps[nm] = b.dram_out("tap_" + nm, [D_MODEL, Tc])

    pbc = contextlib.ExitStack()
    b.scopes.append(pbc)

    def pcb(name, shape, dtype=F32):
        t = pbc.enter_context(b.nc.sbuf_tensor(name, list(shape), dtype))
        b.sb[name] = t
        return t

    wstr = pcb("wstrB", [128, 2048], F32)
    wbb = pcb("wbbB", [128, 2048], BF16)
    mbias = pcb("mbiasT", [128, 3 * 384], F32)
    protB = pcb("protBT", [128, 128], BF16)
    headsel2 = pcb("headsel2T", [128, 2], BF16)
    rnsel = pcb("rnselT", [2, 128], BF16)
    ranksel = pcb("rankselT", [128, 8], F32)
    halot = pcb("halotB", [128, 8 * HALO_B], F32)
    qscol = pcb("qscolT", [128, 1], F32)
    kscol = pcb("kscolT", [128, 1], F32)
    biasf = pcb("biasfT", [128, NFM], F32)
    gcol8 = pcb("gcol8T", [128, 8], F32)
    lnbcol8 = pcb("lnbcol8T", [128, 8], F32)
    b2col8 = pcb("b2col8T", [128, 8], F32)
    xk1 = pcb("xk1T", [128, 8], BF16)

    for nm, dst in (("mbias", mbias), ("ProtB", protB),
                    ("headsel2", headsel2), ("rnsel", rnsel),
                    ("rankselB", ranksel)):
        r.dma("sync", dst[:], ins[nm], sL)
    r.barrier_from("vector", sL)

    feeder = CastFeeder(b, wstr, wbb)
    for i in range(NL):
        feeder.add(ins[f"WqkvS{i}"], WqkvbD, 12288, 256, 1024)
        feeder.add(ins[f"WoS{i}"], WobD, 1024, 1024, 256, key=f"attn{i}")
        feeder.add(ins[f"W1S{i}"], W1bD, NFM * 1024, 128, 2048)
        feeder.add(ins[f"W2S{i}"], W2bD, 4 * 2816, 256, 1024, key=f"ffn{i}")

    def halo_gather(width):
        hi, ho = (haloI, haloO) if width == HALO_B else (haloI8, haloO8)
        r.barrier_from("gpsimd", sV)
        for k in range(8):
            r.dma("gpsimd", hi[ts(k, 128), 0:width],
                  S[k][:, HALO_B + Tc - width:HALO_B + Tc], sG)
        if NCORE > 1:
            r.emit("gpsimd", lambda e, hi=hi, ho=ho: e.collective_compute(
                "AllGather", ALU.bypass, replica_groups=[list(range(NCORE))],
                ins=[hi.opt()], outs=[ho.opt()]), inc=sC, inc_n=1)
            r.wait("sync", sC, r.count(sC))
            hv = ho.rearrange("(c q) w -> q c w", q=1024)
            for k in range(8):
                r.barrier_from("sync", sV)
                r.dma("sync",
                      halot[:, 0:NCORE * width].rearrange(
                          "p (c w) -> p c w", c=NCORE),
                      hv[ts(k, 128), :, 0:width], sL)
                r.barrier_from("vector", sL)
                r.emit("vector", lambda e, k=k, width=width: e.tensor_scalar(
                    out=S[k][:, HALO_B - width:HALO_B],
                    in0=halot[:, 0:width], scalar1=ranksel[:, 0:1],
                    scalar2=None, op0=ALU.mult))
                for j in range(1, NCORE):
                    r.emit("vector", lambda e, k=k, j=j, width=width:
                           e.scalar_tensor_tensor(
                               out=S[k][:, HALO_B - width:HALO_B],
                               in0=halot[:, j * width:(j + 1) * width],
                               scalar=ranksel[:, j:j + 1],
                               in1=S[k][:, HALO_B - width:HALO_B],
                               op0=ALU.mult, op1=ALU.add),
                           inc=(sV if j == NCORE - 1 else None))
        else:
            for k in range(8):
                r.emit("vector", lambda e, k=k, width=width: e.memset(
                    S[k][:, HALO_B - width:HALO_B], 0.0),
                    inc=(sV if k == 7 else None))

    class _StopBuild(Exception):
        pass

    stop2 = getattr(cfg, "pb_stop2", None)

    def attn_layer(li):
        try:
            _attn_layer(li)
        except _StopBuild:
            raise

    def _attn_layer(li):
        def ck(tag):
            if li == 0 and stop2 == tag:
                raise _StopBuild()
        asc = contextlib.ExitStack()
        b.scopes.append(asc)

        def ab(name, shape, dtype=BF16):
            return asc.enter_context(
                b.nc.sbuf_tensor(name + str(li), list(shape), dtype))

        csq8 = ab("csq8", [128, Tc])
        ssq8 = ab("ssq8", [128, Tc])
        csk = ab("csk", [128, Tc + 256])
        ssk = ab("ssk", [128, Tc + 256])
        xkeep = [ab(f"xkeep{k}", [128, 257]) for k in range(8)]
        qhat = [ab(f"qhat{k}", [128, QB]) for k in range(8)]
        khat = [ab(f"khat{k}", [128, QB + 256]) for k in range(8)]
        vhat = [ab(f"vhat{j}", [128, 16 * 65]) for j in range(NIT + 2)]
        qrawb = ab("qraw", [128, 768])
        q2b = ab("q2b", [128, 768])
        qnb = ab("qnb", [128, 512])
        rnrowb = ab("rnrow", [2, 512])
        tq1 = ab("tq1", [128, 512], F32)
        vbring = [ab(f"vbring{i}", [128, 128]) for i in range(6)]
        nmcol = [ab(f"nmcol{i}", [128, 1], F32) for i in range(2)]
        invz = [ab(f"invz{i}", [128, 1], F32) for i in range(2)]

        r.barrier_from("gpsimd", sL)
        r.barrier_from("gpsimd", "w1t0")
        r.barrier_from("gpsimd", "w1t1")
        feeder.wait_use("sync", f"attn{li}")
        r.dma("sync", csq8[:], ins["csq8"], sL)
        r.dma("sync", ssq8[:], ins["ssq8"], sL)
        r.dma("sync", csk[:], ins["csk"], sL)
        r.dma("sync", ssk[:], ins["ssk"], sL)
        r.dma("sync", qscol[:], ins[f"qscol{li}"], sL)
        r.dma("sync", kscol[:], ins[f"kscol{li}"], sL)
        halo_gather(HALO_B)
        r.barrier_from("vector", sL)
        ck("t")
        wqv = WqkvbD
        wov = WobD

        for qb in range(nqb):
            qb0 = qb * QB
            sub = contextlib.ExitStack()
            b.scopes.append(sub)

            def sb2(name, shape, dtype=BF16):
                return sub.enter_context(
                    b.nc.sbuf_tensor(f"{name}{li}_{qb}", list(shape), dtype))

            xsb = [sb2(f"xsb{k}", [128, 768]) for k in range(8)]
            wqb = [sb2(f"wqb{i}", [128, 2048]) for i in range(2)]

            r.barrier_from("vector", sT)
            r.barrier_from("vector", sS)
            for k in range(8):
                dlt = 0 if k < 4 else -1
                if qb == 0:
                    r.emit("vector", lambda e, k=k, dlt=dlt, qb0=qb0:
                           e.tensor_copy(
                               out=xsb[k][:],
                               in_=S[k][:, HALO_B + qb0 - 256 + dlt:
                                        HALO_B + qb0 + 512 + dlt]),
                           inc=(sV if k == 7 else None))
                else:
                    if k < 4:
                        r.emit("vector", lambda e, k=k: e.tensor_copy(
                            out=xsb[k][:, 0:256], in_=xkeep[k][:, 1:257]))
                    else:
                        r.emit("vector", lambda e, k=k: e.tensor_copy(
                            out=xsb[k][:, 0:257], in_=xkeep[k][:, 0:257]))
                    c0 = 256 if k < 4 else 257
                    r.emit("vector", lambda e, k=k, c0=c0, qb0=qb0:
                           e.tensor_copy(
                               out=xsb[k][:, c0:768],
                               in_=S[k][:, HALO_B + qb0:
                                        HALO_B + qb0 + 768 - c0]),
                           inc=(sV if k == 7 else None))
            for j in range(NIT + 2):
                r.emit("vector", lambda e, j=j: e.memset(vhat[j][:], 1.0),
                       inc=(sV if j == NIT + 1 else None))
            r.wait("tensor", sV, r.count(sV))
            ck("x")

            wavemark = []
            evacmark = [0, 0]
            for w in range(12):
                wb = w % 2
                if w >= 2:
                    r.wait("sync", sT, wavemark[w - 2])
                r.dma("sync", wqb[wb][:].rearrange("p (k c) -> p k c", k=8),
                      wqv[w * 1024:(w + 1) * 1024, :].rearrange(
                          "(k p) c -> p k c", p=128), f"qWl{wb}",
                      serial=False)
                r.wait("tensor", f"qWl{wb}", r.count(f"qWl{wb}"))
                if w >= 1:
                    r.wait("tensor", sV, evacmark[0])
                    r.wait("tensor", sS, evacmark[1])
                for k in range(8):
                    for oin in range(2):
                        for (c0, cw) in ((0, 512), (512, 256)):
                            r.emit("tensor",
                                   lambda e, k=k, oin=oin, c0=c0, cw=cw, wb=wb:
                                   e.matmul(
                                       out=pwide[:, oin * 1024 + c0:
                                                 oin * 1024 + c0 + cw],
                                       lhsT=wqb[wb][:, k * 256 + oin * 128:
                                                    k * 256 + oin * 128 + 128],
                                       rhs=xsb[k][:, c0:c0 + cw],
                                       start=(k == 0), stop=(k == 7)),
                                   inc=(sT if (k == 7 and oin == 1 and
                                               c0 == 512) else None))
                wavemark.append(r.count(sT))
                ck(f"m{w}")
                for oin in range(2):
                    ot = 2 * w + oin
                    psl = pwide[:, oin * 1024:oin * 1024 + 768]
                    if ot < 16:
                        isq = ot < 8
                        lo = 256 if isq else 0
                        W = 768 - lo
                        r.wait("vector", sT, wavemark[-1])
                        r.emit("vector", lambda e, psl=psl, lo=lo, W=W:
                               e.tensor_copy(out=qrawb[:, 0:W],
                                             in_=psl[:, lo:lo + W]), inc=sV)
                        evacmark[0] = r.count(sV)
                        ck(f"ev{ot}")
                        r.wait("scalar", sV, evacmark[0])
                        r.emit("scalar", lambda e, W=W:
                               e.activation(out=q2b[:, 0:W],
                                            in_=qrawb[:, 0:W],
                                            func=AF.Square), inc=sS)
                        evacmark[1] = r.count(sS)
                        ck(f"e{ot}")
                        scol = qscol if isq else kscol
                        cstab = csq8 if isq else csk
                        sstab = ssq8 if isq else ssk
                        dest = qhat[ot] if isq else khat[ot - 8]
                        for c in range(0, W, 512):
                            cw = min(512, W - c)
                            cc = qb0 + c
                            r.wait("tensor", sS, r.count(sS))
                            r.wait("tensor", sV, r.count(sV))
                            r.emit("tensor", lambda e, c=c, cw=cw: e.matmul(
                                out=ptrf[0:2, 0:cw], lhsT=headsel2[:],
                                rhs=q2b[:, c:c + cw], start=True, stop=True),
                                inc=sT)
                            r.wait("vector", sT, r.count(sT))
                            r.emit("vector", lambda e, cw=cw: e.tensor_copy(
                                out=tq1[0:2, 0:cw], in_=ptrf[0:2, 0:cw]),
                                inc=sV)
                            r.wait("scalar", sV, r.count(sV))
                            r.emit("scalar", lambda e, cw=cw: e.activation(
                                out=tq1[64:66, 0:cw], in_=tq1[0:2, 0:cw],
                                func=AF.Ln, bias=1e-24), inc=sS)
                            r.wait("scalar", sS, r.count(sS))
                            r.emit("scalar", lambda e, cw=cw: e.activation(
                                out=rnrowb[0:2, 0:cw], in_=tq1[64:66, 0:cw],
                                func=AF.Exp, scale=-0.5), inc=sS)
                            r.wait("tensor", sS, r.count(sS))
                            r.emit("tensor", lambda e, cw=cw: e.matmul(
                                out=pwide2[:, 0:cw], lhsT=rnsel[:],
                                rhs=rnrowb[0:2, 0:cw], start=True, stop=True),
                                inc=sT)
                            r.wait("vector", sT, r.count(sT))
                            r.emit("vector", lambda e, c=c, cw=cw, scol=scol:
                                   e.scalar_tensor_tensor(
                                       out=qnb[:, 0:cw],
                                       in0=qrawb[:, c:c + cw],
                                       scalar=scol[:, 0:1],
                                       in1=pwide2[:, 0:cw],
                                       op0=ALU.mult, op1=ALU.mult), inc=sV)
                            r.wait("tensor", sV, r.count(sV))
                            r.emit("tensor", lambda e, cw=cw: e.matmul(
                                out=pwide2[:, 512:512 + cw], lhsT=protB[:],
                                rhs=qnb[:, 0:cw], start=True, stop=True),
                                inc=sT)
                            r.emit("vector", lambda e, c=c, cw=cw, cc=cc,
                                   dest=dest, cstab=cstab: e.tensor_mul(
                                       out=dest[:, c:c + cw],
                                       in0=qnb[:, 0:cw],
                                       in1=cstab[:, cc:cc + cw]))
                            r.wait("vector", sT, r.count(sT))
                            r.emit("vector", lambda e, cw=cw, cc=cc,
                                   sstab=sstab: e.tensor_mul(
                                       out=tq1[:, 0:cw],
                                       in0=pwide2[:, 512:512 + cw],
                                       in1=sstab[:, cc:cc + cw]))
                            r.emit("vector", lambda e, c=c, cw=cw, dest=dest:
                                   e.tensor_add(out=dest[:, c:c + cw],
                                                in0=dest[:, c:c + cw],
                                                in1=tq1[:, 0:cw]), inc=sV)
                        ck(f"p{ot}")
                    else:
                        vo = ot - 16
                        r.wait("vector", sT, wavemark[-1])
                        r.wait("vector", sT, r.count(sT))
                        for jt in range(6):
                            r.emit("vector", lambda e, psl=psl, jt=jt:
                                   e.tensor_copy(
                                       out=vbring[jt][:],
                                       in_=psl[:, jt * 128:jt * 128 + 128]),
                                   inc=(sV if jt == 5 else None))
                        evacmark[0] = r.count(sV)
                        ck(f"v{ot}")
                        r.wait("tensor", sV, r.count(sV))
                        for jt in range(6):
                            r.emit("tensor", lambda e, jt=jt: e.transpose(
                                out=ptr[:, jt * 128:jt * 128 + 128],
                                in_=vbring[jt][:], identity=identb[:]),
                                inc=(sT if jt == 5 else None))
                        r.wait("vector", sT, r.count(sT))
                        for jt in range(6):
                            for hh in range(2):
                                h = 2 * vo + hh
                                r.emit("vector", lambda e, jt=jt, hh=hh, h=h:
                                       e.tensor_copy(
                                           out=vhat[jt][:, 65 * h:65 * h + 64],
                                           in_=ptr[:, jt * 128 + 64 * hh:
                                                   jt * 128 + 64 * hh + 64]),
                                       inc=(sV if (jt == 5 and hh == 1)
                                            else None))
            ck("w")
            feeder.feed(33 // (2 * nqb) + 2, cap_key=f"ffn{li}")
            r.barrier_from("tensor", sV)
            r.barrier_from("tensor", sS)

            aoTM = sb2("aoTM", [128, 1024])
            aofm = [sb2(f"aofm{k}", [128, QB]) for k in range(8)]
            snb = [sb2(f"snb{i}", [128, 384], F32) for i in range(2)]
            pb_ = [sb2(f"pb{i}", [128, 384]) for i in range(2)]
            pTb = [sb2(f"pTb{i}", [128, 384]) for i in range(2)]
            sslot = [ptrf[:, 0:384], pwide2[:, 512:896]]
            avslot = [pwide2[:, 0:65], pwide2[:, 128:193]]
            snm = [0, 0]
            ptm = [0, 0]
            aom = [0, 0]
            for it in range(NIT):
                cls = min(qb * NIT + it, 2)
                for h in range(16):
                    hp = h % 2
                    oq = h // 2
                    bq = (h % 2) * 64
                    i0 = it * 128
                    if snm[hp]:
                        r.wait("tensor", sV, snm[hp])
                    r.emit("tensor", lambda e, hp=hp, oq=oq, bq=bq, i0=i0:
                           e.matmul(out=sslot[hp],
                                    lhsT=qhat[oq][bq:bq + 64, i0:i0 + 128],
                                    rhs=khat[oq][bq:bq + 64, i0:i0 + 384],
                                    start=True, stop=True), inc=sT)
                    r.wait("vector", sT, r.count(sT))
                    r.emit("vector", lambda e, hp=hp, cls=cls: e.tensor_add(
                        out=snb[hp][:], in0=sslot[hp],
                        in1=mbias[:, cls * 384:(cls + 1) * 384]))
                    snm[hp] = r.count(sV) + 1
                    r.emit("vector", lambda e, hp=hp: e.reduce_max(
                        out=nmcol[hp][:], in_=snb[hp][:], axis=AX.X,
                        negate=True), inc=sV)
                    r.wait("scalar", sV, r.count(sV))
                    r.emit("scalar", lambda e, hp=hp: e.activation(
                        out=pb_[hp][:], in_=snb[hp][:], func=AF.Exp,
                        bias=nmcol[hp][:]), inc=sS)
                    r.wait("tensor", sS, r.count(sS))
                    if ptm[hp]:
                        r.wait("tensor", sV, ptm[hp])
                    for jsub in range(3):
                        r.emit("tensor", lambda e, hp=hp, jsub=jsub:
                               e.transpose(
                                   out=ptr[:, hp * 512 + jsub * 128:
                                           hp * 512 + jsub * 128 + 128],
                                   in_=pb_[hp][:, jsub * 128:jsub * 128 + 128],
                                   identity=identb[:]),
                               inc=(sT if jsub == 2 else None))
                    r.wait("vector", sT, r.count(sT))
                    r.emit("vector", lambda e, hp=hp: e.tensor_copy(
                        out=pTb[hp][:], in_=ptr[:, hp * 512:hp * 512 + 384]),
                        inc=sV)
                    ptm[hp] = r.count(sV)
                    r.wait("tensor", sV, ptm[hp])
                    if aom[hp]:
                        r.wait("tensor", sV, aom[hp])
                    for jsub in range(3):
                        r.emit("tensor", lambda e, hp=hp, jsub=jsub, it=it,
                               h=h: e.matmul(
                                   out=avslot[hp],
                                   lhsT=pTb[hp][:, jsub * 128:jsub * 128 + 128],
                                   rhs=vhat[it + jsub][:, 65 * h:65 * h + 65],
                                   start=(jsub == 0), stop=(jsub == 2)),
                               inc=(sT if jsub == 2 else None))
                    r.wait("vector", sT, r.count(sT))
                    r.emit("vector", lambda e, hp=hp: e.reciprocal(
                        out=invz[hp][:], in_=avslot[hp][:, 64:65]))
                    r.emit("vector", lambda e, hp=hp, h=h: e.tensor_scalar(
                        out=aoTM[:, 64 * h:64 * h + 64],
                        in0=avslot[hp][:, 0:64], scalar1=invz[hp][:, 0:1],
                        scalar2=None, op0=ALU.mult), inc=sV)
                    aom[hp] = r.count(sV)
                r.wait("tensor", sV, r.count(sV))
                for k in range(8):
                    r.emit("tensor", lambda e, k=k: e.transpose(
                        out=ptr[:, k * 128:k * 128 + 128],
                        in_=aoTM[:, k * 128:k * 128 + 128],
                        identity=identb[:]), inc=(sT if k == 7 else None))
                r.wait("vector", sT, r.count(sT))
                for k in range(8):
                    r.emit("vector", lambda e, k=k, it=it: e.tensor_copy(
                        out=aofm[k][:, it * 128:it * 128 + 128],
                        in_=ptr[:, k * 128:k * 128 + 128]),
                        inc=(sV if k == 7 else None))
                ptm = [r.count(sV), r.count(sV)]
                ck(f"s{it}")
            feeder.feed(33 // (2 * nqb) + 2, cap_key=f"ffn{li}")

            wos = sb2("wos", [128, 4096])
            r.barrier_from("tensor", sV)
            for w in range(2):
                r.barrier_from("sync", sT)
                r.dma("sync", wos[:].rearrange("p (k d) -> p k d", k=8),
                      wov[:, w * 512:(w + 1) * 512].rearrange(
                          "(k p) d -> p k d", p=128), sL)
                ck(f"wl{w}")
                r.wait("tensor", sL, r.count(sL))
                r.barrier_from("tensor", sV)
                for k in range(8):
                    for dm2 in range(4):
                        r.emit("tensor", lambda e, k=k, dm2=dm2: e.matmul(
                            out=pwide[:, dm2 * 512:dm2 * 512 + 512],
                            lhsT=wos[:, k * 512 + dm2 * 128:
                                     k * 512 + dm2 * 128 + 128],
                            rhs=aofm[k][:], start=(k == 0), stop=(k == 7)),
                            inc=(sT if (k == 7 and dm2 == 3) else None))
                ck(f"wm{w}")
                if w == 0:
                    for k in range(8):
                        r.emit("vector", lambda e, k=k, qb0=qb0: e.tensor_copy(
                            out=xkeep[k][:],
                            in_=S[k][:, HALO_B + qb0 + 255:
                                     HALO_B + qb0 + 512]))
                r.wait("vector", sT, r.count(sT))
                for dm2 in range(4):
                    dm = w * 4 + dm2
                    r.emit("vector", lambda e, dm=dm, dm2=dm2, qb0=qb0:
                           e.tensor_add(
                               out=S[dm][:, HALO_B + qb0:HALO_B + qb0 + 512],
                               in0=pwide[:, dm2 * 512:dm2 * 512 + 512],
                               in1=S[dm][:, HALO_B + qb0:HALO_B + qb0 + 512]),
                           inc=(sV if dm2 == 3 else None))
                ck(f"we{w}")
            r.barrier_from("tensor", sV)
            b.scopes.remove(sub)
            sub.close()
        nm = f"pba{li}"
        if nm in taps:
            r.barrier_from("gpsimd", sV)
            for k in range(8):
                r.dma("gpsimd", taps[nm][ts(k, 128), :], S[k][:, HALO_B:], sG)
        b.scopes.remove(asc)
        asc.close()

    def ffn_layer(li):
        fsc = contextlib.ExitStack()
        b.scopes.append(fsc)

        def fb(name, shape, dtype=BF16):
            return fsc.enter_context(
                b.nc.sbuf_tensor(name + str(li), list(shape), dtype))

        xnb = [fb(f"xnb{k}", [128, TBB]) for k in range(8)]
        ag = [fb(f"ag{j}", [128, TBB]) for j in range(22)]
        w1l = [fb(f"w1l{i}", [128, 1024]) for i in range(2)]
        w2w = fb("w2w", [128, 22 * 256])
        r2b = [fb(f"r2b{i}", [128, 512]) for i in range(2)]
        Rt = fb("RtF", [128, 512], F32)
        Mt = fb("MtF", [128, 512], F32)
        tmpf = fb("tmpfF", [128, 512], F32)
        ge = fb("geF", [128, TBB])
        gbf = fb("gbfF", [128, TBB], F32)
        rowA = halot[0:1, 0:TBB]
        rowB = halot[0:1, TBB:2 * TBB]

        r.barrier_from("gpsimd", sL)
        r.barrier_from("gpsimd", "qWl0")
        r.barrier_from("gpsimd", "qWl1")
        feeder.wait_use("sync", f"ffn{li}")
        r.dma("sync", biasf[:], ins[f"b1p{li}"], sL)
        r.dma("sync", gcol8[:], ins[f"gcol8{li}"], sL)
        r.dma("sync", lnbcol8[:], ins[f"lnbcol8{li}"], sL)
        r.dma("sync", b2col8[:], ins[f"b2col8{li}"], sL)
        halo_gather(8)
        r.barrier_from("vector", sL)

        for tb in range(ntb):
            tb0 = tb * TBB
            r.barrier_from("vector", sT)
            r.barrier_from("scalar", sT)
            for k in range(8):
                dlt = 0 if k < 4 else -1
                r.emit("vector", lambda e, k=k, dlt=dlt, tb0=tb0:
                       e.tensor_copy(out=xnb[k][:],
                                     in_=S[k][:, HALO_B + tb0 + dlt:
                                              HALO_B + tb0 + TBB + dlt]),
                       inc=(sV if (tb == 0 or k < 4) and k == 7 else None))
                if tb > 0 and k >= 4:
                    r.emit("vector", lambda e, k=k: e.tensor_copy(
                        out=xnb[k][:, 0:1], in_=xk1[:, k:k + 1]),
                        inc=(sV if k == 7 else None))
            nsub = TBB // 512
            for sb_ in range(nsub):
                smk = []
                for k in range(8):
                    r.wait("scalar", sV, r.count(sV))
                    if k >= 2:
                        r.wait("scalar", sT, smk[k - 2])
                    r.emit("scalar", lambda e, k=k, sb_=sb_: e.activation(
                        out=r2b[k % 2][:],
                        in_=xnb[k][:, sb_ * 512:sb_ * 512 + 512],
                        func=AF.Square), inc=sS)
                    r.wait("tensor", sS, r.count(sS))
                    r.wait("tensor", sV, r.count(sV))
                    r.emit("tensor", lambda e, k=k, sb_=sb_: e.matmul(
                        out=pwide[0:1, sb_ * 512:sb_ * 512 + 512],
                        lhsT=ones_colb[:],
                        rhs=xnb[k][:, sb_ * 512:sb_ * 512 + 512],
                        start=(k == 0), stop=False))
                    r.emit("tensor", lambda e, k=k, sb_=sb_: e.matmul(
                        out=pwide[0:33, sb_ * 512:sb_ * 512 + 512],
                        lhsT=ones33[:], rhs=r2b[k % 2][:],
                        start=False, stop=(k == 7)),
                        inc=sT)
                    smk.append(r.count(sT))
            r.wait("scalar", sT, r.count(sT))
            for sb_ in range(nsub):
                r.emit("scalar", lambda e, sb_=sb_: e.mul(
                    out=rowA[:, sb_ * 512:sb_ * 512 + 512],
                    in_=pwide[0:1, sb_ * 512:sb_ * 512 + 512],
                    mul=1.0 / 1024))
                r.emit("scalar", lambda e, sb_=sb_: e.mul(
                    out=rowB[:, sb_ * 512:sb_ * 512 + 512],
                    in_=pwide[32:33, sb_ * 512:sb_ * 512 + 512],
                    mul=1.0 / 1024), inc=(sS if sb_ == nsub - 1 else None))
            r.wait("vector", sS, r.count(sS))
            for sb_ in range(nsub):
                sl = slice(sb_ * 512, sb_ * 512 + 512)
                r.emit("vector", lambda e, sl=sl: e.tensor_mul(
                    out=Rt[0:1, 0:512], in0=rowA[:, sl], in1=rowA[:, sl]))
                r.emit("vector", lambda e, sl=sl: e.tensor_sub(
                    out=rowB[:, sl], in0=rowB[:, sl], in1=Rt[0:1, 0:512]),
                    inc=(sV if sb_ == nsub - 1 else None))
            r.wait("scalar", sV, r.count(sV))
            r.emit("scalar", lambda e: e.activation(out=rowB[:], in_=rowB[:],
                                                    func=AF.Ln, bias=1e-5),
                   inc=sS)
            r.wait("scalar", sS, r.count(sS))
            r.emit("scalar", lambda e: e.activation(out=rowB[:], in_=rowB[:],
                                                    func=AF.Exp, scale=-0.5),
                   inc=sS)
            r.wait("vector", sS, r.count(sS))
            r.emit("vector", lambda e: e.scalar_tensor_tensor(
                out=rowA[:], in0=rowA[:], scalar=-1.0, in1=rowB[:],
                op0=ALU.mult, op1=ALU.mult), inc=sV)
            for sb_ in range(nsub):
                r.wait("tensor", sV, r.count(sV))
                r.emit("tensor", lambda e, sb_=sb_: e.matmul(
                    out=pwide2[:, 0:512], lhsT=ones_rowf[:],
                    rhs=rowB[:, sb_ * 512:sb_ * 512 + 512],
                    start=True, stop=True))
                r.emit("tensor", lambda e, sb_=sb_: e.matmul(
                    out=pwide2[:, 512:1024], lhsT=ones_rowf[:],
                    rhs=rowA[:, sb_ * 512:sb_ * 512 + 512],
                    start=True, stop=True), inc=sT)
                r.wait("vector", sT, r.count(sT))
                r.emit("vector", lambda e: e.tensor_copy(
                    out=Rt[:], in_=pwide2[:, 0:512]))
                r.emit("vector", lambda e: e.tensor_copy(
                    out=Mt[:], in_=pwide2[:, 512:1024]), inc=sV)
                for k in range(8):
                    sl = slice(sb_ * 512, sb_ * 512 + 512)
                    r.emit("vector", lambda e, k=k, sl=sl: e.tensor_mul(
                        out=tmpf[:], in0=xnb[k][:, sl], in1=Rt[:]))
                    r.emit("vector", lambda e: e.tensor_add(
                        out=tmpf[:], in0=tmpf[:], in1=Mt[:]))
                    r.emit("vector", lambda e, k=k, sl=sl: e.tensor_scalar(
                        out=xnb[k][:, sl], in0=tmpf[:],
                        scalar1=gcol8[:, k:k + 1], scalar2=lnbcol8[:, k:k + 1],
                        op0=ALU.mult, op1=ALU.add),
                        inc=(sV if k == 7 else None))
            if li == 0 and tb == 0 and "xn0" in cfg.taps:
                xnD = b.dram_out("tap_xn0", [1024, TBB], BF16)
                r.wait("gpsimd", sV, r.count(sV))
                for k in range(8):
                    r.dma("gpsimd", xnD[ts(k, 128), :], xnb[k][:], sG)
            r.barrier_from("tensor", sV)

            w1m = []
            egm = [0, 0, 0]
            slots = [pwide[:, 0:1024], pwide[:, 1024:2048], pwide2[:, 0:1024]]
            for j in range(22):
                for hf in range(2):
                    fM = 2 * j + hf
                    lb = fM % 2
                    if fM >= 2:
                        r.wait("sync", sT, w1m[fM - 2])
                    r.dma("sync",
                          w1l[lb][:].rearrange("p (k c) -> p k c", k=8),
                          W1bD[fM * 1024:(fM + 1) * 1024, :].rearrange(
                              "(k p) c -> p k c", p=128), f"w1t{lb}",
                          serial=False)
                    r.wait("tensor", f"w1t{lb}", r.count(f"w1t{lb}"))
                    if egm[fM % 3]:
                        r.wait("tensor", sV, egm[fM % 3])
                    for k in range(8):
                        for c2 in range(TBB // 512):
                            r.emit("tensor", lambda e, k=k, c2=c2, lb=lb,
                                   fM=fM: e.matmul(
                                       out=slots[fM % 3][:, c2 * 512:
                                                         c2 * 512 + 512],
                                       lhsT=w1l[lb][:, k * 128:k * 128 + 128],
                                       rhs=xnb[k][:, c2 * 512:c2 * 512 + 512],
                                       start=(k == 0), stop=(k == 7)),
                                   inc=(sT if (k == 7 and
                                               c2 == TBB // 512 - 1)
                                        else None))
                    w1m.append(r.count(sT))
                r.wait("vector", sT, w1m[-1])
                r.emit("vector", lambda e, j=j: e.tensor_copy(
                    out=gbf[:], in_=slots[(2 * j + 1) % 3][:, 0:TBB]), inc=sV)
                egm[(2 * j + 1) % 3] = r.count(sV)
                r.wait("scalar", sV, r.count(sV))
                r.emit("scalar", lambda e, j=j: e.activation(
                    out=ge[:], in_=gbf[:],
                    func=AF.Gelu, bias=biasf[:, 2 * j + 1:2 * j + 2]), inc=sS)
                r.wait("vector", sS, r.count(sS))
                r.emit("vector", lambda e, j=j: e.scalar_tensor_tensor(
                    out=ag[j][:], in0=slots[(2 * j) % 3][:, 0:TBB],
                    scalar=biasf[:, 2 * j:2 * j + 1], in1=ge[:],
                    op0=ALU.add, op1=ALU.mult), inc=sV)
                egm[(2 * j) % 3] = r.count(sV)
            if li + 1 < NL:
                r.barrier_from("gpsimd", sL)
                r.barrier_from("gpsimd", "w1t0")
                r.barrier_from("gpsimd", "w1t1")
                feeder.feed(16 // ntb + 1, cap_key=f"attn{li + 1}")
            r.barrier_from("tensor", sV)
            r.barrier_from("tensor", sS)

            for w in range(4):
                r.barrier_from("sync", sT)
                r.dma("sync", w2w[:].rearrange("p (k c) -> p k c", k=22),
                      W2bD[w * 2816:(w + 1) * 2816, :].rearrange(
                          "(k p) c -> p k c", p=128), sL)
                r.wait("tensor", sL, r.count(sL))
                r.barrier_from("tensor", sV)
                for k in range(22):
                    for d2 in range(2):
                        for c2 in range(TBB // 512):
                            r.emit("tensor", lambda e, k=k, d2=d2, c2=c2:
                                   e.matmul(
                                       out=pwide[:, d2 * 1024 + c2 * 512:
                                                 d2 * 1024 + c2 * 512 + 512],
                                       lhsT=w2w[:, k * 256 + d2 * 128:
                                                k * 256 + d2 * 128 + 128],
                                       rhs=ag[k][:, c2 * 512:c2 * 512 + 512],
                                       start=(k == 0), stop=(k == 21)),
                                   inc=(sT if (k == 21 and d2 == 1 and
                                               c2 == TBB // 512 - 1)
                                        else None))
                if w == 0:
                    for k in range(8):
                        r.emit("vector", lambda e, k=k, tb0=tb0: e.tensor_copy(
                            out=xk1[:, k:k + 1],
                            in_=S[k][:, HALO_B + tb0 + TBB - 1:
                                     HALO_B + tb0 + TBB]))
                r.wait("vector", sT, r.count(sT))
                for d2 in range(2):
                    dm = w * 2 + d2
                    r.emit("vector", lambda e, dm=dm, d2=d2, tb0=tb0:
                           e.scalar_tensor_tensor(
                               out=S[dm][:, HALO_B + tb0:HALO_B + tb0 + TBB],
                               in0=pwide[:, d2 * 1024:d2 * 1024 + TBB],
                               scalar=b2col8[:, dm:dm + 1],
                               in1=S[dm][:, HALO_B + tb0:HALO_B + tb0 + TBB],
                               op0=ALU.add, op1=ALU.add),
                           inc=(sV if d2 == 1 else None))
            r.barrier_from("tensor", sV)
        nm = f"pbf{li}"
        if nm in taps:
            r.barrier_from("gpsimd", sV)
            for k in range(8):
                r.dma("gpsimd", taps[nm][ts(k, 128), :], S[k][:, HALO_B:], sG)
        b.scopes.remove(fsc)
        fsc.close()

    stop = getattr(cfg, "pb_stop", None)
    feeder.feed(16)
    done = stop == "pre"
    if not done:
        try:
            for li in range(NL):
                attn_layer(li)
                if stop == f"a{li}":
                    done = True
                    break
                ffn_layer(li)
                if stop == f"f{li}":
                    done = True
                    break
        except _StopBuild:
            done = True
    if done:
        r.barrier_from("gpsimd", sV)
        r.barrier_from("gpsimd", sT)
        for k in range(8):
            r.dma("gpsimd", outD[ts(k, 128), :], S[k][:, HALO_B:], sG)
        phase_barrier(b)
        return dict(outD=outD)

    fse = contextlib.ExitStack()
    b.scopes.append(fse)

    def feb(name, shape, dtype=F32):
        t = fse.enter_context(b.nc.sbuf_tensor(name + "Z", list(shape), dtype))
        b.sb[name] = t
        return t

    lnrb = [feb(f"lnrbF{i}", [128, 512], BF16) for i in range(2)]
    lnr2b = [feb(f"lnr2bF{i}", [128, 512], BF16) for i in range(2)]
    feb("Rt", [128, Tc])
    feb("Mt", [128, Tc])
    feb("lnscr", [128, Tc])
    feb("rowA", [1, Tc])
    feb("rowB", [1, Tc])
    b.ln_bufs = (lnrb, lnr2b)
    ln_fm(b, extra_res=b.x1saveD)
    r.wait("gpsimd", sV, r.count(sV))
    for k in range(8):
        r.dma("gpsimd", outD[ts(k, 128), :], S[k][:, HALO_B:], sG)
    phase_barrier(b)
    return dict(outD=outD)


def kernel(**inputs):
    from concourse.bass_utils import run_bass_kernel_spmd
    cfg = Cfg(Tc=2048, n_cores=8, taps=())
    b = Build(cfg)
    build_mamba(b)
    build_partb(b)
    nc = b.run_streams_and_close()
    in_maps = []
    for core in range(cfg.n_cores):
        m = host_inputs_mamba(inputs, cfg, core)
        m.update(host_inputs_partb(inputs, cfg, core))
        in_maps.append(m)
    res = run_bass_kernel_spmd(nc, in_maps, core_ids=list(range(cfg.n_cores)))
    out = np.concatenate([np.asarray(res.results[c]["outD"]).T
                          for c in range(cfg.n_cores)], 0)
    return np.ascontiguousarray(out[None]).astype(np.float32)
